# version 06
# baseline (speedup 1.0000x reference)
import os as _osmod
import numpy as np
import ml_dtypes
import concourse.bass as bass
import concourse.mybir as mybir
from concourse.bass_utils import run_bass_kernel_spmd

F32 = mybir.dt.float32
BF16 = mybir.dt.bfloat16
AF = mybir.ActivationFunctionType
ALU = mybir.AluOpType
AX = mybir.AxisListType

B, S, E = 2, 8192, 1024
H, D, RK = 16, 64, 16
LN16 = float(np.log(16.0))
STAB0 = 4.0
EPS_LN = 1e-5
EPS_PERF = 1e-6
NT = 2048
NTB = 16
SERIAL_ATT = True
RG = [[0, 1, 2, 3], [4, 5, 6, 7]]
BF = ml_dtypes.bfloat16


class Sem:
    __slots__ = ("name", "v", "h")

    def __init__(self, name):
        self.name, self.v, self.h = name, 0, None


TOTAL_ORDER = True


class Rec:
    ENGINES = ("tensor", "vector", "scalar", "gpsimd", "sync")

    def __init__(self):
        self.q = {e: [] for e in self.ENGINES}
        self.sems = []
        self.total = TOTAL_ORDER
        if self.total:
            self.G = self.sem("Gtot")

    def sem(self, name):
        s = Sem(name)
        self.sems.append(s)
        return s

    def op(self, eng, fn, wait=(), inc=None):
        if self.total:
            ws = [(self.G, self.G.v)]
            amt = inc[1] if inc is not None else 1
            if inc is not None:
                inc[0].v += inc[1]
            if fn is None:
                self.q[eng].append((ws, None, None))
                return
            self.G.v += amt
            self.q[eng].append((ws, fn, (self.G, amt)))
            return
        ws = [(w, w.v) if isinstance(w, Sem) else w for w in wait]
        if inc is not None:
            inc[0].v += inc[1]
        self.q[eng].append((ws, fn, inc))

    def flush(self, nc, stack):
        for s in self.sems:
            s.h = stack.enter_context(nc.semaphore(s.name))
        block = stack.enter_context(nc.Block())
        for eng in self.ENGINES:
            items = self.q[eng]
            if not items:
                continue

            def make(items):
                def body(e):
                    for ws, fn, inc in items:
                        for s, v in ws:
                            if v > 0:
                                e.wait_ge(s.h, v)
                        if fn is not None:
                            inst = fn(e)
                            if inc is not None:
                                inst.then_inc(inc[0].h, inc[1])
                        elif inc is not None:
                            e.nop().then_inc(inc[0].h, inc[1])
                return body
            getattr(block, eng)(make(items))


def _qk_pack(Wm):
    Wp = Wm.reshape(8, 2, 1024, 64).transpose(0, 2, 1, 3).reshape(8, 1024, 128)
    Wp = Wp.reshape(8, 8, 128, 128).transpose(0, 2, 1, 3)
    return np.ascontiguousarray(Wp, np.float32)


def pack_consts(ins):
    f32 = np.float32
    P = {}
    P["WqP"] = _qk_pack(ins["L0_attn_Wq"])
    P["WkP"] = _qk_pack(ins["L0_attn_Wk"])
    P["bqP"] = np.ascontiguousarray(ins["L0_attn_bq"].reshape(8, 128).T, f32)
    P["bkP"] = np.ascontiguousarray(ins["L0_attn_bk"].reshape(8, 128).T, f32)
    Wvd = ins["L0_attn_Wvd"].reshape(4, 4, 1024, 16).transpose(0, 2, 1, 3).reshape(4, 1024, 64)
    Wvd = Wvd.reshape(4, 8, 128, 64).transpose(0, 2, 1, 3)
    P["WvdP"] = np.ascontiguousarray(Wvd, f32)
    P["bvdP"] = np.ascontiguousarray(ins["L0_attn_bvd"].reshape(4, 64).T, f32)
    Wvu = ins["L0_attn_Wvu"]
    W2 = np.zeros((8, 32, 128), f32)
    for g in range(8):
        W2[g, 0:16, 0:64] = Wvu[2 * g]
        W2[g, 16:32, 64:128] = Wvu[2 * g + 1]
    P["Wvu2"] = W2
    P["bvu2"] = np.ascontiguousarray(ins["L0_attn_bvu"].reshape(8, 1, 128), f32)
    P["WeP"] = np.ascontiguousarray(ins["L0_ffn_We"], f32)
    P["beP"] = np.ascontiguousarray(ins["L0_ffn_be"].reshape(8, 1, 1024), f32)
    P["WgP"] = np.ascontiguousarray(ins["L0_ffn_Wg"], f32)
    P["bgP"] = np.ascontiguousarray(ins["L0_ffn_bg"].reshape(1, 8), f32)
    Wqkv = ins["L1_attn_W_qkv"]
    P["W1qP"] = _qk_pack(np.ascontiguousarray(Wqkv[:, 0:1024]).reshape(1024, 16, 64).transpose(1, 0, 2))
    P["W1kP"] = _qk_pack(np.ascontiguousarray(Wqkv[:, 1024:2048]).reshape(1024, 16, 64).transpose(1, 0, 2))
    P["W1v"] = np.ascontiguousarray(Wqkv[:, 2048:3072], f32)
    P["pmT"] = np.ascontiguousarray(ins["L1_attn_proj"].transpose(0, 2, 1), f32)
    P["W1o"] = np.ascontiguousarray(ins["L1_attn_W_out"], f32)
    P["Wf"] = np.ascontiguousarray(ins["L1_ffn_weight"], f32)
    P["identb"] = np.eye(128, dtype=BF)
    P["identf"] = np.eye(128, dtype=f32)
    P["tri"] = np.tril(np.ones((128, 128), f32)).astype(BF)
    return P


def shard_rows(x_full, c):
    g, m = c // 4, c % 4
    xb = x_full[g]
    return np.ascontiguousarray(
        np.concatenate([xb[m * 1024:(m + 1) * 1024], xb[(7 - m) * 1024:(8 - m) * 1024]], axis=0),
        np.float32)


def unshard_rows(shards):
    out = np.empty((B, S, E), np.float32)
    for c in range(8):
        g, m = c // 4, c % 4
        out[g, m * 1024:(m + 1) * 1024] = shards[c][0:1024]
        out[g, (7 - m) * 1024:(8 - m) * 1024] = shards[c][1024:2048]
    return out


class Ctx:
    pass


def alloc_common(nc, stack):
    cx = Ctx()
    cx.nc = nc
    cx.R = Rec()
    ec = stack.enter_context
    t = {}

    def sb(name, shape, dt):
        t[name] = ec(nc.sbuf_tensor(name, shape, dt))
    cx.t = t
    sb("X", [128, NTB * 1024], BF16)
    sb("OS0", [128, 1024], F32)
    sb("OS1", [128, 1024], F32)
    sb("XN", [128, 8 * NT], BF16)
    sb("QD", [128, 8 * NT], BF16)
    sb("KSTR0", [128, 2048], BF16)
    sb("KSTR1", [128, 2048], BF16)
    sb("WEB", [128, 8192], BF16)
    sb("VAP", [128, 16 * 1040], BF16)
    sb("VT0", [128, 8 * 130], BF16)
    sb("VT1", [128, 8 * 130], BF16)
    sb("VD4", [64, NT], BF16)
    sb("ES0", [128, 512], BF16)
    sb("ES1", [128, 512], BF16)
    sb("AVSB", [128, 512], BF16)
    sb("W0", [128, 1024], BF16)
    sb("W1", [128, 1024], BF16)
    sb("WR0", [32, 1024], BF16)
    sb("XT0", [128, 1024], BF16)
    sb("STATS", [128, 164], F32)
    sb("CONSTB", [128, 256], BF16)
    sb("ONE1", [1, 128], BF16)
    sb("BQK", [128, 16], F32)
    sb("CB", [128, 4], F32)
    sb("BVD", [64, 4], F32)
    sb("BR", [1, 1024], BF16)
    sb("WG", [128, 64], BF16)
    sb("BG1", [1, 8], BF16)
    sb("NEGD", [128, 48], F32)
    sb("KT64", [128, 64], BF16)
    sb("K264", [128, 64], BF16)
    sb("KVA", [128, 2080], F32)
    sb("KVB", [128, 2080], F32)
    sb("QFT", [128, 256], BF16)
    sb("ATN", [128, 64], BF16)
    sb("KF", [128, 256], BF16)
    sb("MKB", [128, 128], BF16)
    sb("ZB5", [128, 512], BF16)
    sb("EPI", [128, 528], BF16)
    sb("KVS", [128, 130], F32)
    sb("KVH0", [128, 130], BF16)
    sb("KVH1", [128, 130], BF16)
    sb("WV2", [32, 128], BF16)
    sb("PM0", [64, 128], BF16)
    sb("PM1", [64, 128], BF16)
    sb("MX", [128, 16], F32)
    sb("SG", [128, 24], F32)
    sb("SR16", [16, 128], BF16)
    sb("ZS", [128, 256], F32)
    sb("ST16", [16, 1], F32)
    cx.pb = [ec(nc.psum_tensor(f"pb{i}", [128, 512], F32)) for i in range(6)]
    cx.pt = [ec(nc.psum_tensor(f"pt{i}", [128, 1024], BF16)) for i in range(2)]
    return cx


def att_alias(cx):
    return cx.t["VAP"]


def emit_consts(cx, dram):
    R, t = cx.R, cx.t
    ld = R.sem("ldc")
    pairs = [
        (dram["identb"], t["CONSTB"][:, 0:128]),
        (dram["tri"], t["CONSTB"][:, 128:256]),
        (dram["bqP"], t["BQK"][:, 0:8]),
        (dram["bkP"], t["BQK"][:, 8:16]),
        (dram["bvdP"], t["BVD"][:]),
    ]
    for src, dst in pairs:
        R.op("sync", lambda e, src=src, dst=dst: e.dma_start(out=dst, in_=src), inc=(ld, 16))
    R.op("gpsimd", lambda e: e.memset(t["ONE1"][:], 1.0), inc=(ld, 1))
    R.op("gpsimd", lambda e: e.memset(t["CB"][:, 0:1], EPS_LN), inc=(ld, 1))
    R.op("gpsimd", lambda e: e.memset(t["CB"][:, 1:2], -STAB0), inc=(ld, 1))
    return ld


def emit_ln(cx, dep, x_src="X"):
    R, t = cx.R, cx.t
    X, XN = t[x_src], t["XN"]
    SM = t["STATS"]
    s_sq = R.sem("ln_sq")
    s_st = R.sem("ln_st")
    s_xn = R.sem("ln_xn")
    s_tr = R.sem("ln_tr")
    s_ev = R.sem("ln_ev")
    s_s2 = R.sem("ln_s2")
    for tb in range(NTB):
        xt = X[:, tb * 1024:(tb + 1) * 1024]
        sqb = t["W0"] if tb % 2 == 0 else t["W1"]
        xnb = t["XT0"]
        c = 10 * tb
        wsq = list(dep) + ([] if tb < 2 else [(s_s2, tb - 1)])
        R.op("scalar", lambda e, xt=xt, sqb=sqb: e.activation(out=sqb[:], in_=xt, func=AF.Square),
             wait=wsq, inc=(s_sq, 1))
        R.op("vector", lambda e, xt=xt, c=c: e.reduce_sum(SM[:, c:c + 1], xt, axis=AX.X), wait=list(dep))
        R.op("vector", lambda e, sqb=sqb, c=c: e.reduce_sum(SM[:, c + 1:c + 2], sqb[:], axis=AX.X),
             wait=[(s_sq, tb + 1)], inc=(s_s2, 1))
        R.op("vector", lambda e, c=c: e.tensor_scalar(out=SM[:, c + 2:c + 3], in0=SM[:, c:c + 1],
                                                      scalar1=1.0 / 1024, scalar2=None, op0=ALU.mult))
        R.op("vector", lambda e, c=c: e.tensor_scalar(out=SM[:, c + 3:c + 4], in0=SM[:, c + 1:c + 2],
                                                      scalar1=1.0 / 1024, scalar2=None, op0=ALU.mult))
        R.op("vector", lambda e, c=c: e.tensor_tensor(out=SM[:, c + 4:c + 5], in0=SM[:, c + 2:c + 3],
                                                      in1=SM[:, c + 2:c + 3], op=ALU.mult))
        R.op("vector", lambda e, c=c: e.tensor_tensor(out=SM[:, c + 5:c + 6], in0=SM[:, c + 3:c + 4],
                                                      in1=SM[:, c + 4:c + 5], op=ALU.subtract), inc=(s_xn, 1))
        R.op("scalar", lambda e, c=c: e.activation(out=SM[:, c + 6:c + 7], in_=SM[:, c + 5:c + 6],
                                                   func=AF.Ln, bias=t["CB"][:, 0:1]), wait=[(s_xn, 2 * tb + 1)])
        R.op("scalar", lambda e, c=c: e.activation(out=SM[:, c + 7:c + 8], in_=SM[:, c + 6:c + 7],
                                                   func=AF.Exp, scale=-0.5), inc=(s_st, 1))
        wxn = [(s_st, tb + 1)] + ([] if tb < 1 else [(s_tr, tb * 8)])
        R.op("vector", lambda e, c=c: e.scalar_tensor_tensor(out=SM[:, c + 8:c + 9], in0=SM[:, c + 2:c + 3],
                                                             scalar=-1.0, in1=SM[:, c + 7:c + 8],
                                                             op0=ALU.mult, op1=ALU.mult), wait=wxn)
        R.op("vector", lambda e, xt=xt, xnb=xnb, c=c: e.tensor_scalar(
            out=xnb[:], in0=xt, scalar1=SM[:, c + 7:c + 8], scalar2=SM[:, c + 8:c + 9],
            op0=ALU.mult, op1=ALU.add), inc=(s_xn, 1))
        for f in range(8):
            wt = [(s_xn, 2 * tb + 2)] if f == 0 else []
            if f == 0 and tb >= 2:
                wt.append((s_ev, tb - 1))
            R.op("tensor", lambda e, xnb=xnb, f=f, tb=tb: e.transpose(
                cx.pt[tb % 2][:, f * 128:(f + 1) * 128], xnb[:, f * 128:(f + 1) * 128],
                cx.t["CONSTB"][:, 0:128]), wait=wt, inc=(s_tr, 1))
        R.op("scalar", lambda e, tb=tb: e.copy(
            out=XN.rearrange("p (f c) -> p f c", f=8)[:, :, tb * 128:(tb + 1) * 128],
            in_=cx.pt[tb % 2].rearrange("p (f c) -> p f c", f=8)[:, :, :]),
            wait=[(s_tr, (tb + 1) * 8)], inc=(s_ev, 1))
    return s_ev


def _mm_proj_pairs(cx, Wdram, n_pairs, m_width, dst, bias_ap_fn, dep, tag,
                   kc_n=8, dep_w=(), dst_stride=NT, after_pair=None):
    R, t = cx.R, cx.t
    XN = t["XN"]
    s_wld = R.sem(f"{tag}_wld")
    s_mm = R.sem(f"{tag}_mm")
    s_done = R.sem(f"{tag}_ev")
    s_dma = R.sem(f"{tag}_dma") if after_pair is not None else None
    for pr in range(n_pairs):
        wb = t["W0"] if pr % 2 == 0 else t["W1"]
        wld = (list(dep_w) if pr < 2 else []) + ([] if pr < 2 else [(s_mm, 2 * pr - 2)])
        R.op("gpsimd", lambda e, pr=pr, wb=wb: e.dma_start(
            out=wb[0:128, 0:kc_n * m_width],
            in_=Wdram[pr].rearrange("p k m -> p (k m)")), wait=wld, inc=(s_wld, 16))
        for nq2 in range(2):
            it = pr * 2 + nq2
            b0, b1 = (it % 2) * 2, (it % 2) * 2 + 1
            for kc in range(kc_n):
                w = []
                if kc == 0:
                    w.append((s_wld, 16 * (pr + 1)))
                    if it == 0:
                        w.extend(dep)
                    if it >= 2:
                        w.append((s_done, 2 * it - 2))
                for half in range(2):
                    nq = nq2 * 2 + half
                    pbk = cx.pb[b0 if half == 0 else b1]
                    R.op("tensor", lambda e, kc=kc, nq=nq, wb=wb, pbk=pbk: e.matmul(
                        pbk[0:m_width, :],
                        wb[:, kc * m_width:(kc + 1) * m_width],
                        XN[:, kc * NT + nq * 512: kc * NT + (nq + 1) * 512],
                        start=(kc == 0), stop=(kc == kc_n - 1)),
                        wait=w if half == 0 else [],
                        inc=(s_mm, 1) if (kc == kc_n - 1 and half == 1) else None)
            for half in range(2):
                nq = nq2 * 2 + half
                pbk = cx.pb[b0 if half == 0 else b1]
                bias = bias_ap_fn(pr)
                wev = [(s_mm, it + 1)] if half == 0 else []
                if it == 0 and half == 0:
                    wev.extend(dep)
                if after_pair is not None and pr >= 1 and it == 2 * pr and half == 0:
                    wev.append((s_dma, 16 * pr))
                if bias is not None:
                    R.op("vector", lambda e, pr=pr, nq=nq, pbk=pbk, bias=bias: e.tensor_scalar(
                        out=dst[0:m_width, pr * dst_stride + nq * 512: pr * dst_stride + (nq + 1) * 512],
                        in0=pbk[0:m_width, :], scalar1=bias, scalar2=None, op0=ALU.add),
                        wait=wev, inc=(s_done, 1))
                else:
                    R.op("vector", lambda e, pr=pr, nq=nq, pbk=pbk: e.tensor_copy(
                        out=dst[0:m_width, pr * dst_stride + nq * 512: pr * dst_stride + (nq + 1) * 512],
                        in_=pbk[0:m_width, :]),
                        wait=wev, inc=(s_done, 1))
        if after_pair is not None:
            dram_ap, sbuf_ap = after_pair(pr)
            R.op("sync", lambda e, dram_ap=dram_ap, sbuf_ap=sbuf_ap: e.dma_start(
                out=dram_ap, in_=sbuf_ap), wait=[(s_done, 4 * (pr + 1))], inc=(s_dma, 16))
    if after_pair is not None:
        return s_done, s_mm, s_dma
    return s_done, s_mm


def emit_l0_proj(cx, dram, dep_xn, dep_c):
    t = cx.t
    dep = list(dep_xn) + list(dep_c)
    s_q, m_q = _mm_proj_pairs(cx, dram["WqP"], 8, 128, t["QD"],
                              lambda pr: t["BQK"][:, pr:pr + 1], dep, "p0q",
                              dep_w=list(dep_xn))
    return s_q, m_q


def emit_k_proj(cx, Wdram, bias_col0, k_dram, dep_xn, dep_c, dep_w, tag):
    t = cx.t
    dep = list(dep_xn) + list(dep_c)
    bias_fn = (lambda pr: None) if bias_col0 is None else (
        lambda pr: t["BQK"][:, bias_col0 + pr:bias_col0 + pr + 1])
    s_k, m_k, s_dma = _mm_proj_pairs(
        cx, Wdram, 8, 128, t["KSTR0"], bias_fn, dep, tag,
        dep_w=list(dep_w), dst_stride=0,
        after_pair=lambda pr: (k_dram[pr * 128:(pr + 1) * 128, :], t["KSTR0"][:, :]))
    return s_k, m_k, s_dma


def emit_l0_vd(cx, dram, vd_dram, dep_xn, dep_c, dep_w):
    t = cx.t
    dep = list(dep_xn) + list(dep_c)
    s_v, m_v, s_dma = _mm_proj_pairs(
        cx, dram["WvdP"], 4, 64, t["VD4"],
        lambda pr: t["BVD"][:, pr:pr + 1], dep, "p0v",
        dep_w=list(dep_w), dst_stride=0,
        after_pair=lambda pr: (vd_dram[pr * 64:(pr + 1) * 64, :], t["VD4"][0:64, :]))
    return s_v, m_v, s_dma


def emit_load_x(cx, x_dram, cast=True):
    R, t = cx.R, cx.t
    s = R.sem("ldx")
    eng = "gpsimd" if cast else "sync"
    R.op(eng, lambda e: e.dma_start(
        out=t["X"].rearrange("p (tb c) -> p tb c", tb=NTB),
        in_=x_dram.rearrange("(tb p) c -> p tb c", p=128)), inc=(s, 16))
    return s



def emit_l0_attn(cx, dram, kg_fn, vdg_fn, own_ge, dep_q, dep_kv, dep_x, dep_pb, dep_c=(), dbg=None, groups=range(8)):
    R, t = cx.R, cx.t
    QD, X = t["QD"], t["X"]
    s_ini = R.sem("a_ini")
    s_kld = R.sem("a_kld")
    s_vld = R.sem("a_vld")
    s_vt = R.sem("a_vt")
    s_vte = R.sem("a_vte")
    s_sc = R.sem("a_sc")
    s_ex = R.sem("a_ex")
    s_mk = R.sem("a_mk")
    s_av = R.sem("a_av")
    s_ep = R.sem("a_ep")
    s_ept = R.sem("a_ept")
    s_stt = R.sem("a_stt")
    s_ku = R.sem("a_ku")
    emit_l0_attn._dep_c = list(dep_c)
    es_free = {0: 0, 1: 0}
    es_i = 0
    it = 0
    started = set()
    R.op("gpsimd", lambda e: e.memset(t["ZB5"][:], 0.0), inc=(s_ini, 1))
    for vtb in (t["VT0"], t["VT1"]):
        for col in (64, 129):
            R.op("gpsimd", lambda e, vtb=vtb, col=col: e.memset(
                vtb.rearrange("p (tb c) -> p tb c", tb=8)[:, :, col:col + 1], 1.0),
                inc=(s_ini, 1))
    first_te = [True]
    for g in groups:
        wc = [(s_vt, s_vt.v)]
        R.op("gpsimd", lambda e, g=g: e.dma_start(out=t["WV2"][:], in_=dram["Wvu2"][g]),
             wait=wc, inc=(s_vld, 16))
        R.op("gpsimd", lambda e, g=g: e.dma_start(out=t["BR"][0:1, 0:128], in_=dram["bvu2"][g]),
             wait=[(s_vt, s_vt.v)], inc=(s_vld, 16))
        wv_ready = (s_vld, s_vld.v)
        for ei, ge in enumerate(own_ge):
            for ke in range(ge, 8):
                kb = t["KSTR0"] if it % 2 == 0 else t["KSTR1"]
                vtb = t["VT0"] if it % 2 == 0 else t["VT1"]
                wk = list(dep_kv) + ([(s_ku, it - 1)] if it >= 2 else [])
                R.op("sync", lambda e, g=g, ke=ke, kb=kb: e.dma_start(
                    out=kb[:, 0:1024], in_=kg_fn(g, ke)), wait=wk, inc=(s_kld, 16))
                R.op("sync", lambda e, g=g, ke=ke: e.dma_start(
                    out=t["WR0"][:, 0:1024], in_=vdg_fn(g, ke)),
                    wait=[(s_vt, s_vt.v)], inc=(s_vld, 16))
                vld_here = (s_vld, s_vld.v)
                for tb8 in range(8):
                    w = []
                    if tb8 == 0:
                        w = [wv_ready, vld_here, (s_ex, s_ex.v),
                             (s_vte, s_vte.v), (s_ini, 5)]
                        if it >= 2:
                            w.append((s_vte, s_vte.v))
                        if first_te[0]:
                            w += list(dep_q) + list(dep_pb) + list(emit_l0_attn._dep_c)
                            first_te[0] = False
                    else:
                        w = [(s_vte, s_vte.v)]
                    R.op("tensor", lambda e, tb8=tb8: e.matmul(
                        cx.pb[4][:, 0:128],
                        t["WR0"][0:32, tb8 * 128:(tb8 + 1) * 128], t["WV2"][:],
                        start=True, stop=False), wait=w)
                    R.op("tensor", lambda e: e.matmul(
                        cx.pb[4][:, 0:128], t["ONE1"][0:1, 0:128], t["BR"][0:1, 0:128],
                        start=False, stop=True), inc=(s_vt, 1))
                    wev = [(s_vt, s_vt.v)]
                    if tb8 == 0 and it >= 2:
                        wev.append((s_ku, it - 1))
                    R.op("scalar", lambda e, tb8=tb8, vtb=vtb: e.copy(
                        out=vtb[:, tb8 * 130:tb8 * 130 + 130].rearrange(
                            "p (hh c) -> p hh c", hh=2)[:, :, 0:64],
                        in_=cx.pb[4][:, 0:128].rearrange("p (hh c) -> p hh c", hh=2)),
                        wait=wev, inc=(s_vte, 1))
                if dbg is not None and "vt" in dbg and it == 8:
                    R.op("sync", lambda e, vtb=vtb: e.dma_start(out=dbg["vt"], in_=vtb[:, :]),
                         wait=[(s_vte, s_vte.v)], inc=(s_ini, 16))
                for hh in range(2):
                    tcs = [(tc, (tc + 1) * 128 if ke == ge else 1024) for tc in range(8)]
                    for tc, ns in tcs:
                        for sh in range((ns + 511) // 512):
                            n = min(512, ns - sh * 512)
                            par = es_i % 2
                            esb = t["ES0"] if par == 0 else t["ES1"]
                            scb = cx.pb[4 + par]
                            wsc = [(s_kld, s_kld.v), (s_vte, s_vte.v), (s_ex, s_ex.v)]
                            if SERIAL_ATT:
                                wsc += [(s_av, s_av.v), (s_mk, s_mk.v), (s_stt, s_stt.v)]
                            R.op("tensor", lambda e, g=g, hh=hh, tc=tc, sh=sh, n=n, ei=ei, kb=kb, scb=scb: e.matmul(
                                scb[:, 0:n],
                                kb[hh * 64:hh * 64 + 64, tc * 128:(tc + 1) * 128],
                                QD[hh * 64:hh * 64 + 64,
                                   g * NT + ei * 1024 + sh * 512: g * NT + ei * 1024 + sh * 512 + n],
                                start=True, stop=True),
                                wait=wsc,
                                inc=(s_sc, 1))
                            wex = [(s_sc, s_sc.v)]
                            if es_i == 0:
                                wex += list(emit_l0_attn._dep_c)
                            if es_free[par] > 0:
                                wex.append((s_av, es_free[par]))
                            R.op("scalar", lambda e, esb=esb, scb=scb, n=n: e.activation(
                                out=esb[:, 0:n], in_=scb[:, 0:n], func=AF.Exp,
                                scale=1.0 / 32, bias=t["CB"][:, 1:2]),
                                wait=wex, inc=(s_ex, 1))
                            masked = (ke == ge) and (tc // 4 == sh)
                            j0 = tc * 128 - sh * 512 if masked else n
                            if masked:
                                R.op("vector", lambda e, esb=esb, j0=j0: e.tensor_tensor(
                                    out=t["MKB"][:], in0=esb[:, j0:j0 + 128],
                                    in1=t["CONSTB"][:, 128:256], op=ALU.mult),
                                    wait=[(s_ex, s_ex.v)], inc=(s_mk, 1))
                            if dbg is not None and "es" in dbg and es_i == 0:
                                R.op("sync", lambda e, esb=esb, n=n: e.dma_start(
                                    out=dbg["es"][:, 0:n], in_=esb[:, 0:n]),
                                    wait=[(s_mk, s_mk.v)] if masked else [(s_ex, s_ex.v)],
                                    inc=(s_ini, 16))
                            bank = 2 * hh + sh
                            key = (g, ei, bank)
                            st_flag = key not in started
                            started.add(key)
                            stop_flag = False
                            w = [(s_ex, s_ex.v)]
                            if st_flag:
                                w.append((s_ep, s_ep.v))
                            if j0 > 0:
                                R.op("tensor", lambda e, vtb=vtb, hh=hh, tc=tc, bank=bank, esb=esb, j0=j0,
                                     st=st_flag, sp=(stop_flag and not masked): e.matmul(
                                    cx.pb[bank][0:65, 0:j0],
                                    vtb[:, tc * 130 + hh * 65: tc * 130 + hh * 65 + 65],
                                    esb[:, 0:j0], start=st, stop=sp),
                                    wait=w, inc=(s_av, 1))
                                st_flag = False
                            if masked and not _osmod.environ.get("SKIPMKB"):
                                w2 = [(s_mk, s_mk.v)]
                                if st_flag:
                                    w2.append((s_ep, s_ep.v))
                                R.op("tensor", lambda e, vtb=vtb, hh=hh, tc=tc, bank=bank, j0=j0,
                                     st=st_flag, sp=stop_flag: e.matmul(
                                    cx.pb[bank][0:65, j0:j0 + 128],
                                    vtb[:, tc * 130 + hh * 65: tc * 130 + hh * 65 + 65],
                                    t["MKB"][:], start=st, stop=sp),
                                    wait=w2, inc=(s_av, 1))
                            es_free[par] = s_av.v
                            es_i += 1
                R.op("tensor", None, inc=(s_ku, 1))
                it += 1
            for bank in range(4):
                R.op("tensor", lambda e, bank=bank: e.matmul(
                    cx.pb[bank][0:65, 0:512],
                    (t["VT0"] if it % 2 == 1 else t["VT1"])[:, 0:65],
                    t["ZB5"][:], start=False, stop=True), inc=(s_av, 1))
            for hh in range(2):
                h = 2 * g + hh
                for sh in range(2):
                    R.op("scalar", lambda e, hh=hh, sh=sh: e.copy(
                        out=t["AVSB"][0:65, 0:512], in_=cx.pb[2 * hh + sh][0:65, 0:512]),
                        wait=[(s_av, s_av.v), (s_ept, s_ept.v)], inc=(s_ep, 1))
                    if dbg is not None and "x1banks" in dbg and g == 0:
                        r0 = (ei * 2 + hh) * 128
                        R.op("sync", lambda e, sh=sh, r0=r0: e.dma_start(
                            out=dbg["x1banks"][r0:r0 + 128, sh * 512:(sh + 1) * 512],
                            in_=t["AVSB"][0:128, 0:512]),
                            wait=[(s_ep, s_ep.v)], inc=(s_ini, 16))
                    for s4 in range(4):
                        slot = sh * 4 + s4
                        w = [(s_ep, s_ep.v)] if s4 == 0 else []
                        if s4 == 0 and sh == 0:
                            w.append((s_stt, s_stt.v))
                        R.op("tensor", lambda e, s4=s4, slot=slot: e.transpose(
                            cx.pt[0][:, slot * 128: slot * 128 + 65],
                            t["AVSB"][0:65, s4 * 128:(s4 + 1) * 128],
                            t["CONSTB"][0:65, 0:65]), wait=w, inc=(s_ept, 1))
                for slot in range(8):
                    wep = [(s_ept, s_ept.v), (s_stt, s_stt.v)] if slot == 0 else []
                    R.op("vector", lambda e, slot=slot: e.tensor_copy(
                        out=t["EPI"][:, slot * 66: slot * 66 + 65],
                        in_=cx.pt[0][:, slot * 128: slot * 128 + 65]),
                        wait=wep, inc=(s_ep, 1) if slot == 7 else None)
                for slot in range(8):
                    tb = ei * 8 + slot
                    w = [(s_ep, s_ep.v)] if slot == 0 else []
                    if g == 0 and ei == 0 and hh == 0 and slot == 0:
                        w += list(dep_x)
                    if dbg is not None:
                        R.op("scalar", lambda e, slot=slot, tb=tb, h=h: e.copy(
                            out=t["ZS"][:, h * 16 + tb: h * 16 + tb + 1],
                            in_=cx.pt[0][:, slot * 128 + 64: slot * 128 + 65]),
                            wait=[(s_ept, s_ept.v)])
                    R.op("vector", lambda e, slot=slot: e.reciprocal(
                        t["STATS"][:, 160:161], t["EPI"][:, slot * 66 + 64: slot * 66 + 65]),
                        wait=w)
                    if _osmod.environ.get("DBGZ"):
                        R.op("vector", lambda e, slot=slot, tb=tb, h=h: e.tensor_copy(
                            out=X[:, tb * 1024 + h * 64: tb * 1024 + h * 64 + 65],
                            in_=t["EPI"][:, slot * 66: slot * 66 + 65]), inc=(s_stt, 1))
                    elif _osmod.environ.get("DBGATT"):
                        R.op("vector", lambda e, slot=slot, tb=tb, h=h: e.tensor_scalar(
                            out=X[:, tb * 1024 + h * 64: tb * 1024 + h * 64 + 64],
                            in0=t["EPI"][:, slot * 66: slot * 66 + 64],
                            scalar1=t["STATS"][:, 160:161], scalar2=None,
                            op0=ALU.mult), inc=(s_stt, 1))
                    else:
                        R.op("vector", lambda e, slot=slot, tb=tb, h=h: e.scalar_tensor_tensor(
                            out=X[:, tb * 1024 + h * 64: tb * 1024 + h * 64 + 64],
                            in0=t["EPI"][:, slot * 66: slot * 66 + 64],
                            scalar=t["STATS"][:, 160:161],
                            in1=X[:, tb * 1024 + h * 64: tb * 1024 + h * 64 + 64],
                            op0=ALU.mult, op1=ALU.add), inc=(s_stt, 1))
            started = {k for k in started if k[0] != g or k[1] != ei}
    return s_stt


def emit_moe(cx, dram, dep):
    R, t = cx.R, cx.t
    X, XN = t["X"], t["XN"]
    s = R.sem("moe")
    R.op("gpsimd", lambda e: e.dma_start(
        out=t["WG"][:, 0:64].rearrange("p (kc c) -> p kc c", kc=8),
        in_=dram["WgP"].rearrange("(kc p) c -> p kc c", p=128)), wait=dep, inc=(s, 16))
    R.op("gpsimd", lambda e: e.dma_start(out=t["BG1"][:], in_=dram["bgP"]), inc=(s, 16))
    SG = t["SG"]
    WC = t["KVA"]
    for tb in range(NTB):
        for kc in range(8):
            R.op("tensor", lambda e, tb=tb, kc=kc: e.matmul(
                cx.pb[0][:, 0:8],
                XN[:, kc * NT + tb * 128: kc * NT + tb * 128 + 128],
                t["WG"][:, kc * 8:(kc + 1) * 8], start=(kc == 0), stop=False))
        R.op("tensor", lambda e: e.matmul(
            cx.pb[0][:, 0:8], t["ONE1"][0:1, 0:128], t["BG1"][:],
            start=False, stop=True))
        R.op("vector", lambda e: e.tensor_copy(out=SG[:, 0:8], in_=cx.pb[0][:, 0:8]))
        R.op("vector", lambda e: e.reduce_max(SG[:, 16:17], SG[:, 0:8], axis=AX.X, negate=True))
        R.op("scalar", lambda e: e.activation(out=SG[:, 8:16], in_=SG[:, 0:8],
                                              func=AF.Exp, bias=SG[:, 16:17]))
        R.op("vector", lambda e: e.tensor_scalar(out=SG[:, 0:8], in0=SG[:, 8:16],
                                                 scalar1=0.99999, scalar2=None, op0=ALU.is_ge))
        R.op("vector", lambda e: e.tensor_tensor(out=SG[:, 16:24], in0=SG[:, 8:16],
                                                 in1=SG[:, 0:8], op=ALU.subtract))
        R.op("vector", lambda e: e.reduce_max(SG[:, 17:18] if False else t["NEGD"][:, 0:1],
                                              SG[:, 16:24], axis=AX.X))
        R.op("vector", lambda e: e.tensor_scalar(out=SG[:, 16:24], in0=SG[:, 8:16],
                                                 scalar1=t["NEGD"][:, 0:1], scalar2=None,
                                                 op0=ALU.is_ge))
        R.op("vector", lambda e: e.tensor_tensor(out=SG[:, 8:16], in0=SG[:, 8:16],
                                                 in1=SG[:, 16:24], op=ALU.mult))
        R.op("vector", lambda e: e.reduce_sum(t["NEGD"][:, 1:2], SG[:, 8:16], axis=AX.X))
        R.op("vector", lambda e: e.reciprocal(t["NEGD"][:, 2:3], t["NEGD"][:, 1:2]))
        R.op("vector", lambda e, tb=tb: e.tensor_scalar(out=WC[:, tb * 8:(tb + 1) * 8],
                                                        in0=SG[:, 8:16],
                                                        scalar1=t["NEGD"][:, 2:3], scalar2=None,
                                                        op0=ALU.mult))
    for ex in range(8):
        R.op("gpsimd", lambda e, ex=ex: e.dma_start(
            out=t["WEB"].rearrange("p (kc c) -> p kc c", kc=8),
            in_=dram["WeP"][ex].rearrange("(kc p) c -> p kc c", p=128)), inc=(s, 16))
        R.op("gpsimd", lambda e, ex=ex: e.dma_start(
            out=t["BR"][0:1, 0:1024], in_=dram["beP"][ex]), inc=(s, 16))
        for tb in range(NTB):
            for half in range(2):
                pbk = cx.pb[half]
                for kc in range(8):
                    R.op("tensor", lambda e, tb=tb, kc=kc, half=half, pbk=pbk: e.matmul(
                        pbk[:, 0:512],
                        XN[:, kc * NT + tb * 128: kc * NT + tb * 128 + 128],
                        t["WEB"][:, kc * 1024 + half * 512: kc * 1024 + half * 512 + 512],
                        start=(kc == 0), stop=False))
                R.op("tensor", lambda e, half=half, pbk=pbk: e.matmul(
                    pbk[:, 0:512], t["ONE1"][0:1, 0:128],
                    t["BR"][0:1, half * 512:(half + 1) * 512], start=False, stop=True))
                R.op("vector", lambda e, tb=tb, half=half, ex=ex, pbk=pbk: e.scalar_tensor_tensor(
                    out=X[:, tb * 1024 + half * 512: tb * 1024 + half * 512 + 512],
                    in0=pbk[:, 0:512], scalar=WC[:, tb * 8 + ex: tb * 8 + ex + 1],
                    in1=X[:, tb * 1024 + half * 512: tb * 1024 + half * 512 + 512],
                    op0=ALU.mult, op1=ALU.add), inc=(s, 1))
    return s


def emit_perf_a(cx, dram, outs, dep):
    R, t = cx.R, cx.t
    X, XN = t["X"], t["XN"]
    s = R.sem("pa")
    s_q, m_q = _mm_proj_pairs(cx, dram["W1qP"], 8, 128, t["QD"], lambda pr: None,
                              list(dep), "paq", dep_w=list(dep))
    R.op("sync", lambda e: e.dma_start(
        out=outs["QD1"].rearrange("(pr p) c -> p pr c", p=128),
        in_=t["QD"].rearrange("p (pr c) -> p pr c", pr=8)), inc=(s, 16))
    s_k, m_k, s_kd = emit_k_proj(cx, dram["W1kP"], None, outs["KD1"], list(dep), [], [], "pak")
    R.op("gpsimd", lambda e: e.memset(
        t["VAP"].rearrange("p (tb h c) -> p tb h c", tb=16, h=16)[:, :, :, 64:65], 1.0))
    for kc in range(8):
        R.op("gpsimd", lambda e, kc=kc: e.dma_start(
            out=t["W0"][:, 0:1024], in_=dram["W1v"][kc * 128:(kc + 1) * 128, :]), inc=(s, 16))
        for tb in range(NTB):
            for half in range(2):
                R.op("tensor", lambda e, kc=kc, tb=tb, half=half: e.matmul(
                    cx.pb[2 + half][:, 0:512],
                    XN[:, kc * NT + tb * 128: kc * NT + tb * 128 + 128],
                    t["W0"][:, half * 512:(half + 1) * 512],
                    start=(kc == 0), stop=(kc == 7)))
                if kc == 7:
                    R.op("scalar", lambda e, tb=tb, half=half: e.copy(
                        out=t["VAP"][:, tb * 1040 + half * 520: tb * 1040 + (half + 1) * 520]
                        .rearrange("p (h c) -> p h c", h=8)[:, :, 0:64],
                        in_=cx.pb[2 + half][:, 0:512].rearrange("p (h c) -> p h c", h=8)))
    R.op("sync", lambda e: e.dma_start(
        out=outs["VA1"].rearrange("(tb p) c -> p tb c", p=128),
        in_=t["VAP"].rearrange("p (tb c) -> p tb c", tb=16)), inc=(s, 16))
    for h in range(16):
        pr, hh = h // 2, h % 2
        R.op("gpsimd", lambda e, h=h: e.dma_start(out=t["PM0"][:], in_=dram["pmT"][h]), inc=(s, 16))
        R.op("scalar", lambda e: e.mul(out=t["PM1"][:], in_=t["PM0"][:], mul=0.125))
        for tb in range(NTB):
            qsl0 = t["QD"][hh * 64:hh * 64 + 64, pr * NT + tb * 128: pr * NT + tb * 128 + 128]
            R.op("vector", lambda e, qsl0=qsl0: e.tensor_copy(out=t["KSTR1"][0:64, 0:128], in_=qsl0))
            qsl = t["KSTR1"][0:64, 0:128]
            ksl_dram = outs["KD1"][pr * 128 + hh * 64: pr * 128 + hh * 64 + 64,
                                   tb * 128:(tb + 1) * 128]
            R.op("sync", lambda e, ksl_dram=ksl_dram: e.dma_start(
                out=t["KSTR0"][0:64, 0:128], in_=ksl_dram), inc=(s, 16))
            R.op("tensor", lambda e: e.matmul(cx.pb[0][:, 0:128], t["KSTR0"][0:64, 0:128],
                                              t["PM1"][:], start=True, stop=True))
            R.op("vector", lambda e: e.reduce_max(t["NEGD"][:, 3:4], cx.pb[0][:, 0:128], axis=AX.X))
            if tb == 0:
                R.op("vector", lambda e, h=h: e.tensor_copy(out=t["MX"][:, h % 16:h % 16 + 1],
                                                            in_=t["NEGD"][:, 3:4]))
            else:
                R.op("vector", lambda e, h=h: e.tensor_tensor(
                    out=t["MX"][:, h % 16:h % 16 + 1], in0=t["MX"][:, h % 16:h % 16 + 1],
                    in1=t["NEGD"][:, 3:4], op=ALU.max))
            R.op("tensor", lambda e: e.transpose(cx.pt[0][:, 0:64], t["KSTR0"][0:64, 0:128],
                                                 t["CONSTB"][0:64, 0:64]))
            R.op("vector", lambda e: e.tensor_copy(out=t["KT64"][:], in_=cx.pt[0][:, 0:64]))
            R.op("scalar", lambda e: e.activation(out=t["K264"][:], in_=t["KT64"][:], func=AF.Square))
            R.op("vector", lambda e: e.reduce_sum(t["NEGD"][:, 4:5], t["K264"][:], axis=AX.X))
            R.op("vector", lambda e, h=h, tb=tb: e.tensor_scalar(
                out=t["NEGD"][:, 16 + h:17 + h], in0=t["NEGD"][:, 4:5],
                scalar1=0.5, scalar2=None, op0=ALU.mult))
            R.op("tensor", lambda e, qsl=qsl: e.matmul(cx.pb[1][:, 0:128], qsl, t["PM1"][:],
                                                       start=True, stop=True))
            R.op("vector", lambda e: e.reduce_max(t["NEGD"][:, 5:6], cx.pb[1][:, 0:128],
                                                  axis=AX.X, negate=True))
            R.op("tensor", lambda e, qsl=qsl: e.transpose(cx.pt[0][:, 0:64], qsl,
                                                          t["CONSTB"][0:64, 0:64]))
            R.op("vector", lambda e: e.tensor_copy(out=t["KT64"][:], in_=cx.pt[0][:, 0:64]))
            R.op("scalar", lambda e: e.activation(out=t["K264"][:], in_=t["KT64"][:], func=AF.Square))
            R.op("vector", lambda e: e.reduce_sum(t["NEGD"][:, 6:7], t["K264"][:], axis=AX.X))
            R.op("vector", lambda e: e.scalar_tensor_tensor(
                out=t["NEGD"][:, 7:8], in0=t["NEGD"][:, 6:7], scalar=-0.5,
                in1=t["NEGD"][:, 5:6], op0=ALU.mult, op1=ALU.add))
            R.op("vector", lambda e, h=h: e.tensor_scalar(
                out=t["SG"][:, h:h + 1], in0=t["NEGD"][:, 7:8],
                scalar1=1.0, scalar2=-LN16, op0=ALU.mult, op1=ALU.add))
            if h == 15:
                R.op("sync", lambda e, tb=tb: e.dma_start(
                    out=outs["NEGDQ"][tb * 128:(tb + 1) * 128, :], in_=t["SG"][:, 0:16]),
                    inc=(s, 16))
                R.op("sync", lambda e, tb=tb: e.dma_start(
                    out=outs["DIAGK"][tb * 128:(tb + 1) * 128, :], in_=t["NEGD"][:, 16:32]),
                    inc=(s, 16))
    R.op("vector", lambda e: e.tensor_copy(out=t["MKB"][:, 0:16], in_=t["MX"][:, 0:16]))
    R.op("tensor", lambda e: e.transpose(cx.pt[0][0:16, 0:128], t["MKB"][:, 0:16],
                                         t["CONSTB"][:, 0:128]))
    R.op("vector", lambda e: e.tensor_copy(out=t["SR16"][0:16, 0:128], in_=cx.pt[0][0:16, 0:128]))
    R.op("vector", lambda e: e.reduce_max(t["ST16"][0:16, 0:1], t["SR16"][0:16, 0:128], axis=AX.X))
    R.op("vector", lambda e: e.tensor_copy(out=t["MKB"][0:16, 16:17], in_=t["ST16"][0:16, 0:1]))
    R.op("tensor", lambda e: e.transpose(cx.pt[0][0:1, 0:16], t["MKB"][0:16, 16:17],
                                         t["CONSTB"][0:16, 0:16]))
    R.op("vector", lambda e: e.tensor_copy(out=t["NEGD"][0:1, 40:41] if False else t["STATS"][0:1, 0:16], in_=cx.pt[0][0:1, 0:16]))
    R.op("sync", lambda e: e.dma_start(out=outs["STABL"], in_=t["STATS"][0:1, 0:16]), inc=(s, 16))
    return s


def emit_perf_b(cx, dram, ins_d, outs, dep):
    R, t = cx.R, cx.t
    s = R.sem("pb_")
    R.op("sync", lambda e: e.dma_start(out=t["STATS"][0:1, 0:64], in_=ins_d["STABG"]),
         wait=dep, inc=(s, 16))
    R.op("vector", lambda e: e.tensor_tensor(out=t["STATS"][0:1, 64:80],
                                             in0=t["STATS"][0:1, 0:16],
                                             in1=t["STATS"][0:1, 16:32], op=ALU.max))
    R.op("vector", lambda e: e.tensor_tensor(out=t["STATS"][0:1, 80:96],
                                             in0=t["STATS"][0:1, 32:48],
                                             in1=t["STATS"][0:1, 48:64], op=ALU.max))
    R.op("vector", lambda e: e.tensor_tensor(out=t["STATS"][0:1, 96:112],
                                             in0=t["STATS"][0:1, 64:80],
                                             in1=t["STATS"][0:1, 80:96], op=ALU.max))
    R.op("vector", lambda e: e.tensor_copy(out=t["BR"][0:1, 0:16], in_=t["STATS"][0:1, 96:112]))
    for h in range(16):
        R.op("tensor", lambda e, h=h: e.matmul(cx.pb[2][:, h:h + 1], t["ONE1"][0:1, 0:128],
                                               t["BR"][0:1, h:h + 1], start=True, stop=True))
    R.op("vector", lambda e: e.tensor_copy(out=t["NEGD"][:, 0:16], in_=cx.pb[2][:, 0:16]))
    R.op("sync", lambda e: e.dma_start(
        out=t["VAP"].rearrange("p (tb c) -> p tb c", tb=16),
        in_=ins_d["VA1o"].rearrange("(tb p) c -> p tb c", p=128)), inc=(s, 16))
    R.op("sync", lambda e: e.dma_start(
        out=t["KVB"][:, 0:256].rearrange("p (tb c) -> p tb c", tb=16),
        in_=ins_d["DIAGKo"].rearrange("(tb p) c -> p tb c", p=128)), inc=(s, 16))
    for h in range(16):
        pr, hh = h // 2, h % 2
        R.op("gpsimd", lambda e, h=h: e.dma_start(out=t["PM0"][:], in_=dram["pmT"][h]), inc=(s, 16))
        R.op("scalar", lambda e: e.mul(out=t["PM1"][:], in_=t["PM0"][:], mul=0.125))
        R.op("sync", lambda e, pr=pr, hh=hh: e.dma_start(
            out=t["KSTR0"][0:64, 0:2048],
            in_=ins_d["KD1o"][pr * 128 + hh * 64: pr * 128 + hh * 64 + 64, :]), inc=(s, 16))
        for tb in range(NTB):
            R.op("vector", lambda e, h=h, tb=tb: e.tensor_tensor(
                out=t["NEGD"][:, 33:34], in0=t["KVB"][:, tb * 16 + h: tb * 16 + h + 1],
                in1=t["NEGD"][:, h:h + 1], op=ALU.add))
            R.op("vector", lambda e: e.tensor_scalar(out=t["NEGD"][:, 34:35], in0=t["NEGD"][:, 33:34],
                                                     scalar1=-1.0, scalar2=-LN16,
                                                     op0=ALU.mult, op1=ALU.add))
            R.op("tensor", lambda e, tb=tb: e.matmul(
                cx.pb[0][:, 0:128], t["KSTR0"][0:64, tb * 128:(tb + 1) * 128],
                t["PM1"][:], start=True, stop=True))
            R.op("scalar", lambda e: e.activation(out=t["KF"][:, 0:128], in_=cx.pb[0][:, 0:128],
                                                  func=AF.Exp, bias=t["NEGD"][:, 34:35], scale=1.0))
            R.op("scalar", lambda e: e.activation(out=t["KF"][:, 128:256], in_=cx.pb[0][:, 0:128],
                                                  func=AF.Exp, bias=t["NEGD"][:, 34:35], scale=-1.0))
            for sgn in range(2):
                R.op("tensor", lambda e, tb=tb, sgn=sgn, h=h: e.matmul(
                    cx.pb[2 + sgn][0:128, 0:65],
                    t["KF"][:, sgn * 128:(sgn + 1) * 128],
                    t["VAP"][:, tb * 1040 + h * 65: tb * 1040 + h * 65 + 65],
                    start=(tb == 0), stop=(tb == NTB - 1)))
        for sgn in range(2):
            R.op("vector", lambda e, sgn=sgn: e.tensor_copy(out=t["KVS"][:, sgn * 65:(sgn + 1) * 65],
                                                            in_=cx.pb[2 + sgn][0:128, 0:65]))
            R.op("sync", lambda e, h=h, sgn=sgn: e.dma_start(
                out=outs["KVL"][h * 256 + sgn * 128: h * 256 + sgn * 128 + 128, :],
                in_=t["KVS"][:, sgn * 65:(sgn + 1) * 65]), inc=(s, 16))
    return s


def emit_perf_c(cx, dram, ins_d, dep):
    R, t = cx.R, cx.t
    X = t["X"]
    s = R.sem("pc_")
    R.op("sync", lambda e: e.dma_start(
        out=t["KVA"].rearrange("p (g c) -> p g c", g=32),
        in_=ins_d["KV4"][0].rearrange("(g p) c -> p g c", p=128)), wait=dep, inc=(s, 16))
    for r in range(1, 4):
        R.op("sync", lambda e, r=r: e.dma_start(
            out=t["KVB"].rearrange("p (g c) -> p g c", g=32),
            in_=ins_d["KV4"][r].rearrange("(g p) c -> p g c", p=128)), inc=(s, 16))
        R.op("vector", lambda e: e.tensor_tensor(out=t["KVA"][:], in0=t["KVA"][:],
                                                 in1=t["KVB"][:], op=ALU.add))
    R.op("vector", lambda e: e.tensor_copy(out=t["KSTR1"][:, 0:2015], in_=t["KVA"][:, 0:2015]))
    R.op("vector", lambda e: e.tensor_copy(out=t["MKB"][:, 0:65], in_=t["KVA"][:, 2015:2080]))
    R.op("sync", lambda e: e.dma_start(
        out=t["QD"].rearrange("p (pr c) -> p pr c", pr=8),
        in_=ins_d["QD1o"].rearrange("(pr p) c -> p pr c", p=128)), inc=(s, 16))
    R.op("sync", lambda e: e.dma_start(
        out=t["KVB"][:, 0:256].rearrange("p (tb c) -> p tb c", tb=16),
        in_=ins_d["NEGDQo"].rearrange("(tb p) c -> p tb c", p=128)), inc=(s, 16))
    for h in range(16):
        pr, hh = h // 2, h % 2
        R.op("gpsimd", lambda e, h=h: e.dma_start(out=t["PM0"][:], in_=dram["pmT"][h]), inc=(s, 16))
        R.op("scalar", lambda e: e.mul(out=t["PM1"][:], in_=t["PM0"][:], mul=0.125))
        for tb in range(NTB):
            R.op("vector", lambda e, tb=tb, h=h: e.tensor_copy(
                out=t["NEGD"][:, 0:1], in_=t["KVB"][:, tb * 16 + h: tb * 16 + h + 1]))
            qsl0 = t["QD"][hh * 64:hh * 64 + 64, pr * NT + tb * 128: pr * NT + tb * 128 + 128]
            R.op("vector", lambda e, qsl0=qsl0: e.tensor_copy(out=t["QFT"][0:64, 0:128], in_=qsl0))
            qsl = t["QFT"][0:64, 0:128]
            R.op("tensor", lambda e, qsl=qsl: e.matmul(cx.pb[0][:, 0:128], qsl, t["PM1"][:],
                                                       start=True, stop=True))
            R.op("scalar", lambda e: e.activation(out=t["KF"][:, 0:128], in_=cx.pb[0][:, 0:128],
                                                  func=AF.Exp, bias=t["NEGD"][:, 0:1], scale=1.0))
            R.op("scalar", lambda e: e.activation(out=t["KF"][:, 128:256], in_=cx.pb[0][:, 0:128],
                                                  func=AF.Exp, bias=t["NEGD"][:, 0:1], scale=-1.0))
            for sgn in range(2):
                R.op("tensor", lambda e, sgn=sgn: e.transpose(
                    cx.pt[0][:, sgn * 128:(sgn + 1) * 128],
                    t["KF"][:, sgn * 128:(sgn + 1) * 128], t["CONSTB"][:, 0:128]))
            R.op("vector", lambda e: e.tensor_copy(out=t["QFT"][:], in_=cx.pt[0][:, 0:256]))
            for sgn in range(2):
                g_ = h * 2 + sgn
                if g_ < 31:
                    rhs = t["KSTR1"][:, g_ * 65: g_ * 65 + 65]
                else:
                    rhs = t["MKB"][:, 0:65]
                R.op("tensor", lambda e, sgn=sgn, rhs=rhs: e.matmul(
                    cx.pb[1][:, 0:65], t["QFT"][:, sgn * 128:(sgn + 1) * 128], rhs,
                    start=(sgn == 0), stop=(sgn == 1)))
            R.op("vector", lambda e: e.tensor_copy(out=t["NEGD"][:, 1:2], in_=cx.pb[1][:, 64:65]))
            R.op("vector", lambda e: e.tensor_scalar(out=t["NEGD"][:, 2:3], in0=t["NEGD"][:, 1:2],
                                                     scalar1=1.0, scalar2=EPS_PERF,
                                                     op0=ALU.mult, op1=ALU.add))
            R.op("vector", lambda e: e.reciprocal(t["NEGD"][:, 3:4], t["NEGD"][:, 2:3]))
            R.op("vector", lambda e: e.tensor_scalar(out=t["ATN"][:], in0=cx.pb[1][:, 0:64],
                                                     scalar1=t["NEGD"][:, 3:4], scalar2=None,
                                                     op0=ALU.mult))
            R.op("tensor", lambda e: e.transpose(cx.pt[0][:, 0:128] if False else cx.pt[0][0:64, 256:384],
                                                 t["ATN"][:], t["CONSTB"][:, 0:128]))
            R.op("vector", lambda e, h=h, tb=tb, hh=hh, pr=pr: e.tensor_copy(
                out=att_alias(cx)[hh * 64:hh * 64 + 64, pr * 2048 + tb * 128: pr * 2048 + tb * 128 + 128],
                in_=cx.pt[0][0:64, 256:384]))
    R.op("gpsimd", lambda e: e.dma_start(
        out=t["WEB"].rearrange("p (c n) -> p c n", c=8),
        in_=dram["W1o"].rearrange("(c p) n -> p c n", p=128)), inc=(s, 16))
    ATT = att_alias(cx)
    for tb in range(NTB):
        for half in range(2):
            for c in range(8):
                R.op("tensor", lambda e, tb=tb, half=half, c=c: e.matmul(
                    cx.pb[2][:, 0:512],
                    ATT[:, c * 2048 + tb * 128: c * 2048 + tb * 128 + 128],
                    t["WEB"][:, c * 1024 + half * 512: c * 1024 + half * 512 + 512],
                    start=(c == 0), stop=(c == 7)))
            R.op("vector", lambda e, tb=tb, half=half: e.tensor_tensor(
                out=X[:, tb * 1024 + half * 512: tb * 1024 + half * 512 + 512],
                in0=X[:, tb * 1024 + half * 512: tb * 1024 + half * 512 + 512],
                in1=cx.pb[2][:, 0:512], op=ALU.add), inc=(s, 1))
    return s


def emit_ffn(cx, dram, out_dram, dep):
    R, t = cx.R, cx.t
    X, XN = t["X"], t["XN"]
    s = R.sem("ff")
    for oc in range(8):
        R.op("gpsimd", lambda e, oc=oc: e.dma_start(
            out=t["W0"][:, 0:1024], in_=dram["Wf"][oc * 128:(oc + 1) * 128, :]),
            wait=dep, inc=(s, 16))
        R.op("scalar", lambda e: e.sign(out=t["W1"][:, 0:1024], in_=t["W0"][:, 0:1024]))
        for ic in range(8):
            R.op("tensor", lambda e, ic=ic: e.transpose(
                cx.pt[0][:, 0:128], t["W1"][:, ic * 128:(ic + 1) * 128], t["CONSTB"][:, 0:128]))
            R.op("vector", lambda e, ic=ic, oc=oc: e.tensor_copy(
                out=t["WEB"][:, ic * 1024 + oc * 128: ic * 1024 + oc * 128 + 128],
                in_=cx.pt[0][:, 0:128]))
    for tb in range(NTB):
        for half in range(2):
            for ic in range(8):
                R.op("tensor", lambda e, tb=tb, half=half, ic=ic: e.matmul(
                    cx.pb[0][:, 0:512],
                    XN[:, ic * NT + tb * 128: ic * NT + tb * 128 + 128],
                    t["WEB"][:, ic * 1024 + half * 512: ic * 1024 + half * 512 + 512],
                    start=(ic == 0), stop=(ic == 7)))
            osb = t["OS0"] if tb % 2 == 0 else t["OS1"]
            R.op("vector", lambda e, tb=tb, half=half, osb=osb: e.tensor_tensor(
                out=osb[:, half * 512:(half + 1) * 512],
                in0=X[:, tb * 1024 + half * 512: tb * 1024 + half * 512 + 512],
                in1=cx.pb[0][:, 0:512], op=ALU.add))
        R.op("sync", lambda e, tb=tb, osb=osb: e.dma_start(
            out=out_dram[tb * 128:(tb + 1) * 128, :], in_=osb[:]), inc=(s, 16))
    return s


def np_ln(x):
    m = x.mean(-1, keepdims=True)
    v = x.var(-1, keepdims=True)
    return (x - m) / np.sqrt(v + EPS_LN)


NPDT = {F32: np.float32, BF16: BF}


def declare_inputs(nc, arrs, names):
    d = {}
    for n in names:
        a = arrs[n]
        dt = BF16 if a.dtype == BF else F32
        d[n] = nc.dram_tensor(n, list(a.shape), dt, kind="ExternalInput").ap()
    return d


def emit_store(cx, dram_out, sbuf_ap, wait, s_st):
    cx.R.op("sync", lambda e: e.dma_start(out=dram_out, in_=sbuf_ap), wait=wait, inc=(s_st, 16))


SHP = {"WqP": [8, 128, 8, 128], "WkP": [8, 128, 8, 128], "WvdP": [4, 128, 8, 64],
       "bqP": [128, 8], "bkP": [128, 8], "bvdP": [64, 4],
       "identb": [128, 128], "tri": [128, 128],
       "Wvu2": [8, 32, 128], "bvu2": [8, 1, 128],
       "WeP": [8, 1024, 1024], "beP": [8, 1, 1024], "WgP": [1024, 8], "bgP": [1, 8],
       "W1qP": [8, 128, 8, 128], "W1kP": [8, 128, 8, 128], "W1v": [1024, 1024],
       "pmT": [16, 64, 128], "W1o": [1024, 1024], "Wf": [1024, 1024]}
BF16_CONSTS = {"identb", "tri"}


def _decl(nc, names):
    return {n: nc.dram_tensor(n, SHP[n], BF16 if n in BF16_CONSTS else F32,
                              kind="ExternalInput").ap() for n in names}


def _fin(R, sems):
    R.op("sync", None, wait=[(s, s.v) for s in sems])


def build_L1():
    from contextlib import ExitStack
    nc = bass.Bass("TRN2", target_bir_lowering=False, debug=False)
    with ExitStack() as stack:
        xd = nc.dram_tensor("x", [NT, 1024], F32, kind="ExternalInput").ap()
        dram = _decl(nc, ["WkP", "WvdP", "bqP", "bkP", "bvdP", "identb", "tri"])
        oKD = nc.dram_tensor("oKD", [1024, NT], BF16, kind="ExternalOutput").ap()
        oVD = nc.dram_tensor("oVD", [256, NT], BF16, kind="ExternalOutput").ap()
        cx = alloc_common(nc, stack)
        R = cx.R
        ldc = emit_consts(cx, dram)
        ldx = emit_load_x(cx, xd)
        s_ev = emit_ln(cx, [(ldx, 16), (ldc, ldc.v)])
        s_k, m_k, s_kdma = emit_k_proj(cx, dram["WkP"], 8, oKD, [(s_ev, NTB)],
                                       [(ldc, ldc.v)], [(s_ev, NTB)], "p0k")
        s_v, m_v, s_vdma = emit_l0_vd(cx, dram, oVD, [(s_ev, NTB)], [(ldc, ldc.v)],
                                      [(m_k, 16)])
        _fin(R, [s_kdma, s_vdma])
        R.flush(nc, stack)
    return nc


def build_L2(m):
    from contextlib import ExitStack
    nc = bass.Bass("TRN2", target_bir_lowering=False, debug=False)
    with ExitStack() as stack:
        xd = nc.dram_tensor("x", [NT, 1024], F32, kind="ExternalInput").ap()
        dram = _decl(nc, ["WqP", "bqP", "bkP", "bvdP", "identb", "tri", "Wvu2", "bvu2"])
        KGd = nc.dram_tensor("KG", [1024, 8192], BF16, kind="ExternalInput").ap()
        VDGd = nc.dram_tensor("VDG", [256, 8192], BF16, kind="ExternalInput").ap()
        oX1 = nc.dram_tensor("oX1", [NT, 1024], BF16, kind="ExternalOutput").ap()
        cx = alloc_common(nc, stack)
        R = cx.R
        ldc = emit_consts(cx, dram)
        ldx = emit_load_x(cx, xd)
        s_ev = emit_ln(cx, [(ldx, 16), (ldc, ldc.v)])
        s_q, m_q = emit_l0_proj(cx, dram, [(s_ev, NTB)], [(ldc, ldc.v)])
        kg_fn = lambda g, ke: KGd[g * 128:(g + 1) * 128, ke * 1024:(ke + 1) * 1024]
        vdg_fn = lambda g, ke: VDGd[(g // 2) * 64 + (g % 2) * 32:(g // 2) * 64 + (g % 2) * 32 + 32,
                                    ke * 1024:(ke + 1) * 1024]
        s_stt = emit_l0_attn(cx, dram, kg_fn, vdg_fn, [m, 7 - m],
                             dep_q=[(s_q, s_q.v)], dep_kv=[], dep_x=[(s_ev, NTB)],
                             dep_pb=[(s_q, s_q.v)], dep_c=[(ldc, ldc.v)])
        st = R.sem("st2")
        R.op("sync", lambda e: e.dma_start(
            out=oX1.rearrange("(tb p) c -> p tb c", p=128),
            in_=cx.t["X"].rearrange("p (tb c) -> p tb c", tb=NTB)),
            wait=[(s_stt, s_stt.v)], inc=(st, 16))
        _fin(R, [st])
        R.flush(nc, stack)
    return nc


def build_L3():
    from contextlib import ExitStack
    nc = bass.Bass("TRN2", target_bir_lowering=False, debug=False)
    with ExitStack() as stack:
        xd = nc.dram_tensor("x", [NT, 1024], BF16, kind="ExternalInput").ap()
        dram = _decl(nc, ["WeP", "beP", "WgP", "bgP", "identb", "tri", "bqP", "bkP", "bvdP"])
        oX2 = nc.dram_tensor("oX2", [NT, 1024], BF16, kind="ExternalOutput").ap()
        cx = alloc_common(nc, stack)
        R = cx.R
        ldc = emit_consts(cx, dram)
        ldx = emit_load_x(cx, xd, cast=False)
        s_ev = emit_ln(cx, [(ldx, 16), (ldc, ldc.v)])
        s_m = emit_moe(cx, dram, [(s_ev, NTB)])
        st = R.sem("st3")
        R.op("sync", lambda e: e.dma_start(
            out=oX2.rearrange("(tb p) c -> p tb c", p=128),
            in_=cx.t["X"].rearrange("p (tb c) -> p tb c", tb=NTB)),
            wait=[(s_m, s_m.v)], inc=(st, 16))
        _fin(R, [st])
        R.flush(nc, stack)
    return nc


def build_L4a():
    from contextlib import ExitStack
    nc = bass.Bass("TRN2", target_bir_lowering=False, debug=False)
    with ExitStack() as stack:
        xd = nc.dram_tensor("x", [NT, 1024], BF16, kind="ExternalInput").ap()
        dram = _decl(nc, ["W1qP", "W1kP", "W1v", "pmT", "identb", "tri", "bqP", "bkP", "bvdP"])
        outs = {
            "QD1": nc.dram_tensor("oQD1", [1024, NT], BF16, kind="ExternalOutput").ap(),
            "KD1": nc.dram_tensor("oKD1", [1024, NT], BF16, kind="ExternalOutput").ap(),
            "VA1": nc.dram_tensor("oVA1", [NT, 1040], BF16, kind="ExternalOutput").ap(),
            "NEGDQ": nc.dram_tensor("oNEGDQ", [NT, 16], F32, kind="ExternalOutput").ap(),
            "DIAGK": nc.dram_tensor("oDIAGK", [NT, 16], F32, kind="ExternalOutput").ap(),
            "STABL": nc.dram_tensor("oSTABL", [1, 16], F32, kind="ExternalOutput").ap(),
        }
        cx = alloc_common(nc, stack)
        R = cx.R
        ldc = emit_consts(cx, dram)
        ldx = emit_load_x(cx, xd, cast=False)
        s_ev = emit_ln(cx, [(ldx, 16), (ldc, ldc.v)])
        s = emit_perf_a(cx, dram, outs, [(s_ev, NTB)])
        _fin(R, [s])
        R.flush(nc, stack)
    return nc


def build_L4b():
    from contextlib import ExitStack
    nc = bass.Bass("TRN2", target_bir_lowering=False, debug=False)
    with ExitStack() as stack:
        dram = _decl(nc, ["pmT", "identb", "tri", "bqP", "bkP", "bvdP"])
        ins_d = {
            "KD1o": nc.dram_tensor("KD1o", [1024, NT], BF16, kind="ExternalInput").ap(),
            "VA1o": nc.dram_tensor("VA1o", [NT, 1040], BF16, kind="ExternalInput").ap(),
            "DIAGKo": nc.dram_tensor("DIAGKo", [NT, 16], F32, kind="ExternalInput").ap(),
            "STABG": nc.dram_tensor("STABG", [1, 64], F32, kind="ExternalInput").ap(),
        }
        oKVL = nc.dram_tensor("oKVL", [4096, 65], F32, kind="ExternalOutput").ap()
        cx = alloc_common(nc, stack)
        R = cx.R
        ldc = emit_consts(cx, dram)
        s = emit_perf_b(cx, dram, ins_d, {"KVL": oKVL}, [(ldc, ldc.v)])
        _fin(R, [s])
        R.flush(nc, stack)
    return nc


def build_L4c():
    from contextlib import ExitStack
    nc = bass.Bass("TRN2", target_bir_lowering=False, debug=False)
    with ExitStack() as stack:
        xd = nc.dram_tensor("x", [NT, 1024], BF16, kind="ExternalInput").ap()
        dram = _decl(nc, ["pmT", "W1o", "identb", "tri", "bqP", "bkP", "bvdP"])
        ins_d = {
            "QD1o": nc.dram_tensor("QD1o", [1024, NT], BF16, kind="ExternalInput").ap(),
            "NEGDQo": nc.dram_tensor("NEGDQo", [NT, 16], F32, kind="ExternalInput").ap(),
            "KV4": nc.dram_tensor("KV4", [4, 4096, 65], F32, kind="ExternalInput").ap(),
        }
        oX3 = nc.dram_tensor("oX3", [NT, 1024], BF16, kind="ExternalOutput").ap()
        cx = alloc_common(nc, stack)
        R = cx.R
        ldc = emit_consts(cx, dram)
        ldx = emit_load_x(cx, xd, cast=False)
        s = emit_perf_c(cx, dram, ins_d, [(ldx, 16), (ldc, ldc.v)])
        st = R.sem("st4c")
        R.op("sync", lambda e: e.dma_start(
            out=oX3.rearrange("(tb p) c -> p tb c", p=128),
            in_=cx.t["X"].rearrange("p (tb c) -> p tb c", tb=NTB)),
            wait=[(s, s.v)], inc=(st, 16))
        _fin(R, [st])
        R.flush(nc, stack)
    return nc


def build_L5():
    from contextlib import ExitStack
    nc = bass.Bass("TRN2", target_bir_lowering=False, debug=False)
    with ExitStack() as stack:
        xd = nc.dram_tensor("x", [NT, 1024], BF16, kind="ExternalInput").ap()
        dram = _decl(nc, ["Wf", "identb", "tri", "bqP", "bkP", "bvdP"])
        oOut = nc.dram_tensor("oOut", [NT, 1024], F32, kind="ExternalOutput").ap()
        cx = alloc_common(nc, stack)
        R = cx.R
        ldc = emit_consts(cx, dram)
        ldx = emit_load_x(cx, xd, cast=False)
        s_ev = emit_ln(cx, [(ldx, 16), (ldc, ldc.v)])
        s = emit_ffn(cx, dram, oOut, [(s_ev, NTB)])
        _fin(R, [s])
        R.flush(nc, stack)
    return nc


def kernel(**ins):
    P = pack_consts(ins)
    x = np.asarray(ins["L0_attn_x"], np.float32)
    CN1 = ["WkP", "WvdP", "bqP", "bkP", "bvdP", "identb", "tri"]
    nc1 = build_L1()
    im1 = [dict(x=shard_rows(x, c), **{n: P[n] for n in CN1}) for c in range(8)]
    r1 = run_bass_kernel_spmd(nc1, im1, core_ids=list(range(8))).results
    KGs, VDGs = {}, {}
    for g in range(2):
        KG = np.zeros((1024, 8192), P["identb"].dtype)
        VDG = np.zeros((256, 8192), P["identb"].dtype)
        for mm in range(4):
            c = g * 4 + mm
            for ei, ge in enumerate([mm, 7 - mm]):
                KG[:, ge * 1024:(ge + 1) * 1024] = r1[c]["oKD"][:, ei * 1024:(ei + 1) * 1024]
                VDG[:, ge * 1024:(ge + 1) * 1024] = r1[c]["oVD"][:, ei * 1024:(ei + 1) * 1024]
        KGs[g], VDGs[g] = KG, VDG
    CN2 = ["WqP", "bqP", "bkP", "bvdP", "identb", "tri", "Wvu2", "bvu2"]
    x1s = [None] * 8
    for mm in range(4):
        nc2 = build_L2(mm)
        cores = [mm, 4 + mm]
        im2 = [dict(x=shard_rows(x, c), KG=KGs[c // 4], VDG=VDGs[c // 4],
                    **{n: P[n] for n in CN2}) for c in cores]
        r2 = run_bass_kernel_spmd(nc2, im2, core_ids=cores).results
        for i, c in enumerate(cores):
            x1s[c] = r2[i]["oX1"]
    CN3 = ["WeP", "beP", "WgP", "bgP", "identb", "tri", "bqP", "bkP", "bvdP"]
    nc3 = build_L3()
    im3 = [dict(x=x1s[c], **{n: P[n] for n in CN3}) for c in range(8)]
    r3 = run_bass_kernel_spmd(nc3, im3, core_ids=list(range(8))).results
    x2s = [r3[c]["oX2"] for c in range(8)]
    CN4a = ["W1qP", "W1kP", "W1v", "pmT", "identb", "tri", "bqP", "bkP", "bvdP"]
    nc4a = build_L4a()
    im4a = [dict(x=x2s[c], **{n: P[n] for n in CN4a}) for c in range(8)]
    r4a = run_bass_kernel_spmd(nc4a, im4a, core_ids=list(range(8))).results
    CN4b = ["pmT", "identb", "tri", "bqP", "bkP", "bvdP"]
    nc4b = build_L4b()
    im4b = []
    for c in range(8):
        g = c // 4
        stabg = np.concatenate([r4a[g * 4 + j]["oSTABL"] for j in range(4)], axis=1)
        im4b.append(dict(KD1o=r4a[c]["oKD1"], VA1o=r4a[c]["oVA1"],
                         DIAGKo=r4a[c]["oDIAGK"], STABG=stabg,
                         **{n: P[n] for n in CN4b}))
    r4b = run_bass_kernel_spmd(nc4b, im4b, core_ids=list(range(8))).results
    CN4c = ["pmT", "W1o", "identb", "tri", "bqP", "bkP", "bvdP"]
    nc4c = build_L4c()
    im4c = []
    for c in range(8):
        g = c // 4
        kv4 = np.stack([r4b[g * 4 + j]["oKVL"] for j in range(4)], axis=0)
        im4c.append(dict(x=x2s[c], QD1o=r4a[c]["oQD1"], NEGDQo=r4a[c]["oNEGDQ"],
                         KV4=kv4, **{n: P[n] for n in CN4c}))
    r4c = run_bass_kernel_spmd(nc4c, im4c, core_ids=list(range(8))).results
    x3s = [r4c[c]["oX3"] for c in range(8)]
    CN5 = ["Wf", "identb", "tri", "bqP", "bkP", "bvdP"]
    nc5 = build_L5()
    im5 = [dict(x=x3s[c], **{n: P[n] for n in CN5}) for c in range(8)]
    r5 = run_bass_kernel_spmd(nc5, im5, core_ids=list(range(8))).results
    return unshard_rows([r5[c]["oOut"] for c in range(8)])


CONST_NAMES_A = ["WqP", "WkP", "WvdP", "bqP", "bkP", "bvdP", "identb", "identf", "tri"]


def build_stageA_test():
    from contextlib import ExitStack
    nc = bass.Bass("TRN2", target_bir_lowering=False, debug=False)
    stack = ExitStack()
    with stack:
        xd = nc.dram_tensor("x", [NT, 1024], F32, kind="ExternalInput").ap()
        P = {n: None for n in CONST_NAMES_A}
        shapes = {"WqP": [8, 128, 8, 128], "WkP": [8, 128, 8, 128], "WvdP": [4, 128, 8, 64],
                  "bqP": [128, 8], "bkP": [128, 8], "bvdP": [64, 4],
                  "identb": [128, 128], "identf": [128, 128], "tri": [128, 128]}
        dts = {"identb": BF16, "tri": BF16}
        dram = {n: nc.dram_tensor(n, shapes[n], dts.get(n, F32), kind="ExternalInput").ap()
                for n in CONST_NAMES_A}
        oXN = nc.dram_tensor("oXN", [128, 8 * NT], BF16, kind="ExternalOutput").ap()
        oQD = nc.dram_tensor("oQD", [128, 8 * NT], BF16, kind="ExternalOutput").ap()
        oKD = nc.dram_tensor("oKD", [1024, NT], BF16, kind="ExternalOutput").ap()
        oVD = nc.dram_tensor("oVD", [256, NT], BF16, kind="ExternalOutput").ap()
        cx = alloc_common(nc, stack)
        R = cx.R
        ldc = emit_consts(cx, dram)
        ldx = emit_load_x(cx, xd)
        s_ev = emit_ln(cx, [(ldx, 16)])
        s_q, m_q = emit_l0_proj(cx, dram, [(s_ev, NTB)], [(ldc, ldc.v)])
        s_k, m_k, s_kdma = emit_k_proj(cx, dram["WkP"], 8, oKD, [(s_ev, NTB)],
                                       [(ldc, ldc.v)], [(m_q, 16)], "p0k")
        s_v, m_v, s_vdma = emit_l0_vd(cx, dram, oVD, [(s_ev, NTB)], [(ldc, ldc.v)],
                                      [(m_k, 16)])
        st = R.sem("st")
        emit_store(cx, oXN, cx.t["XN"][:], [(s_ev, NTB)], st)
        emit_store(cx, oQD, cx.t["QD"][:], [(s_q, s_q.v)], st)
        R.op("sync", None, wait=[(st, st.v), (s_vdma, s_vdma.v), (s_kdma, s_kdma.v)])
        R.flush(nc, stack)
    return nc


CONST_NAMES_B = CONST_NAMES_A + ["Wvu2", "bvu2"]


def build_stageB_test():
    from contextlib import ExitStack
    nc = bass.Bass("TRN2", target_bir_lowering=False, debug=False)
    stack = ExitStack()
    with stack:
        xd = nc.dram_tensor("x", [NT, 1024], F32, kind="ExternalInput").ap()
        shapes = {"WqP": [8, 128, 8, 128], "WkP": [8, 128, 8, 128], "WvdP": [4, 128, 8, 64],
                  "bqP": [128, 8], "bkP": [128, 8], "bvdP": [64, 4],
                  "identb": [128, 128], "identf": [128, 128], "tri": [128, 128],
                  "Wvu2": [8, 32, 128], "bvu2": [8, 1, 128]}
        dts = {"identb": BF16, "tri": BF16}
        dram = {n: nc.dram_tensor(n, shapes[n], dts.get(n, F32), kind="ExternalInput").ap()
                for n in CONST_NAMES_B}
        KGd = nc.dram_tensor("KG", [1024, 8192], BF16, kind="ExternalInput").ap()
        VDGd = nc.dram_tensor("VDG", [256, 8192], BF16, kind="ExternalInput").ap()
        oX1 = nc.dram_tensor("oX1", [NT, 1024], BF16, kind="ExternalOutput").ap()
        oZ = nc.dram_tensor("oZ", [128, 256], F32, kind="ExternalOutput").ap()
        cx = alloc_common(nc, stack)
        R = cx.R
        ldc = emit_consts(cx, dram)
        ldx = emit_load_x(cx, xd)
        s_ev = emit_ln(cx, [(ldx, 16), (ldc, ldc.v)])
        s_q, m_q = emit_l0_proj(cx, dram, [(s_ev, NTB)], [(ldc, ldc.v)])
        kg_fn = lambda g, ke: KGd[g * 128:(g + 1) * 128, ke * 1024:(ke + 1) * 1024]
        vdg_fn = lambda g, ke: VDGd[(g // 2) * 64 + (g % 2) * 32:(g // 2) * 64 + (g % 2) * 32 + 32,
                                    ke * 1024:(ke + 1) * 1024]
        s_stt = emit_l0_attn(cx, dram, kg_fn, vdg_fn, [0, 7],
                             dep_q=[(s_q, s_q.v)], dep_kv=[], dep_x=[(s_ev, NTB)],
                             dep_pb=[(s_q, s_q.v)], dep_c=[(ldc, ldc.v)],
                             dbg={"x1banks": oX1} if _osmod.environ.get("DBGB") else None, groups=range(int(_osmod.environ.get("NGROUPS", "8"))))
        st = R.sem("st")
        if not _osmod.environ.get("DBGB"):
            R.op("sync", lambda e: e.dma_start(
                out=oX1.rearrange("(tb p) c -> p tb c", p=128),
                in_=cx.t["X"].rearrange("p (tb c) -> p tb c", tb=NTB)),
                wait=[(s_stt, s_stt.v)], inc=(st, 16))
        else:
            pass
        R.op("sync", lambda e: e.dma_start(out=oZ, in_=cx.t["ZS"][:]),
            wait=[(s_stt, s_stt.v)], inc=(st, 16))
        R.op("sync", None, wait=[(st, st.v)])
        R.flush(nc, stack)
    return nc





# version 07
# speedup vs baseline: 1.0016x; 1.0016x over previous
import os as _osmod
import numpy as np
import ml_dtypes
import concourse.bass as bass
import concourse.mybir as mybir
from concourse.bass_utils import run_bass_kernel_spmd

F32 = mybir.dt.float32
BF16 = mybir.dt.bfloat16
AF = mybir.ActivationFunctionType
ALU = mybir.AluOpType
AX = mybir.AxisListType

B, S, E = 2, 8192, 1024
H, D, RK = 16, 64, 16
LN16 = float(np.log(16.0))
STAB0 = 4.0
EPS_LN = 1e-5
EPS_PERF = 1e-6
NT = 2048
NTB = 16
SERIAL_ATT = True
RG = [[0, 1, 2, 3], [4, 5, 6, 7]]
BF = ml_dtypes.bfloat16


class Sem:
    __slots__ = ("name", "v", "h")

    def __init__(self, name):
        self.name, self.v, self.h = name, 0, None


TOTAL_ORDER = True


class Rec:
    ENGINES = ("tensor", "vector", "scalar", "gpsimd", "sync")

    def __init__(self):
        self.q = {e: [] for e in self.ENGINES}
        self.sems = []
        self.total = TOTAL_ORDER
        if self.total:
            self.G = self.sem("Gtot")

    def sem(self, name):
        s = Sem(name)
        self.sems.append(s)
        return s

    def op(self, eng, fn, wait=(), inc=None):
        if self.total:
            ws = [(self.G, self.G.v)]
            amt = inc[1] if inc is not None else 1
            if inc is not None:
                inc[0].v += inc[1]
            if fn is None:
                self.q[eng].append((ws, None, None))
                return
            self.G.v += amt
            self.q[eng].append((ws, fn, (self.G, amt)))
            return
        ws = [(w, w.v) if isinstance(w, Sem) else w for w in wait]
        if inc is not None:
            inc[0].v += inc[1]
        self.q[eng].append((ws, fn, inc))

    def flush(self, nc, stack):
        for s in self.sems:
            s.h = stack.enter_context(nc.semaphore(s.name))
        block = stack.enter_context(nc.Block())
        for eng in self.ENGINES:
            items = self.q[eng]
            if not items:
                continue

            def make(items):
                def body(e):
                    for ws, fn, inc in items:
                        for s, v in ws:
                            if v > 0:
                                e.wait_ge(s.h, v)
                        if fn is not None:
                            inst = fn(e)
                            if inc is not None:
                                inst.then_inc(inc[0].h, inc[1])
                        elif inc is not None:
                            e.nop().then_inc(inc[0].h, inc[1])
                return body
            getattr(block, eng)(make(items))


def _qk_pack(Wm):
    Wp = Wm.reshape(8, 2, 1024, 64).transpose(0, 2, 1, 3).reshape(8, 1024, 128)
    Wp = Wp.reshape(8, 8, 128, 128).transpose(0, 2, 1, 3)
    return np.ascontiguousarray(Wp, np.float32)


def pack_consts(ins):
    f32 = np.float32
    P = {}
    P["WqP"] = _qk_pack(ins["L0_attn_Wq"])
    P["WkP"] = _qk_pack(ins["L0_attn_Wk"])
    P["bqP"] = np.ascontiguousarray(ins["L0_attn_bq"].reshape(8, 128).T, f32)
    P["bkP"] = np.ascontiguousarray(ins["L0_attn_bk"].reshape(8, 128).T, f32)
    Wvd = ins["L0_attn_Wvd"].reshape(4, 4, 1024, 16).transpose(0, 2, 1, 3).reshape(4, 1024, 64)
    Wvd = Wvd.reshape(4, 8, 128, 64).transpose(0, 2, 1, 3)
    P["WvdP"] = np.ascontiguousarray(Wvd, f32)
    P["bvdP"] = np.ascontiguousarray(ins["L0_attn_bvd"].reshape(4, 64).T, f32)
    Wvu = ins["L0_attn_Wvu"]
    W2 = np.zeros((8, 32, 128), f32)
    for g in range(8):
        W2[g, 0:16, 0:64] = Wvu[2 * g]
        W2[g, 16:32, 64:128] = Wvu[2 * g + 1]
    P["Wvu2"] = W2
    P["bvu2"] = np.ascontiguousarray(ins["L0_attn_bvu"].reshape(8, 1, 128), f32)
    P["WeP"] = np.ascontiguousarray(ins["L0_ffn_We"], f32)
    P["beP"] = np.ascontiguousarray(ins["L0_ffn_be"].reshape(8, 1, 1024), f32)
    P["WgP"] = np.ascontiguousarray(ins["L0_ffn_Wg"], f32)
    P["bgP"] = np.ascontiguousarray(ins["L0_ffn_bg"].reshape(1, 8), f32)
    Wqkv = ins["L1_attn_W_qkv"]
    P["W1qP"] = _qk_pack(np.ascontiguousarray(Wqkv[:, 0:1024]).reshape(1024, 16, 64).transpose(1, 0, 2))
    P["W1kP"] = _qk_pack(np.ascontiguousarray(Wqkv[:, 1024:2048]).reshape(1024, 16, 64).transpose(1, 0, 2))
    P["W1v"] = np.ascontiguousarray(Wqkv[:, 2048:3072], f32)
    P["pmT"] = np.ascontiguousarray(ins["L1_attn_proj"].transpose(0, 2, 1), f32)
    P["W1o"] = np.ascontiguousarray(ins["L1_attn_W_out"], f32)
    P["Wf"] = np.ascontiguousarray(ins["L1_ffn_weight"], f32)
    P["identb"] = np.eye(128, dtype=BF)
    P["identf"] = np.eye(128, dtype=f32)
    P["tri"] = np.tril(np.ones((128, 128), f32)).astype(BF)
    return P


def shard_rows(x_full, c):
    g, m = c // 4, c % 4
    xb = x_full[g]
    return np.ascontiguousarray(
        np.concatenate([xb[m * 1024:(m + 1) * 1024], xb[(7 - m) * 1024:(8 - m) * 1024]], axis=0),
        np.float32)


def unshard_rows(shards):
    out = np.empty((B, S, E), np.float32)
    for c in range(8):
        g, m = c // 4, c % 4
        out[g, m * 1024:(m + 1) * 1024] = shards[c][0:1024]
        out[g, (7 - m) * 1024:(8 - m) * 1024] = shards[c][1024:2048]
    return out


class Ctx:
    pass


def alloc_common(nc, stack):
    cx = Ctx()
    cx.nc = nc
    cx.R = Rec()
    ec = stack.enter_context
    t = {}

    def sb(name, shape, dt):
        t[name] = ec(nc.sbuf_tensor(name, shape, dt))
    cx.t = t
    sb("X", [128, NTB * 1024], BF16)
    sb("OS0", [128, 1024], F32)
    sb("OS1", [128, 1024], F32)
    sb("XN", [128, 8 * NT], BF16)
    sb("QD", [128, 8 * NT], BF16)
    sb("KSTR0", [128, 2048], BF16)
    sb("KSTR1", [128, 2048], BF16)
    sb("WEB", [128, 8192], BF16)
    sb("VAP", [128, 16 * 1040], BF16)
    sb("VT0", [128, 8 * 130], BF16)
    sb("VT1", [128, 8 * 130], BF16)
    sb("VD4", [64, NT], BF16)
    sb("ES0", [128, 512], BF16)
    sb("ES1", [128, 512], BF16)
    sb("AVSB", [128, 512], BF16)
    sb("W0", [128, 1024], BF16)
    sb("W1", [128, 1024], BF16)
    sb("WR0", [32, 1024], BF16)
    sb("XT0", [128, 1024], BF16)
    sb("STATS", [128, 164], F32)
    sb("CONSTB", [128, 256], BF16)
    sb("ONE1", [1, 128], BF16)
    sb("BQK", [128, 16], F32)
    sb("CB", [128, 4], F32)
    sb("BVD", [64, 4], F32)
    sb("BR", [1, 1024], BF16)
    sb("WG", [128, 64], BF16)
    sb("BG1", [1, 8], BF16)
    sb("NEGD", [128, 48], F32)
    sb("KT64", [128, 64], BF16)
    sb("K264", [128, 64], BF16)
    sb("KVA", [128, 2080], F32)
    sb("KVB", [128, 2080], F32)
    sb("QFT", [128, 256], BF16)
    sb("ATN", [128, 64], BF16)
    sb("KF", [128, 256], BF16)
    sb("MKB", [128, 128], BF16)
    sb("ZB5", [128, 512], BF16)
    sb("EPI", [128, 528], BF16)
    sb("KVS", [128, 130], F32)
    sb("KVH0", [128, 130], BF16)
    sb("KVH1", [128, 130], BF16)
    sb("WV2", [32, 128], BF16)
    sb("PM0", [64, 128], BF16)
    sb("PM1", [64, 128], BF16)
    sb("MX", [128, 16], F32)
    sb("SG", [128, 24], F32)
    sb("SR16", [16, 128], BF16)
    sb("ZS", [128, 256], F32)
    sb("ST16", [16, 1], F32)
    cx.pb = [ec(nc.psum_tensor(f"pb{i}", [128, 512], F32)) for i in range(6)]
    cx.pt = [ec(nc.psum_tensor(f"pt{i}", [128, 1024], BF16)) for i in range(2)]
    return cx


def att_alias(cx):
    return cx.t["VAP"]


def emit_consts(cx, dram):
    R, t = cx.R, cx.t
    ld = R.sem("ldc")
    pairs = [
        (dram["identb"], t["CONSTB"][:, 0:128]),
        (dram["tri"], t["CONSTB"][:, 128:256]),
        (dram["bqP"], t["BQK"][:, 0:8]),
        (dram["bkP"], t["BQK"][:, 8:16]),
        (dram["bvdP"], t["BVD"][:]),
    ]
    for src, dst in pairs:
        R.op("sync", lambda e, src=src, dst=dst: e.dma_start(out=dst, in_=src), inc=(ld, 16))
    R.op("gpsimd", lambda e: e.memset(t["ONE1"][:], 1.0), inc=(ld, 1))
    R.op("gpsimd", lambda e: e.memset(t["CB"][:, 0:1], EPS_LN), inc=(ld, 1))
    R.op("gpsimd", lambda e: e.memset(t["CB"][:, 1:2], -STAB0), inc=(ld, 1))
    return ld


def emit_ln(cx, dep, x_src="X", tag="ln"):
    R, t = cx.R, cx.t
    X, XN = t[x_src], t["XN"]
    SM = t["STATS"]
    s_sq = R.sem(tag + "_sq")
    s_st = R.sem(tag + "_st")
    s_xn = R.sem(tag + "_xn")
    s_tr = R.sem(tag + "_tr")
    s_ev = R.sem(tag + "_ev")
    s_s2 = R.sem(tag + "_s2")
    for tb in range(NTB):
        xt = X[:, tb * 1024:(tb + 1) * 1024]
        sqb = t["W0"] if tb % 2 == 0 else t["W1"]
        xnb = t["XT0"]
        c = 10 * tb
        wsq = list(dep) + ([] if tb < 2 else [(s_s2, tb - 1)])
        R.op("scalar", lambda e, xt=xt, sqb=sqb: e.activation(out=sqb[:], in_=xt, func=AF.Square),
             wait=wsq, inc=(s_sq, 1))
        R.op("vector", lambda e, xt=xt, c=c: e.reduce_sum(SM[:, c:c + 1], xt, axis=AX.X), wait=list(dep))
        R.op("vector", lambda e, sqb=sqb, c=c: e.reduce_sum(SM[:, c + 1:c + 2], sqb[:], axis=AX.X),
             wait=[(s_sq, tb + 1)], inc=(s_s2, 1))
        R.op("vector", lambda e, c=c: e.tensor_scalar(out=SM[:, c + 2:c + 3], in0=SM[:, c:c + 1],
                                                      scalar1=1.0 / 1024, scalar2=None, op0=ALU.mult))
        R.op("vector", lambda e, c=c: e.tensor_scalar(out=SM[:, c + 3:c + 4], in0=SM[:, c + 1:c + 2],
                                                      scalar1=1.0 / 1024, scalar2=None, op0=ALU.mult))
        R.op("vector", lambda e, c=c: e.tensor_tensor(out=SM[:, c + 4:c + 5], in0=SM[:, c + 2:c + 3],
                                                      in1=SM[:, c + 2:c + 3], op=ALU.mult))
        R.op("vector", lambda e, c=c: e.tensor_tensor(out=SM[:, c + 5:c + 6], in0=SM[:, c + 3:c + 4],
                                                      in1=SM[:, c + 4:c + 5], op=ALU.subtract), inc=(s_xn, 1))
        R.op("scalar", lambda e, c=c: e.activation(out=SM[:, c + 6:c + 7], in_=SM[:, c + 5:c + 6],
                                                   func=AF.Ln, bias=t["CB"][:, 0:1]), wait=[(s_xn, 2 * tb + 1)])
        R.op("scalar", lambda e, c=c: e.activation(out=SM[:, c + 7:c + 8], in_=SM[:, c + 6:c + 7],
                                                   func=AF.Exp, scale=-0.5), inc=(s_st, 1))
        wxn = [(s_st, tb + 1)] + ([] if tb < 1 else [(s_tr, tb * 8)])
        R.op("vector", lambda e, c=c: e.scalar_tensor_tensor(out=SM[:, c + 8:c + 9], in0=SM[:, c + 2:c + 3],
                                                             scalar=-1.0, in1=SM[:, c + 7:c + 8],
                                                             op0=ALU.mult, op1=ALU.mult), wait=wxn)
        R.op("vector", lambda e, xt=xt, xnb=xnb, c=c: e.tensor_scalar(
            out=xnb[:], in0=xt, scalar1=SM[:, c + 7:c + 8], scalar2=SM[:, c + 8:c + 9],
            op0=ALU.mult, op1=ALU.add), inc=(s_xn, 1))
        for f in range(8):
            wt = [(s_xn, 2 * tb + 2)] if f == 0 else []
            if f == 0 and tb >= 2:
                wt.append((s_ev, tb - 1))
            R.op("tensor", lambda e, xnb=xnb, f=f, tb=tb: e.transpose(
                cx.pt[tb % 2][:, f * 128:(f + 1) * 128], xnb[:, f * 128:(f + 1) * 128],
                cx.t["CONSTB"][:, 0:128]), wait=wt, inc=(s_tr, 1))
        R.op("scalar", lambda e, tb=tb: e.copy(
            out=XN.rearrange("p (f c) -> p f c", f=8)[:, :, tb * 128:(tb + 1) * 128],
            in_=cx.pt[tb % 2].rearrange("p (f c) -> p f c", f=8)[:, :, :]),
            wait=[(s_tr, (tb + 1) * 8)], inc=(s_ev, 1))
    return s_ev


def _mm_proj_pairs(cx, Wdram, n_pairs, m_width, dst, bias_ap_fn, dep, tag,
                   kc_n=8, dep_w=(), dst_stride=NT, after_pair=None):
    R, t = cx.R, cx.t
    XN = t["XN"]
    s_wld = R.sem(f"{tag}_wld")
    s_mm = R.sem(f"{tag}_mm")
    s_done = R.sem(f"{tag}_ev")
    s_dma = R.sem(f"{tag}_dma") if after_pair is not None else None
    for pr in range(n_pairs):
        wb = t["W0"] if pr % 2 == 0 else t["W1"]
        wld = (list(dep_w) if pr < 2 else []) + ([] if pr < 2 else [(s_mm, 2 * pr - 2)])
        R.op("gpsimd", lambda e, pr=pr, wb=wb: e.dma_start(
            out=wb[0:128, 0:kc_n * m_width],
            in_=Wdram[pr].rearrange("p k m -> p (k m)")), wait=wld, inc=(s_wld, 16))
        for nq2 in range(2):
            it = pr * 2 + nq2
            b0, b1 = (it % 2) * 2, (it % 2) * 2 + 1
            for kc in range(kc_n):
                w = []
                if kc == 0:
                    w.append((s_wld, 16 * (pr + 1)))
                    if it == 0:
                        w.extend(dep)
                    if it >= 2:
                        w.append((s_done, 2 * it - 2))
                for half in range(2):
                    nq = nq2 * 2 + half
                    pbk = cx.pb[b0 if half == 0 else b1]
                    R.op("tensor", lambda e, kc=kc, nq=nq, wb=wb, pbk=pbk: e.matmul(
                        pbk[0:m_width, :],
                        wb[:, kc * m_width:(kc + 1) * m_width],
                        XN[:, kc * NT + nq * 512: kc * NT + (nq + 1) * 512],
                        start=(kc == 0), stop=(kc == kc_n - 1)),
                        wait=w if half == 0 else [],
                        inc=(s_mm, 1) if (kc == kc_n - 1 and half == 1) else None)
            for half in range(2):
                nq = nq2 * 2 + half
                pbk = cx.pb[b0 if half == 0 else b1]
                bias = bias_ap_fn(pr)
                wev = [(s_mm, it + 1)] if half == 0 else []
                if it == 0 and half == 0:
                    wev.extend(dep)
                if after_pair is not None and pr >= 1 and it == 2 * pr and half == 0:
                    wev.append((s_dma, 16 * pr))
                if bias is not None:
                    R.op("vector", lambda e, pr=pr, nq=nq, pbk=pbk, bias=bias: e.tensor_scalar(
                        out=dst[0:m_width, pr * dst_stride + nq * 512: pr * dst_stride + (nq + 1) * 512],
                        in0=pbk[0:m_width, :], scalar1=bias, scalar2=None, op0=ALU.add),
                        wait=wev, inc=(s_done, 1))
                else:
                    R.op("vector", lambda e, pr=pr, nq=nq, pbk=pbk: e.tensor_copy(
                        out=dst[0:m_width, pr * dst_stride + nq * 512: pr * dst_stride + (nq + 1) * 512],
                        in_=pbk[0:m_width, :]),
                        wait=wev, inc=(s_done, 1))
        if after_pair is not None:
            dram_ap, sbuf_ap = after_pair(pr)
            R.op("sync", lambda e, dram_ap=dram_ap, sbuf_ap=sbuf_ap: e.dma_start(
                out=dram_ap, in_=sbuf_ap), wait=[(s_done, 4 * (pr + 1))], inc=(s_dma, 16))
    if after_pair is not None:
        return s_done, s_mm, s_dma
    return s_done, s_mm


def emit_l0_proj(cx, dram, dep_xn, dep_c):
    t = cx.t
    dep = list(dep_xn) + list(dep_c)
    s_q, m_q = _mm_proj_pairs(cx, dram["WqP"], 8, 128, t["QD"],
                              lambda pr: t["BQK"][:, pr:pr + 1], dep, "p0q",
                              dep_w=list(dep_xn))
    return s_q, m_q


def emit_k_proj(cx, Wdram, bias_col0, k_dram, dep_xn, dep_c, dep_w, tag):
    t = cx.t
    dep = list(dep_xn) + list(dep_c)
    bias_fn = (lambda pr: None) if bias_col0 is None else (
        lambda pr: t["BQK"][:, bias_col0 + pr:bias_col0 + pr + 1])
    s_k, m_k, s_dma = _mm_proj_pairs(
        cx, Wdram, 8, 128, t["KSTR0"], bias_fn, dep, tag,
        dep_w=list(dep_w), dst_stride=0,
        after_pair=lambda pr: (k_dram[pr * 128:(pr + 1) * 128, :], t["KSTR0"][:, :]))
    return s_k, m_k, s_dma


def emit_l0_vd(cx, dram, vd_dram, dep_xn, dep_c, dep_w):
    t = cx.t
    dep = list(dep_xn) + list(dep_c)
    s_v, m_v, s_dma = _mm_proj_pairs(
        cx, dram["WvdP"], 4, 64, t["VD4"],
        lambda pr: t["BVD"][:, pr:pr + 1], dep, "p0v",
        dep_w=list(dep_w), dst_stride=0,
        after_pair=lambda pr: (vd_dram[pr * 64:(pr + 1) * 64, :], t["VD4"][0:64, :]))
    return s_v, m_v, s_dma


def emit_load_x(cx, x_dram, cast=True):
    R, t = cx.R, cx.t
    s = R.sem("ldx")
    eng = "gpsimd" if cast else "sync"
    R.op(eng, lambda e: e.dma_start(
        out=t["X"].rearrange("p (tb c) -> p tb c", tb=NTB),
        in_=x_dram.rearrange("(tb p) c -> p tb c", p=128)), inc=(s, 16))
    return s



def emit_l0_attn(cx, dram, kg_fn, vdg_fn, own_ge, dep_q, dep_kv, dep_x, dep_pb, dep_c=(), dbg=None, groups=range(8)):
    R, t = cx.R, cx.t
    QD, X = t["QD"], t["X"]
    s_ini = R.sem("a_ini")
    s_kld = R.sem("a_kld")
    s_vld = R.sem("a_vld")
    s_vt = R.sem("a_vt")
    s_vte = R.sem("a_vte")
    s_sc = R.sem("a_sc")
    s_ex = R.sem("a_ex")
    s_mk = R.sem("a_mk")
    s_av = R.sem("a_av")
    s_ep = R.sem("a_ep")
    s_ept = R.sem("a_ept")
    s_stt = R.sem("a_stt")
    s_ku = R.sem("a_ku")
    emit_l0_attn._dep_c = list(dep_c)
    es_free = {0: 0, 1: 0}
    es_i = 0
    it = 0
    started = set()
    R.op("gpsimd", lambda e: e.memset(t["ZB5"][:], 0.0), inc=(s_ini, 1))
    for vtb in (t["VT0"], t["VT1"]):
        for col in (64, 129):
            R.op("gpsimd", lambda e, vtb=vtb, col=col: e.memset(
                vtb.rearrange("p (tb c) -> p tb c", tb=8)[:, :, col:col + 1], 1.0),
                inc=(s_ini, 1))
    first_te = [True]
    for g in groups:
        wc = [(s_vt, s_vt.v)]
        R.op("gpsimd", lambda e, g=g: e.dma_start(out=t["WV2"][:], in_=dram["Wvu2"][g]),
             wait=wc, inc=(s_vld, 16))
        R.op("gpsimd", lambda e, g=g: e.dma_start(out=t["BR"][0:1, 0:128], in_=dram["bvu2"][g]),
             wait=[(s_vt, s_vt.v)], inc=(s_vld, 16))
        wv_ready = (s_vld, s_vld.v)
        for ei, ge in enumerate(own_ge):
            for ke in range(ge, 8):
                kb = t["KSTR0"] if it % 2 == 0 else t["KSTR1"]
                vtb = t["VT0"] if it % 2 == 0 else t["VT1"]
                wk = list(dep_kv) + ([(s_ku, it - 1)] if it >= 2 else [])
                R.op("sync", lambda e, g=g, ke=ke, kb=kb: e.dma_start(
                    out=kb[:, 0:1024], in_=kg_fn(g, ke)), wait=wk, inc=(s_kld, 16))
                R.op("sync", lambda e, g=g, ke=ke: e.dma_start(
                    out=t["WR0"][:, 0:1024], in_=vdg_fn(g, ke)),
                    wait=[(s_vt, s_vt.v)], inc=(s_vld, 16))
                vld_here = (s_vld, s_vld.v)
                for tb8 in range(8):
                    w = []
                    if tb8 == 0:
                        w = [wv_ready, vld_here, (s_ex, s_ex.v),
                             (s_vte, s_vte.v), (s_ini, 5)]
                        if it >= 2:
                            w.append((s_vte, s_vte.v))
                        if first_te[0]:
                            w += list(dep_q) + list(dep_pb) + list(emit_l0_attn._dep_c)
                            first_te[0] = False
                    else:
                        w = [(s_vte, s_vte.v)]
                    R.op("tensor", lambda e, tb8=tb8: e.matmul(
                        cx.pb[4][:, 0:128],
                        t["WR0"][0:32, tb8 * 128:(tb8 + 1) * 128], t["WV2"][:],
                        start=True, stop=False), wait=w)
                    R.op("tensor", lambda e: e.matmul(
                        cx.pb[4][:, 0:128], t["ONE1"][0:1, 0:128], t["BR"][0:1, 0:128],
                        start=False, stop=True), inc=(s_vt, 1))
                    wev = [(s_vt, s_vt.v)]
                    if tb8 == 0 and it >= 2:
                        wev.append((s_ku, it - 1))
                    R.op("scalar", lambda e, tb8=tb8, vtb=vtb: e.copy(
                        out=vtb[:, tb8 * 130:tb8 * 130 + 130].rearrange(
                            "p (hh c) -> p hh c", hh=2)[:, :, 0:64],
                        in_=cx.pb[4][:, 0:128].rearrange("p (hh c) -> p hh c", hh=2)),
                        wait=wev, inc=(s_vte, 1))
                if dbg is not None and "vt" in dbg and it == 8:
                    R.op("sync", lambda e, vtb=vtb: e.dma_start(out=dbg["vt"], in_=vtb[:, :]),
                         wait=[(s_vte, s_vte.v)], inc=(s_ini, 16))
                for hh in range(2):
                    tcs = [(tc, (tc + 1) * 128 if ke == ge else 1024) for tc in range(8)]
                    for tc, ns in tcs:
                        for sh in range((ns + 511) // 512):
                            n = min(512, ns - sh * 512)
                            par = es_i % 2
                            esb = t["ES0"] if par == 0 else t["ES1"]
                            scb = cx.pb[4 + par]
                            wsc = [(s_kld, s_kld.v), (s_vte, s_vte.v), (s_ex, s_ex.v)]
                            if SERIAL_ATT:
                                wsc += [(s_av, s_av.v), (s_mk, s_mk.v), (s_stt, s_stt.v)]
                            R.op("tensor", lambda e, g=g, hh=hh, tc=tc, sh=sh, n=n, ei=ei, kb=kb, scb=scb: e.matmul(
                                scb[:, 0:n],
                                kb[hh * 64:hh * 64 + 64, tc * 128:(tc + 1) * 128],
                                QD[hh * 64:hh * 64 + 64,
                                   g * NT + ei * 1024 + sh * 512: g * NT + ei * 1024 + sh * 512 + n],
                                start=True, stop=True),
                                wait=wsc,
                                inc=(s_sc, 1))
                            wex = [(s_sc, s_sc.v)]
                            if es_i == 0:
                                wex += list(emit_l0_attn._dep_c)
                            if es_free[par] > 0:
                                wex.append((s_av, es_free[par]))
                            R.op("scalar", lambda e, esb=esb, scb=scb, n=n: e.activation(
                                out=esb[:, 0:n], in_=scb[:, 0:n], func=AF.Exp,
                                scale=1.0 / 32, bias=t["CB"][:, 1:2]),
                                wait=wex, inc=(s_ex, 1))
                            masked = (ke == ge) and (tc // 4 == sh)
                            j0 = tc * 128 - sh * 512 if masked else n
                            if masked:
                                R.op("vector", lambda e, esb=esb, j0=j0: e.tensor_tensor(
                                    out=t["MKB"][:], in0=esb[:, j0:j0 + 128],
                                    in1=t["CONSTB"][:, 128:256], op=ALU.mult),
                                    wait=[(s_ex, s_ex.v)], inc=(s_mk, 1))
                            if dbg is not None and "es" in dbg and es_i == 0:
                                R.op("sync", lambda e, esb=esb, n=n: e.dma_start(
                                    out=dbg["es"][:, 0:n], in_=esb[:, 0:n]),
                                    wait=[(s_mk, s_mk.v)] if masked else [(s_ex, s_ex.v)],
                                    inc=(s_ini, 16))
                            bank = 2 * hh + sh
                            key = (g, ei, bank)
                            st_flag = key not in started
                            started.add(key)
                            stop_flag = False
                            w = [(s_ex, s_ex.v)]
                            if st_flag:
                                w.append((s_ep, s_ep.v))
                            if j0 > 0:
                                R.op("tensor", lambda e, vtb=vtb, hh=hh, tc=tc, bank=bank, esb=esb, j0=j0,
                                     st=st_flag, sp=(stop_flag and not masked): e.matmul(
                                    cx.pb[bank][0:65, 0:j0],
                                    vtb[:, tc * 130 + hh * 65: tc * 130 + hh * 65 + 65],
                                    esb[:, 0:j0], start=st, stop=sp),
                                    wait=w, inc=(s_av, 1))
                                st_flag = False
                            if masked and not _osmod.environ.get("SKIPMKB"):
                                w2 = [(s_mk, s_mk.v)]
                                if st_flag:
                                    w2.append((s_ep, s_ep.v))
                                R.op("tensor", lambda e, vtb=vtb, hh=hh, tc=tc, bank=bank, j0=j0,
                                     st=st_flag, sp=stop_flag: e.matmul(
                                    cx.pb[bank][0:65, j0:j0 + 128],
                                    vtb[:, tc * 130 + hh * 65: tc * 130 + hh * 65 + 65],
                                    t["MKB"][:], start=st, stop=sp),
                                    wait=w2, inc=(s_av, 1))
                            es_free[par] = s_av.v
                            es_i += 1
                R.op("tensor", None, inc=(s_ku, 1))
                it += 1
            for bank in range(4):
                R.op("tensor", lambda e, bank=bank: e.matmul(
                    cx.pb[bank][0:65, 0:512],
                    (t["VT0"] if it % 2 == 1 else t["VT1"])[:, 0:65],
                    t["ZB5"][:], start=False, stop=True), inc=(s_av, 1))
            for hh in range(2):
                h = 2 * g + hh
                for sh in range(2):
                    R.op("scalar", lambda e, hh=hh, sh=sh: e.copy(
                        out=t["AVSB"][0:65, 0:512], in_=cx.pb[2 * hh + sh][0:65, 0:512]),
                        wait=[(s_av, s_av.v), (s_ept, s_ept.v)], inc=(s_ep, 1))
                    if dbg is not None and "x1banks" in dbg and g == 0:
                        r0 = (ei * 2 + hh) * 128
                        R.op("sync", lambda e, sh=sh, r0=r0: e.dma_start(
                            out=dbg["x1banks"][r0:r0 + 128, sh * 512:(sh + 1) * 512],
                            in_=t["AVSB"][0:128, 0:512]),
                            wait=[(s_ep, s_ep.v)], inc=(s_ini, 16))
                    for s4 in range(4):
                        slot = sh * 4 + s4
                        w = [(s_ep, s_ep.v)] if s4 == 0 else []
                        if s4 == 0 and sh == 0:
                            w.append((s_stt, s_stt.v))
                        R.op("tensor", lambda e, s4=s4, slot=slot: e.transpose(
                            cx.pt[0][:, slot * 128: slot * 128 + 65],
                            t["AVSB"][0:65, s4 * 128:(s4 + 1) * 128],
                            t["CONSTB"][0:65, 0:65]), wait=w, inc=(s_ept, 1))
                for slot in range(8):
                    wep = [(s_ept, s_ept.v), (s_stt, s_stt.v)] if slot == 0 else []
                    R.op("vector", lambda e, slot=slot: e.tensor_copy(
                        out=t["EPI"][:, slot * 66: slot * 66 + 65],
                        in_=cx.pt[0][:, slot * 128: slot * 128 + 65]),
                        wait=wep, inc=(s_ep, 1) if slot == 7 else None)
                for slot in range(8):
                    tb = ei * 8 + slot
                    w = [(s_ep, s_ep.v)] if slot == 0 else []
                    if g == 0 and ei == 0 and hh == 0 and slot == 0:
                        w += list(dep_x)
                    if dbg is not None:
                        R.op("scalar", lambda e, slot=slot, tb=tb, h=h: e.copy(
                            out=t["ZS"][:, h * 16 + tb: h * 16 + tb + 1],
                            in_=cx.pt[0][:, slot * 128 + 64: slot * 128 + 65]),
                            wait=[(s_ept, s_ept.v)])
                    R.op("vector", lambda e, slot=slot: e.reciprocal(
                        t["STATS"][:, 160:161], t["EPI"][:, slot * 66 + 64: slot * 66 + 65]),
                        wait=w)
                    if _osmod.environ.get("DBGZ"):
                        R.op("vector", lambda e, slot=slot, tb=tb, h=h: e.tensor_copy(
                            out=X[:, tb * 1024 + h * 64: tb * 1024 + h * 64 + 65],
                            in_=t["EPI"][:, slot * 66: slot * 66 + 65]), inc=(s_stt, 1))
                    elif _osmod.environ.get("DBGATT"):
                        R.op("vector", lambda e, slot=slot, tb=tb, h=h: e.tensor_scalar(
                            out=X[:, tb * 1024 + h * 64: tb * 1024 + h * 64 + 64],
                            in0=t["EPI"][:, slot * 66: slot * 66 + 64],
                            scalar1=t["STATS"][:, 160:161], scalar2=None,
                            op0=ALU.mult), inc=(s_stt, 1))
                    else:
                        R.op("vector", lambda e, slot=slot, tb=tb, h=h: e.scalar_tensor_tensor(
                            out=X[:, tb * 1024 + h * 64: tb * 1024 + h * 64 + 64],
                            in0=t["EPI"][:, slot * 66: slot * 66 + 64],
                            scalar=t["STATS"][:, 160:161],
                            in1=X[:, tb * 1024 + h * 64: tb * 1024 + h * 64 + 64],
                            op0=ALU.mult, op1=ALU.add), inc=(s_stt, 1))
            started = {k for k in started if k[0] != g or k[1] != ei}
    return s_stt


def emit_moe(cx, dram, dep):
    R, t = cx.R, cx.t
    X, XN = t["X"], t["XN"]
    s = R.sem("moe")
    R.op("gpsimd", lambda e: e.dma_start(
        out=t["WG"][:, 0:64].rearrange("p (kc c) -> p kc c", kc=8),
        in_=dram["WgP"].rearrange("(kc p) c -> p kc c", p=128)), wait=dep, inc=(s, 16))
    R.op("gpsimd", lambda e: e.dma_start(out=t["BG1"][:], in_=dram["bgP"]), inc=(s, 16))
    SG = t["SG"]
    WC = t["KVA"]
    for tb in range(NTB):
        for kc in range(8):
            R.op("tensor", lambda e, tb=tb, kc=kc: e.matmul(
                cx.pb[0][:, 0:8],
                XN[:, kc * NT + tb * 128: kc * NT + tb * 128 + 128],
                t["WG"][:, kc * 8:(kc + 1) * 8], start=(kc == 0), stop=False))
        R.op("tensor", lambda e: e.matmul(
            cx.pb[0][:, 0:8], t["ONE1"][0:1, 0:128], t["BG1"][:],
            start=False, stop=True))
        R.op("vector", lambda e: e.tensor_copy(out=SG[:, 0:8], in_=cx.pb[0][:, 0:8]))
        R.op("vector", lambda e: e.reduce_max(SG[:, 16:17], SG[:, 0:8], axis=AX.X, negate=True))
        R.op("scalar", lambda e: e.activation(out=SG[:, 8:16], in_=SG[:, 0:8],
                                              func=AF.Exp, bias=SG[:, 16:17]))
        R.op("vector", lambda e: e.tensor_scalar(out=SG[:, 0:8], in0=SG[:, 8:16],
                                                 scalar1=0.99999, scalar2=None, op0=ALU.is_ge))
        R.op("vector", lambda e: e.tensor_tensor(out=SG[:, 16:24], in0=SG[:, 8:16],
                                                 in1=SG[:, 0:8], op=ALU.subtract))
        R.op("vector", lambda e: e.reduce_max(SG[:, 17:18] if False else t["NEGD"][:, 0:1],
                                              SG[:, 16:24], axis=AX.X))
        R.op("vector", lambda e: e.tensor_scalar(out=SG[:, 16:24], in0=SG[:, 8:16],
                                                 scalar1=t["NEGD"][:, 0:1], scalar2=None,
                                                 op0=ALU.is_ge))
        R.op("vector", lambda e: e.tensor_tensor(out=SG[:, 8:16], in0=SG[:, 8:16],
                                                 in1=SG[:, 16:24], op=ALU.mult))
        R.op("vector", lambda e: e.reduce_sum(t["NEGD"][:, 1:2], SG[:, 8:16], axis=AX.X))
        R.op("vector", lambda e: e.reciprocal(t["NEGD"][:, 2:3], t["NEGD"][:, 1:2]))
        R.op("vector", lambda e, tb=tb: e.tensor_scalar(out=WC[:, tb * 8:(tb + 1) * 8],
                                                        in0=SG[:, 8:16],
                                                        scalar1=t["NEGD"][:, 2:3], scalar2=None,
                                                        op0=ALU.mult))
    for ex in range(8):
        R.op("gpsimd", lambda e, ex=ex: e.dma_start(
            out=t["WEB"].rearrange("p (kc c) -> p kc c", kc=8),
            in_=dram["WeP"][ex].rearrange("(kc p) c -> p kc c", p=128)), inc=(s, 16))
        R.op("gpsimd", lambda e, ex=ex: e.dma_start(
            out=t["BR"][0:1, 0:1024], in_=dram["beP"][ex]), inc=(s, 16))
        for tb in range(NTB):
            for half in range(2):
                pbk = cx.pb[half]
                for kc in range(8):
                    R.op("tensor", lambda e, tb=tb, kc=kc, half=half, pbk=pbk: e.matmul(
                        pbk[:, 0:512],
                        XN[:, kc * NT + tb * 128: kc * NT + tb * 128 + 128],
                        t["WEB"][:, kc * 1024 + half * 512: kc * 1024 + half * 512 + 512],
                        start=(kc == 0), stop=False))
                R.op("tensor", lambda e, half=half, pbk=pbk: e.matmul(
                    pbk[:, 0:512], t["ONE1"][0:1, 0:128],
                    t["BR"][0:1, half * 512:(half + 1) * 512], start=False, stop=True))
                R.op("vector", lambda e, tb=tb, half=half, ex=ex, pbk=pbk: e.scalar_tensor_tensor(
                    out=X[:, tb * 1024 + half * 512: tb * 1024 + half * 512 + 512],
                    in0=pbk[:, 0:512], scalar=WC[:, tb * 8 + ex: tb * 8 + ex + 1],
                    in1=X[:, tb * 1024 + half * 512: tb * 1024 + half * 512 + 512],
                    op0=ALU.mult, op1=ALU.add), inc=(s, 1))
    return s


def emit_perf_a(cx, dram, outs, dep):
    R, t = cx.R, cx.t
    X, XN = t["X"], t["XN"]
    s = R.sem("pa")
    s_q, m_q = _mm_proj_pairs(cx, dram["W1qP"], 8, 128, t["QD"], lambda pr: None,
                              list(dep), "paq", dep_w=list(dep))
    R.op("sync", lambda e: e.dma_start(
        out=outs["QD1"].rearrange("(pr p) c -> p pr c", p=128),
        in_=t["QD"].rearrange("p (pr c) -> p pr c", pr=8)), inc=(s, 16))
    s_k, m_k, s_kd = emit_k_proj(cx, dram["W1kP"], None, outs["KD1"], list(dep), [], [], "pak")
    R.op("gpsimd", lambda e: e.memset(
        t["VAP"].rearrange("p (tb h c) -> p tb h c", tb=16, h=16)[:, :, :, 64:65], 1.0))
    for kc in range(8):
        R.op("gpsimd", lambda e, kc=kc: e.dma_start(
            out=t["W0"][:, 0:1024], in_=dram["W1v"][kc * 128:(kc + 1) * 128, :]), inc=(s, 16))
        for tb in range(NTB):
            for half in range(2):
                R.op("tensor", lambda e, kc=kc, tb=tb, half=half: e.matmul(
                    cx.pb[2 + half][:, 0:512],
                    XN[:, kc * NT + tb * 128: kc * NT + tb * 128 + 128],
                    t["W0"][:, half * 512:(half + 1) * 512],
                    start=(kc == 0), stop=(kc == 7)))
                if kc == 7:
                    R.op("scalar", lambda e, tb=tb, half=half: e.copy(
                        out=t["VAP"][:, tb * 1040 + half * 520: tb * 1040 + (half + 1) * 520]
                        .rearrange("p (h c) -> p h c", h=8)[:, :, 0:64],
                        in_=cx.pb[2 + half][:, 0:512].rearrange("p (h c) -> p h c", h=8)))
    R.op("sync", lambda e: e.dma_start(
        out=outs["VA1"].rearrange("(tb p) c -> p tb c", p=128),
        in_=t["VAP"].rearrange("p (tb c) -> p tb c", tb=16)), inc=(s, 16))
    for h in range(16):
        pr, hh = h // 2, h % 2
        R.op("gpsimd", lambda e, h=h: e.dma_start(out=t["PM0"][:], in_=dram["pmT"][h]), inc=(s, 16))
        R.op("scalar", lambda e: e.mul(out=t["PM1"][:], in_=t["PM0"][:], mul=0.125))
        for tb in range(NTB):
            qsl0 = t["QD"][hh * 64:hh * 64 + 64, pr * NT + tb * 128: pr * NT + tb * 128 + 128]
            R.op("vector", lambda e, qsl0=qsl0: e.tensor_copy(out=t["KSTR1"][0:64, 0:128], in_=qsl0))
            qsl = t["KSTR1"][0:64, 0:128]
            ksl_dram = outs["KD1"][pr * 128 + hh * 64: pr * 128 + hh * 64 + 64,
                                   tb * 128:(tb + 1) * 128]
            R.op("sync", lambda e, ksl_dram=ksl_dram: e.dma_start(
                out=t["KSTR0"][0:64, 0:128], in_=ksl_dram), inc=(s, 16))
            R.op("tensor", lambda e: e.matmul(cx.pb[0][:, 0:128], t["KSTR0"][0:64, 0:128],
                                              t["PM1"][:], start=True, stop=True))
            R.op("vector", lambda e: e.reduce_max(t["NEGD"][:, 3:4], cx.pb[0][:, 0:128], axis=AX.X))
            if tb == 0:
                R.op("vector", lambda e, h=h: e.tensor_copy(out=t["MX"][:, h % 16:h % 16 + 1],
                                                            in_=t["NEGD"][:, 3:4]))
            else:
                R.op("vector", lambda e, h=h: e.tensor_tensor(
                    out=t["MX"][:, h % 16:h % 16 + 1], in0=t["MX"][:, h % 16:h % 16 + 1],
                    in1=t["NEGD"][:, 3:4], op=ALU.max))
            R.op("tensor", lambda e: e.transpose(cx.pt[0][:, 0:64], t["KSTR0"][0:64, 0:128],
                                                 t["CONSTB"][0:64, 0:64]))
            R.op("vector", lambda e: e.tensor_copy(out=t["KT64"][:], in_=cx.pt[0][:, 0:64]))
            R.op("scalar", lambda e: e.activation(out=t["K264"][:], in_=t["KT64"][:], func=AF.Square))
            R.op("vector", lambda e: e.reduce_sum(t["NEGD"][:, 4:5], t["K264"][:], axis=AX.X))
            R.op("vector", lambda e, h=h, tb=tb: e.tensor_scalar(
                out=t["NEGD"][:, 16 + h:17 + h], in0=t["NEGD"][:, 4:5],
                scalar1=0.5, scalar2=None, op0=ALU.mult))
            R.op("tensor", lambda e, qsl=qsl: e.matmul(cx.pb[1][:, 0:128], qsl, t["PM1"][:],
                                                       start=True, stop=True))
            R.op("vector", lambda e: e.reduce_max(t["NEGD"][:, 5:6], cx.pb[1][:, 0:128],
                                                  axis=AX.X, negate=True))
            R.op("tensor", lambda e, qsl=qsl: e.transpose(cx.pt[0][:, 0:64], qsl,
                                                          t["CONSTB"][0:64, 0:64]))
            R.op("vector", lambda e: e.tensor_copy(out=t["KT64"][:], in_=cx.pt[0][:, 0:64]))
            R.op("scalar", lambda e: e.activation(out=t["K264"][:], in_=t["KT64"][:], func=AF.Square))
            R.op("vector", lambda e: e.reduce_sum(t["NEGD"][:, 6:7], t["K264"][:], axis=AX.X))
            R.op("vector", lambda e: e.scalar_tensor_tensor(
                out=t["NEGD"][:, 7:8], in0=t["NEGD"][:, 6:7], scalar=-0.5,
                in1=t["NEGD"][:, 5:6], op0=ALU.mult, op1=ALU.add))
            R.op("vector", lambda e, h=h: e.tensor_scalar(
                out=t["SG"][:, h:h + 1], in0=t["NEGD"][:, 7:8],
                scalar1=1.0, scalar2=-LN16, op0=ALU.mult, op1=ALU.add))
            if h == 15:
                R.op("sync", lambda e, tb=tb: e.dma_start(
                    out=outs["NEGDQ"][tb * 128:(tb + 1) * 128, :], in_=t["SG"][:, 0:16]),
                    inc=(s, 16))
                R.op("sync", lambda e, tb=tb: e.dma_start(
                    out=outs["DIAGK"][tb * 128:(tb + 1) * 128, :], in_=t["NEGD"][:, 16:32]),
                    inc=(s, 16))
    R.op("vector", lambda e: e.tensor_copy(out=t["MKB"][:, 0:16], in_=t["MX"][:, 0:16]))
    R.op("tensor", lambda e: e.transpose(cx.pt[0][0:16, 0:128], t["MKB"][:, 0:16],
                                         t["CONSTB"][:, 0:128]))
    R.op("vector", lambda e: e.tensor_copy(out=t["SR16"][0:16, 0:128], in_=cx.pt[0][0:16, 0:128]))
    R.op("vector", lambda e: e.reduce_max(t["ST16"][0:16, 0:1], t["SR16"][0:16, 0:128], axis=AX.X))
    R.op("vector", lambda e: e.tensor_copy(out=t["MKB"][0:16, 16:17], in_=t["ST16"][0:16, 0:1]))
    R.op("tensor", lambda e: e.transpose(cx.pt[0][0:1, 0:16], t["MKB"][0:16, 16:17],
                                         t["CONSTB"][0:16, 0:16]))
    R.op("vector", lambda e: e.tensor_copy(out=t["NEGD"][0:1, 40:41] if False else t["STATS"][0:1, 0:16], in_=cx.pt[0][0:1, 0:16]))
    R.op("sync", lambda e: e.dma_start(out=outs["STABL"], in_=t["STATS"][0:1, 0:16]), inc=(s, 16))
    return s


def emit_perf_b(cx, dram, ins_d, outs, dep):
    R, t = cx.R, cx.t
    s = R.sem("pb_")
    R.op("sync", lambda e: e.dma_start(out=t["STATS"][0:1, 0:64], in_=ins_d["STABG"]),
         wait=dep, inc=(s, 16))
    R.op("vector", lambda e: e.tensor_tensor(out=t["STATS"][0:1, 64:80],
                                             in0=t["STATS"][0:1, 0:16],
                                             in1=t["STATS"][0:1, 16:32], op=ALU.max))
    R.op("vector", lambda e: e.tensor_tensor(out=t["STATS"][0:1, 80:96],
                                             in0=t["STATS"][0:1, 32:48],
                                             in1=t["STATS"][0:1, 48:64], op=ALU.max))
    R.op("vector", lambda e: e.tensor_tensor(out=t["STATS"][0:1, 96:112],
                                             in0=t["STATS"][0:1, 64:80],
                                             in1=t["STATS"][0:1, 80:96], op=ALU.max))
    R.op("vector", lambda e: e.tensor_copy(out=t["BR"][0:1, 0:16], in_=t["STATS"][0:1, 96:112]))
    for h in range(16):
        R.op("tensor", lambda e, h=h: e.matmul(cx.pb[2][:, h:h + 1], t["ONE1"][0:1, 0:128],
                                               t["BR"][0:1, h:h + 1], start=True, stop=True))
    R.op("vector", lambda e: e.tensor_copy(out=t["NEGD"][:, 0:16], in_=cx.pb[2][:, 0:16]))
    R.op("sync", lambda e: e.dma_start(
        out=t["VAP"].rearrange("p (tb c) -> p tb c", tb=16),
        in_=ins_d["VA1o"].rearrange("(tb p) c -> p tb c", p=128)), inc=(s, 16))
    R.op("sync", lambda e: e.dma_start(
        out=t["KVB"][:, 0:256].rearrange("p (tb c) -> p tb c", tb=16),
        in_=ins_d["DIAGKo"].rearrange("(tb p) c -> p tb c", p=128)), inc=(s, 16))
    for h in range(16):
        pr, hh = h // 2, h % 2
        R.op("gpsimd", lambda e, h=h: e.dma_start(out=t["PM0"][:], in_=dram["pmT"][h]), inc=(s, 16))
        R.op("scalar", lambda e: e.mul(out=t["PM1"][:], in_=t["PM0"][:], mul=0.125))
        R.op("sync", lambda e, pr=pr, hh=hh: e.dma_start(
            out=t["KSTR0"][0:64, 0:2048],
            in_=ins_d["KD1o"][pr * 128 + hh * 64: pr * 128 + hh * 64 + 64, :]), inc=(s, 16))
        for tb in range(NTB):
            R.op("vector", lambda e, h=h, tb=tb: e.tensor_tensor(
                out=t["NEGD"][:, 33:34], in0=t["KVB"][:, tb * 16 + h: tb * 16 + h + 1],
                in1=t["NEGD"][:, h:h + 1], op=ALU.add))
            R.op("vector", lambda e: e.tensor_scalar(out=t["NEGD"][:, 34:35], in0=t["NEGD"][:, 33:34],
                                                     scalar1=-1.0, scalar2=-LN16,
                                                     op0=ALU.mult, op1=ALU.add))
            R.op("tensor", lambda e, tb=tb: e.matmul(
                cx.pb[0][:, 0:128], t["KSTR0"][0:64, tb * 128:(tb + 1) * 128],
                t["PM1"][:], start=True, stop=True))
            R.op("scalar", lambda e: e.activation(out=t["KF"][:, 0:128], in_=cx.pb[0][:, 0:128],
                                                  func=AF.Exp, bias=t["NEGD"][:, 34:35], scale=1.0))
            R.op("scalar", lambda e: e.activation(out=t["KF"][:, 128:256], in_=cx.pb[0][:, 0:128],
                                                  func=AF.Exp, bias=t["NEGD"][:, 34:35], scale=-1.0))
            for sgn in range(2):
                R.op("tensor", lambda e, tb=tb, sgn=sgn, h=h: e.matmul(
                    cx.pb[2 + sgn][0:128, 0:65],
                    t["KF"][:, sgn * 128:(sgn + 1) * 128],
                    t["VAP"][:, tb * 1040 + h * 65: tb * 1040 + h * 65 + 65],
                    start=(tb == 0), stop=(tb == NTB - 1)))
        for sgn in range(2):
            R.op("vector", lambda e, sgn=sgn: e.tensor_copy(out=t["KVS"][:, sgn * 65:(sgn + 1) * 65],
                                                            in_=cx.pb[2 + sgn][0:128, 0:65]))
            R.op("sync", lambda e, h=h, sgn=sgn: e.dma_start(
                out=outs["KVL"][h * 256 + sgn * 128: h * 256 + sgn * 128 + 128, :],
                in_=t["KVS"][:, sgn * 65:(sgn + 1) * 65]), inc=(s, 16))
    return s


def emit_perf_c(cx, dram, ins_d, dep):
    R, t = cx.R, cx.t
    X = t["X"]
    s = R.sem("pc_")
    R.op("sync", lambda e: e.dma_start(
        out=t["KVA"].rearrange("p (g c) -> p g c", g=32),
        in_=ins_d["KV4"][0].rearrange("(g p) c -> p g c", p=128)), wait=dep, inc=(s, 16))
    for r in range(1, 4):
        R.op("sync", lambda e, r=r: e.dma_start(
            out=t["KVB"].rearrange("p (g c) -> p g c", g=32),
            in_=ins_d["KV4"][r].rearrange("(g p) c -> p g c", p=128)), inc=(s, 16))
        R.op("vector", lambda e: e.tensor_tensor(out=t["KVA"][:], in0=t["KVA"][:],
                                                 in1=t["KVB"][:], op=ALU.add))
    R.op("vector", lambda e: e.tensor_copy(out=t["KSTR1"][:, 0:2015], in_=t["KVA"][:, 0:2015]))
    R.op("vector", lambda e: e.tensor_copy(out=t["MKB"][:, 0:65], in_=t["KVA"][:, 2015:2080]))
    R.op("sync", lambda e: e.dma_start(
        out=t["QD"].rearrange("p (pr c) -> p pr c", pr=8),
        in_=ins_d["QD1o"].rearrange("(pr p) c -> p pr c", p=128)), inc=(s, 16))
    R.op("sync", lambda e: e.dma_start(
        out=t["KVB"][:, 0:256].rearrange("p (tb c) -> p tb c", tb=16),
        in_=ins_d["NEGDQo"].rearrange("(tb p) c -> p tb c", p=128)), inc=(s, 16))
    for h in range(16):
        pr, hh = h // 2, h % 2
        R.op("gpsimd", lambda e, h=h: e.dma_start(out=t["PM0"][:], in_=dram["pmT"][h]), inc=(s, 16))
        R.op("scalar", lambda e: e.mul(out=t["PM1"][:], in_=t["PM0"][:], mul=0.125))
        for tb in range(NTB):
            R.op("vector", lambda e, tb=tb, h=h: e.tensor_copy(
                out=t["NEGD"][:, 0:1], in_=t["KVB"][:, tb * 16 + h: tb * 16 + h + 1]))
            qsl0 = t["QD"][hh * 64:hh * 64 + 64, pr * NT + tb * 128: pr * NT + tb * 128 + 128]
            R.op("vector", lambda e, qsl0=qsl0: e.tensor_copy(out=t["QFT"][0:64, 0:128], in_=qsl0))
            qsl = t["QFT"][0:64, 0:128]
            R.op("tensor", lambda e, qsl=qsl: e.matmul(cx.pb[0][:, 0:128], qsl, t["PM1"][:],
                                                       start=True, stop=True))
            R.op("scalar", lambda e: e.activation(out=t["KF"][:, 0:128], in_=cx.pb[0][:, 0:128],
                                                  func=AF.Exp, bias=t["NEGD"][:, 0:1], scale=1.0))
            R.op("scalar", lambda e: e.activation(out=t["KF"][:, 128:256], in_=cx.pb[0][:, 0:128],
                                                  func=AF.Exp, bias=t["NEGD"][:, 0:1], scale=-1.0))
            for sgn in range(2):
                R.op("tensor", lambda e, sgn=sgn: e.transpose(
                    cx.pt[0][:, sgn * 128:(sgn + 1) * 128],
                    t["KF"][:, sgn * 128:(sgn + 1) * 128], t["CONSTB"][:, 0:128]))
            R.op("vector", lambda e: e.tensor_copy(out=t["QFT"][:], in_=cx.pt[0][:, 0:256]))
            for sgn in range(2):
                g_ = h * 2 + sgn
                if g_ < 31:
                    rhs = t["KSTR1"][:, g_ * 65: g_ * 65 + 65]
                else:
                    rhs = t["MKB"][:, 0:65]
                R.op("tensor", lambda e, sgn=sgn, rhs=rhs: e.matmul(
                    cx.pb[1][:, 0:65], t["QFT"][:, sgn * 128:(sgn + 1) * 128], rhs,
                    start=(sgn == 0), stop=(sgn == 1)))
            R.op("vector", lambda e: e.tensor_copy(out=t["NEGD"][:, 1:2], in_=cx.pb[1][:, 64:65]))
            R.op("vector", lambda e: e.tensor_scalar(out=t["NEGD"][:, 2:3], in0=t["NEGD"][:, 1:2],
                                                     scalar1=1.0, scalar2=EPS_PERF,
                                                     op0=ALU.mult, op1=ALU.add))
            R.op("vector", lambda e: e.reciprocal(t["NEGD"][:, 3:4], t["NEGD"][:, 2:3]))
            R.op("vector", lambda e: e.tensor_scalar(out=t["ATN"][:], in0=cx.pb[1][:, 0:64],
                                                     scalar1=t["NEGD"][:, 3:4], scalar2=None,
                                                     op0=ALU.mult))
            R.op("tensor", lambda e: e.transpose(cx.pt[0][:, 0:128] if False else cx.pt[0][0:64, 256:384],
                                                 t["ATN"][:], t["CONSTB"][:, 0:128]))
            R.op("vector", lambda e, h=h, tb=tb, hh=hh, pr=pr: e.tensor_copy(
                out=att_alias(cx)[hh * 64:hh * 64 + 64, pr * 2048 + tb * 128: pr * 2048 + tb * 128 + 128],
                in_=cx.pt[0][0:64, 256:384]))
    R.op("gpsimd", lambda e: e.dma_start(
        out=t["WEB"].rearrange("p (c n) -> p c n", c=8),
        in_=dram["W1o"].rearrange("(c p) n -> p c n", p=128)), inc=(s, 16))
    ATT = att_alias(cx)
    for tb in range(NTB):
        for half in range(2):
            for c in range(8):
                R.op("tensor", lambda e, tb=tb, half=half, c=c: e.matmul(
                    cx.pb[2][:, 0:512],
                    ATT[:, c * 2048 + tb * 128: c * 2048 + tb * 128 + 128],
                    t["WEB"][:, c * 1024 + half * 512: c * 1024 + half * 512 + 512],
                    start=(c == 0), stop=(c == 7)))
            R.op("vector", lambda e, tb=tb, half=half: e.tensor_tensor(
                out=X[:, tb * 1024 + half * 512: tb * 1024 + half * 512 + 512],
                in0=X[:, tb * 1024 + half * 512: tb * 1024 + half * 512 + 512],
                in1=cx.pb[2][:, 0:512], op=ALU.add), inc=(s, 1))
    return s


def emit_ffn(cx, dram, out_dram, dep):
    R, t = cx.R, cx.t
    X, XN = t["X"], t["XN"]
    s = R.sem("ff")
    for oc in range(8):
        R.op("gpsimd", lambda e, oc=oc: e.dma_start(
            out=t["W0"][:, 0:1024], in_=dram["Wf"][oc * 128:(oc + 1) * 128, :]),
            wait=dep, inc=(s, 16))
        R.op("scalar", lambda e: e.sign(out=t["W1"][:, 0:1024], in_=t["W0"][:, 0:1024]))
        for ic in range(8):
            R.op("tensor", lambda e, ic=ic: e.transpose(
                cx.pt[0][:, 0:128], t["W1"][:, ic * 128:(ic + 1) * 128], t["CONSTB"][:, 0:128]))
            R.op("vector", lambda e, ic=ic, oc=oc: e.tensor_copy(
                out=t["WEB"][:, ic * 1024 + oc * 128: ic * 1024 + oc * 128 + 128],
                in_=cx.pt[0][:, 0:128]))
    for tb in range(NTB):
        for half in range(2):
            for ic in range(8):
                R.op("tensor", lambda e, tb=tb, half=half, ic=ic: e.matmul(
                    cx.pb[0][:, 0:512],
                    XN[:, ic * NT + tb * 128: ic * NT + tb * 128 + 128],
                    t["WEB"][:, ic * 1024 + half * 512: ic * 1024 + half * 512 + 512],
                    start=(ic == 0), stop=(ic == 7)))
            osb = t["OS0"] if tb % 2 == 0 else t["OS1"]
            R.op("vector", lambda e, tb=tb, half=half, osb=osb: e.tensor_tensor(
                out=osb[:, half * 512:(half + 1) * 512],
                in0=X[:, tb * 1024 + half * 512: tb * 1024 + half * 512 + 512],
                in1=cx.pb[0][:, 0:512], op=ALU.add))
        R.op("sync", lambda e, tb=tb, osb=osb: e.dma_start(
            out=out_dram[tb * 128:(tb + 1) * 128, :], in_=osb[:]), inc=(s, 16))
    return s


def np_ln(x):
    m = x.mean(-1, keepdims=True)
    v = x.var(-1, keepdims=True)
    return (x - m) / np.sqrt(v + EPS_LN)


NPDT = {F32: np.float32, BF16: BF}


def declare_inputs(nc, arrs, names):
    d = {}
    for n in names:
        a = arrs[n]
        dt = BF16 if a.dtype == BF else F32
        d[n] = nc.dram_tensor(n, list(a.shape), dt, kind="ExternalInput").ap()
    return d


def emit_store(cx, dram_out, sbuf_ap, wait, s_st):
    cx.R.op("sync", lambda e: e.dma_start(out=dram_out, in_=sbuf_ap), wait=wait, inc=(s_st, 16))


SHP = {"WqP": [8, 128, 8, 128], "WkP": [8, 128, 8, 128], "WvdP": [4, 128, 8, 64],
       "bqP": [128, 8], "bkP": [128, 8], "bvdP": [64, 4],
       "identb": [128, 128], "tri": [128, 128],
       "Wvu2": [8, 32, 128], "bvu2": [8, 1, 128],
       "WeP": [8, 1024, 1024], "beP": [8, 1, 1024], "WgP": [1024, 8], "bgP": [1, 8],
       "W1qP": [8, 128, 8, 128], "W1kP": [8, 128, 8, 128], "W1v": [1024, 1024],
       "pmT": [16, 64, 128], "W1o": [1024, 1024], "Wf": [1024, 1024]}
BF16_CONSTS = {"identb", "tri"}


def _decl(nc, names):
    return {n: nc.dram_tensor(n, SHP[n], BF16 if n in BF16_CONSTS else F32,
                              kind="ExternalInput").ap() for n in names}


def _fin(R, sems):
    R.op("sync", None, wait=[(s, s.v) for s in sems])


def build_L1():
    from contextlib import ExitStack
    nc = bass.Bass("TRN2", target_bir_lowering=False, debug=False)
    with ExitStack() as stack:
        xd = nc.dram_tensor("x", [NT, 1024], F32, kind="ExternalInput").ap()
        dram = _decl(nc, ["WkP", "WvdP", "bqP", "bkP", "bvdP", "identb", "tri"])
        oKD = nc.dram_tensor("oKD", [1024, NT], BF16, kind="ExternalOutput").ap()
        oVD = nc.dram_tensor("oVD", [256, NT], BF16, kind="ExternalOutput").ap()
        cx = alloc_common(nc, stack)
        R = cx.R
        ldc = emit_consts(cx, dram)
        ldx = emit_load_x(cx, xd)
        s_ev = emit_ln(cx, [(ldx, 16), (ldc, ldc.v)])
        s_k, m_k, s_kdma = emit_k_proj(cx, dram["WkP"], 8, oKD, [(s_ev, NTB)],
                                       [(ldc, ldc.v)], [(s_ev, NTB)], "p0k")
        s_v, m_v, s_vdma = emit_l0_vd(cx, dram, oVD, [(s_ev, NTB)], [(ldc, ldc.v)],
                                      [(m_k, 16)])
        _fin(R, [s_kdma, s_vdma])
        R.flush(nc, stack)
    return nc


def build_L2(m):
    from contextlib import ExitStack
    nc = bass.Bass("TRN2", target_bir_lowering=False, debug=False)
    with ExitStack() as stack:
        xd = nc.dram_tensor("x", [NT, 1024], F32, kind="ExternalInput").ap()
        dram = _decl(nc, ["WqP", "bqP", "bkP", "bvdP", "identb", "tri", "Wvu2", "bvu2"])
        KGd = nc.dram_tensor("KG", [1024, 8192], BF16, kind="ExternalInput").ap()
        VDGd = nc.dram_tensor("VDG", [256, 8192], BF16, kind="ExternalInput").ap()
        oX1 = nc.dram_tensor("oX1", [NT, 1024], BF16, kind="ExternalOutput").ap()
        cx = alloc_common(nc, stack)
        R = cx.R
        ldc = emit_consts(cx, dram)
        ldx = emit_load_x(cx, xd)
        s_ev = emit_ln(cx, [(ldx, 16), (ldc, ldc.v)])
        s_q, m_q = emit_l0_proj(cx, dram, [(s_ev, NTB)], [(ldc, ldc.v)])
        kg_fn = lambda g, ke: KGd[g * 128:(g + 1) * 128, ke * 1024:(ke + 1) * 1024]
        vdg_fn = lambda g, ke: VDGd[(g // 2) * 64 + (g % 2) * 32:(g // 2) * 64 + (g % 2) * 32 + 32,
                                    ke * 1024:(ke + 1) * 1024]
        s_stt = emit_l0_attn(cx, dram, kg_fn, vdg_fn, [m, 7 - m],
                             dep_q=[(s_q, s_q.v)], dep_kv=[], dep_x=[(s_ev, NTB)],
                             dep_pb=[(s_q, s_q.v)], dep_c=[(ldc, ldc.v)])
        st = R.sem("st2")
        R.op("sync", lambda e: e.dma_start(
            out=oX1.rearrange("(tb p) c -> p tb c", p=128),
            in_=cx.t["X"].rearrange("p (tb c) -> p tb c", tb=NTB)),
            wait=[(s_stt, s_stt.v)], inc=(st, 16))
        _fin(R, [st])
        R.flush(nc, stack)
    return nc


def build_L3():
    from contextlib import ExitStack
    nc = bass.Bass("TRN2", target_bir_lowering=False, debug=False)
    with ExitStack() as stack:
        xd = nc.dram_tensor("x", [NT, 1024], BF16, kind="ExternalInput").ap()
        dram = _decl(nc, ["WeP", "beP", "WgP", "bgP", "W1qP", "W1kP", "W1v", "pmT",
                          "identb", "tri", "bqP", "bkP", "bvdP"])
        oX2 = nc.dram_tensor("oX2", [NT, 1024], BF16, kind="ExternalOutput").ap()
        outs = {
            "QD1": nc.dram_tensor("oQD1", [1024, NT], BF16, kind="ExternalOutput").ap(),
            "KD1": nc.dram_tensor("oKD1", [1024, NT], BF16, kind="ExternalOutput").ap(),
            "VA1": nc.dram_tensor("oVA1", [NT, 1040], BF16, kind="ExternalOutput").ap(),
            "NEGDQ": nc.dram_tensor("oNEGDQ", [NT, 16], F32, kind="ExternalOutput").ap(),
            "DIAGK": nc.dram_tensor("oDIAGK", [NT, 16], F32, kind="ExternalOutput").ap(),
            "STABL": nc.dram_tensor("oSTABL", [1, 16], F32, kind="ExternalOutput").ap(),
        }
        cx = alloc_common(nc, stack)
        R = cx.R
        ldc = emit_consts(cx, dram)
        ldx = emit_load_x(cx, xd, cast=False)
        s_ev = emit_ln(cx, [(ldx, 16), (ldc, ldc.v)])
        s_m = emit_moe(cx, dram, [(s_ev, NTB)])
        st = R.sem("st3")
        R.op("sync", lambda e: e.dma_start(
            out=oX2.rearrange("(tb p) c -> p tb c", p=128),
            in_=cx.t["X"].rearrange("p (tb c) -> p tb c", tb=NTB)),
            wait=[(s_m, s_m.v)], inc=(st, 16))
        s_ev2 = emit_ln(cx, [(s_m, s_m.v)], tag="ln2")
        s_a = emit_perf_a(cx, dram, outs, [(s_ev2, NTB)])
        _fin(R, [st, s_a])
        R.flush(nc, stack)
    return nc


def build_L4a_unused():
    from contextlib import ExitStack
    nc = bass.Bass("TRN2", target_bir_lowering=False, debug=False)
    with ExitStack() as stack:
        xd = nc.dram_tensor("x", [NT, 1024], BF16, kind="ExternalInput").ap()
        dram = _decl(nc, ["W1qP", "W1kP", "W1v", "pmT", "identb", "tri", "bqP", "bkP", "bvdP"])
        outs = {
            "QD1": nc.dram_tensor("oQD1", [1024, NT], BF16, kind="ExternalOutput").ap(),
            "KD1": nc.dram_tensor("oKD1", [1024, NT], BF16, kind="ExternalOutput").ap(),
            "VA1": nc.dram_tensor("oVA1", [NT, 1040], BF16, kind="ExternalOutput").ap(),
            "NEGDQ": nc.dram_tensor("oNEGDQ", [NT, 16], F32, kind="ExternalOutput").ap(),
            "DIAGK": nc.dram_tensor("oDIAGK", [NT, 16], F32, kind="ExternalOutput").ap(),
            "STABL": nc.dram_tensor("oSTABL", [1, 16], F32, kind="ExternalOutput").ap(),
        }
        cx = alloc_common(nc, stack)
        R = cx.R
        ldc = emit_consts(cx, dram)
        ldx = emit_load_x(cx, xd, cast=False)
        s_ev = emit_ln(cx, [(ldx, 16), (ldc, ldc.v)])
        s = emit_perf_a(cx, dram, outs, [(s_ev, NTB)])
        _fin(R, [s])
        R.flush(nc, stack)
    return nc


def build_L4b():
    from contextlib import ExitStack
    nc = bass.Bass("TRN2", target_bir_lowering=False, debug=False)
    with ExitStack() as stack:
        dram = _decl(nc, ["pmT", "identb", "tri", "bqP", "bkP", "bvdP"])
        ins_d = {
            "KD1o": nc.dram_tensor("KD1o", [1024, NT], BF16, kind="ExternalInput").ap(),
            "VA1o": nc.dram_tensor("VA1o", [NT, 1040], BF16, kind="ExternalInput").ap(),
            "DIAGKo": nc.dram_tensor("DIAGKo", [NT, 16], F32, kind="ExternalInput").ap(),
            "STABG": nc.dram_tensor("STABG", [1, 64], F32, kind="ExternalInput").ap(),
        }
        oKVL = nc.dram_tensor("oKVL", [4096, 65], F32, kind="ExternalOutput").ap()
        cx = alloc_common(nc, stack)
        R = cx.R
        ldc = emit_consts(cx, dram)
        s = emit_perf_b(cx, dram, ins_d, {"KVL": oKVL}, [(ldc, ldc.v)])
        _fin(R, [s])
        R.flush(nc, stack)
    return nc


def build_L4c():
    from contextlib import ExitStack
    nc = bass.Bass("TRN2", target_bir_lowering=False, debug=False)
    with ExitStack() as stack:
        xd = nc.dram_tensor("x", [NT, 1024], BF16, kind="ExternalInput").ap()
        dram = _decl(nc, ["pmT", "W1o", "Wf", "identb", "tri", "bqP", "bkP", "bvdP"])
        ins_d = {
            "QD1o": nc.dram_tensor("QD1o", [1024, NT], BF16, kind="ExternalInput").ap(),
            "NEGDQo": nc.dram_tensor("NEGDQo", [NT, 16], F32, kind="ExternalInput").ap(),
            "KV4": nc.dram_tensor("KV4", [4, 4096, 65], F32, kind="ExternalInput").ap(),
        }
        oX3 = nc.dram_tensor("oOut", [NT, 1024], F32, kind="ExternalOutput").ap()
        cx = alloc_common(nc, stack)
        R = cx.R
        ldc = emit_consts(cx, dram)
        ldx = emit_load_x(cx, xd, cast=False)
        s = emit_perf_c(cx, dram, ins_d, [(ldx, 16), (ldc, ldc.v)])
        s_ev3 = emit_ln(cx, [(s, s.v)], tag="ln3")
        s_f = emit_ffn(cx, dram, oX3, [(s_ev3, NTB)])
        _fin(R, [s_f])
        R.flush(nc, stack)
    return nc


def build_L5():
    from contextlib import ExitStack
    nc = bass.Bass("TRN2", target_bir_lowering=False, debug=False)
    with ExitStack() as stack:
        xd = nc.dram_tensor("x", [NT, 1024], BF16, kind="ExternalInput").ap()
        dram = _decl(nc, ["Wf", "identb", "tri", "bqP", "bkP", "bvdP"])
        oOut = nc.dram_tensor("oOut", [NT, 1024], F32, kind="ExternalOutput").ap()
        cx = alloc_common(nc, stack)
        R = cx.R
        ldc = emit_consts(cx, dram)
        ldx = emit_load_x(cx, xd, cast=False)
        s_ev = emit_ln(cx, [(ldx, 16), (ldc, ldc.v)])
        s = emit_ffn(cx, dram, oOut, [(s_ev, NTB)])
        _fin(R, [s])
        R.flush(nc, stack)
    return nc


def kernel(**ins):
    P = pack_consts(ins)
    x = np.asarray(ins["L0_attn_x"], np.float32)
    CN1 = ["WkP", "WvdP", "bqP", "bkP", "bvdP", "identb", "tri"]
    nc1 = build_L1()
    im1 = [dict(x=shard_rows(x, c), **{n: P[n] for n in CN1}) for c in range(8)]
    r1 = run_bass_kernel_spmd(nc1, im1, core_ids=list(range(8))).results
    KGs, VDGs = {}, {}
    for g in range(2):
        KG = np.zeros((1024, 8192), P["identb"].dtype)
        VDG = np.zeros((256, 8192), P["identb"].dtype)
        for mm in range(4):
            c = g * 4 + mm
            for ei, ge in enumerate([mm, 7 - mm]):
                KG[:, ge * 1024:(ge + 1) * 1024] = r1[c]["oKD"][:, ei * 1024:(ei + 1) * 1024]
                VDG[:, ge * 1024:(ge + 1) * 1024] = r1[c]["oVD"][:, ei * 1024:(ei + 1) * 1024]
        KGs[g], VDGs[g] = KG, VDG
    CN2 = ["WqP", "bqP", "bkP", "bvdP", "identb", "tri", "Wvu2", "bvu2"]
    x1s = [None] * 8
    for mm in range(4):
        nc2 = build_L2(mm)
        cores = [mm, 4 + mm]
        im2 = [dict(x=shard_rows(x, c), KG=KGs[c // 4], VDG=VDGs[c // 4],
                    **{n: P[n] for n in CN2}) for c in cores]
        r2 = run_bass_kernel_spmd(nc2, im2, core_ids=cores).results
        for i, c in enumerate(cores):
            x1s[c] = r2[i]["oX1"]
    CN3 = ["WeP", "beP", "WgP", "bgP", "W1qP", "W1kP", "W1v", "pmT",
           "identb", "tri", "bqP", "bkP", "bvdP"]
    nc3 = build_L3()
    im3 = [dict(x=x1s[c], **{n: P[n] for n in CN3}) for c in range(8)]
    r3 = run_bass_kernel_spmd(nc3, im3, core_ids=list(range(8))).results
    x2s = [r3[c]["oX2"] for c in range(8)]
    r4a = r3
    CN4b = ["pmT", "identb", "tri", "bqP", "bkP", "bvdP"]
    nc4b = build_L4b()
    im4b = []
    for c in range(8):
        g = c // 4
        stabg = np.concatenate([r4a[g * 4 + j]["oSTABL"] for j in range(4)], axis=1)
        im4b.append(dict(KD1o=r4a[c]["oKD1"], VA1o=r4a[c]["oVA1"],
                         DIAGKo=r4a[c]["oDIAGK"], STABG=stabg,
                         **{n: P[n] for n in CN4b}))
    r4b = run_bass_kernel_spmd(nc4b, im4b, core_ids=list(range(8))).results
    CN4c = ["pmT", "W1o", "Wf", "identb", "tri", "bqP", "bkP", "bvdP"]
    nc4c = build_L4c()
    im4c = []
    for c in range(8):
        g = c // 4
        kv4 = np.stack([r4b[g * 4 + j]["oKVL"] for j in range(4)], axis=0)
        im4c.append(dict(x=x2s[c], QD1o=r4a[c]["oQD1"], NEGDQo=r4a[c]["oNEGDQ"],
                         KV4=kv4, **{n: P[n] for n in CN4c}))
    r4c = run_bass_kernel_spmd(nc4c, im4c, core_ids=list(range(8))).results
    return unshard_rows([r4c[c]["oOut"] for c in range(8)])


CONST_NAMES_A = ["WqP", "WkP", "WvdP", "bqP", "bkP", "bvdP", "identb", "identf", "tri"]


def build_stageA_test():
    from contextlib import ExitStack
    nc = bass.Bass("TRN2", target_bir_lowering=False, debug=False)
    stack = ExitStack()
    with stack:
        xd = nc.dram_tensor("x", [NT, 1024], F32, kind="ExternalInput").ap()
        P = {n: None for n in CONST_NAMES_A}
        shapes = {"WqP": [8, 128, 8, 128], "WkP": [8, 128, 8, 128], "WvdP": [4, 128, 8, 64],
                  "bqP": [128, 8], "bkP": [128, 8], "bvdP": [64, 4],
                  "identb": [128, 128], "identf": [128, 128], "tri": [128, 128]}
        dts = {"identb": BF16, "tri": BF16}
        dram = {n: nc.dram_tensor(n, shapes[n], dts.get(n, F32), kind="ExternalInput").ap()
                for n in CONST_NAMES_A}
        oXN = nc.dram_tensor("oXN", [128, 8 * NT], BF16, kind="ExternalOutput").ap()
        oQD = nc.dram_tensor("oQD", [128, 8 * NT], BF16, kind="ExternalOutput").ap()
        oKD = nc.dram_tensor("oKD", [1024, NT], BF16, kind="ExternalOutput").ap()
        oVD = nc.dram_tensor("oVD", [256, NT], BF16, kind="ExternalOutput").ap()
        cx = alloc_common(nc, stack)
        R = cx.R
        ldc = emit_consts(cx, dram)
        ldx = emit_load_x(cx, xd)
        s_ev = emit_ln(cx, [(ldx, 16)])
        s_q, m_q = emit_l0_proj(cx, dram, [(s_ev, NTB)], [(ldc, ldc.v)])
        s_k, m_k, s_kdma = emit_k_proj(cx, dram["WkP"], 8, oKD, [(s_ev, NTB)],
                                       [(ldc, ldc.v)], [(m_q, 16)], "p0k")
        s_v, m_v, s_vdma = emit_l0_vd(cx, dram, oVD, [(s_ev, NTB)], [(ldc, ldc.v)],
                                      [(m_k, 16)])
        st = R.sem("st")
        emit_store(cx, oXN, cx.t["XN"][:], [(s_ev, NTB)], st)
        emit_store(cx, oQD, cx.t["QD"][:], [(s_q, s_q.v)], st)
        R.op("sync", None, wait=[(st, st.v), (s_vdma, s_vdma.v), (s_kdma, s_kdma.v)])
        R.flush(nc, stack)
    return nc


CONST_NAMES_B = CONST_NAMES_A + ["Wvu2", "bvu2"]


def build_stageB_test():
    from contextlib import ExitStack
    nc = bass.Bass("TRN2", target_bir_lowering=False, debug=False)
    stack = ExitStack()
    with stack:
        xd = nc.dram_tensor("x", [NT, 1024], F32, kind="ExternalInput").ap()
        shapes = {"WqP": [8, 128, 8, 128], "WkP": [8, 128, 8, 128], "WvdP": [4, 128, 8, 64],
                  "bqP": [128, 8], "bkP": [128, 8], "bvdP": [64, 4],
                  "identb": [128, 128], "identf": [128, 128], "tri": [128, 128],
                  "Wvu2": [8, 32, 128], "bvu2": [8, 1, 128]}
        dts = {"identb": BF16, "tri": BF16}
        dram = {n: nc.dram_tensor(n, shapes[n], dts.get(n, F32), kind="ExternalInput").ap()
                for n in CONST_NAMES_B}
        KGd = nc.dram_tensor("KG", [1024, 8192], BF16, kind="ExternalInput").ap()
        VDGd = nc.dram_tensor("VDG", [256, 8192], BF16, kind="ExternalInput").ap()
        oX1 = nc.dram_tensor("oX1", [NT, 1024], BF16, kind="ExternalOutput").ap()
        oZ = nc.dram_tensor("oZ", [128, 256], F32, kind="ExternalOutput").ap()
        cx = alloc_common(nc, stack)
        R = cx.R
        ldc = emit_consts(cx, dram)
        ldx = emit_load_x(cx, xd)
        s_ev = emit_ln(cx, [(ldx, 16), (ldc, ldc.v)])
        s_q, m_q = emit_l0_proj(cx, dram, [(s_ev, NTB)], [(ldc, ldc.v)])
        kg_fn = lambda g, ke: KGd[g * 128:(g + 1) * 128, ke * 1024:(ke + 1) * 1024]
        vdg_fn = lambda g, ke: VDGd[(g // 2) * 64 + (g % 2) * 32:(g // 2) * 64 + (g % 2) * 32 + 32,
                                    ke * 1024:(ke + 1) * 1024]
        s_stt = emit_l0_attn(cx, dram, kg_fn, vdg_fn, [0, 7],
                             dep_q=[(s_q, s_q.v)], dep_kv=[], dep_x=[(s_ev, NTB)],
                             dep_pb=[(s_q, s_q.v)], dep_c=[(ldc, ldc.v)],
                             dbg={"x1banks": oX1} if _osmod.environ.get("DBGB") else None, groups=range(int(_osmod.environ.get("NGROUPS", "8"))))
        st = R.sem("st")
        if not _osmod.environ.get("DBGB"):
            R.op("sync", lambda e: e.dma_start(
                out=oX1.rearrange("(tb p) c -> p tb c", p=128),
                in_=cx.t["X"].rearrange("p (tb c) -> p tb c", tb=NTB)),
                wait=[(s_stt, s_stt.v)], inc=(st, 16))
        else:
            pass
        R.op("sync", lambda e: e.dma_start(out=oZ, in_=cx.t["ZS"][:]),
            wait=[(s_stt, s_stt.v)], inc=(st, 16))
        R.op("sync", None, wait=[(st, st.v)])
        R.flush(nc, stack)
    return nc





# version 08
# speedup vs baseline: 1.0041x; 1.0025x over previous
import os as _osmod
import numpy as np
import ml_dtypes
import concourse.bass as bass
import concourse.mybir as mybir
from concourse.bass_utils import run_bass_kernel_spmd

F32 = mybir.dt.float32
BF16 = mybir.dt.bfloat16
AF = mybir.ActivationFunctionType
ALU = mybir.AluOpType
AX = mybir.AxisListType

B, S, E = 2, 8192, 1024
H, D, RK = 16, 64, 16
LN16 = float(np.log(16.0))
STAB0 = 4.0
EPS_LN = 1e-5
EPS_PERF = 1e-6
NT = 2048
NTB = 16
SERIAL_ATT = True
RG = [[0, 1, 2, 3], [4, 5, 6, 7]]
BF = ml_dtypes.bfloat16


class Sem:
    __slots__ = ("name", "v", "h")

    def __init__(self, name):
        self.name, self.v, self.h = name, 0, None


TOTAL_ORDER = True


class Rec:
    ENGINES = ("tensor", "vector", "scalar", "gpsimd", "sync")

    def __init__(self):
        self.q = {e: [] for e in self.ENGINES}
        self.sems = []
        self.total = TOTAL_ORDER
        if self.total:
            self.G = self.sem("Gtot")

    def sem(self, name):
        s = Sem(name)
        self.sems.append(s)
        return s

    def op(self, eng, fn, wait=(), inc=None):
        if self.total:
            ws = [(self.G, self.G.v)]
            amt = inc[1] if inc is not None else 1
            if inc is not None:
                inc[0].v += inc[1]
            if fn is None:
                self.q[eng].append((ws, None, None))
                return
            self.G.v += amt
            self.q[eng].append((ws, fn, (self.G, amt)))
            return
        ws = [(w, w.v) if isinstance(w, Sem) else w for w in wait]
        if inc is not None:
            inc[0].v += inc[1]
        self.q[eng].append((ws, fn, inc))

    def flush(self, nc, stack):
        for s in self.sems:
            s.h = stack.enter_context(nc.semaphore(s.name))
        block = stack.enter_context(nc.Block())
        for eng in self.ENGINES:
            items = self.q[eng]
            if not items:
                continue

            def make(items):
                def body(e):
                    for ws, fn, inc in items:
                        for s, v in ws:
                            if v > 0:
                                e.wait_ge(s.h, v)
                        if fn is not None:
                            inst = fn(e)
                            if inc is not None:
                                inst.then_inc(inc[0].h, inc[1])
                        elif inc is not None:
                            e.nop().then_inc(inc[0].h, inc[1])
                return body
            getattr(block, eng)(make(items))


def _qk_pack(Wm):
    Wp = Wm.reshape(8, 2, 1024, 64).transpose(0, 2, 1, 3).reshape(8, 1024, 128)
    Wp = Wp.reshape(8, 8, 128, 128).transpose(0, 2, 1, 3)
    return np.ascontiguousarray(Wp, np.float32)


def pack_consts(ins):
    f32 = np.float32
    P = {}
    P["WqP"] = _qk_pack(ins["L0_attn_Wq"])
    P["WkP"] = _qk_pack(ins["L0_attn_Wk"])
    P["bqP"] = np.ascontiguousarray(ins["L0_attn_bq"].reshape(8, 128).T, f32)
    P["bkP"] = np.ascontiguousarray(ins["L0_attn_bk"].reshape(8, 128).T, f32)
    Wvd = ins["L0_attn_Wvd"].reshape(4, 4, 1024, 16).transpose(0, 2, 1, 3).reshape(4, 1024, 64)
    Wvd = Wvd.reshape(4, 8, 128, 64).transpose(0, 2, 1, 3)
    P["WvdP"] = np.ascontiguousarray(Wvd, f32)
    P["bvdP"] = np.ascontiguousarray(ins["L0_attn_bvd"].reshape(4, 64).T, f32)
    Wvu = ins["L0_attn_Wvu"]
    W2 = np.zeros((8, 32, 128), f32)
    for g in range(8):
        W2[g, 0:16, 0:64] = Wvu[2 * g]
        W2[g, 16:32, 64:128] = Wvu[2 * g + 1]
    P["Wvu2"] = W2
    P["bvu2"] = np.ascontiguousarray(ins["L0_attn_bvu"].reshape(8, 1, 128), f32)
    P["WeP"] = np.ascontiguousarray(ins["L0_ffn_We"], f32)
    P["beP"] = np.ascontiguousarray(ins["L0_ffn_be"].reshape(8, 1, 1024), f32)
    P["WgP"] = np.ascontiguousarray(ins["L0_ffn_Wg"], f32)
    P["bgP"] = np.ascontiguousarray(ins["L0_ffn_bg"].reshape(1, 8), f32)
    Wqkv = ins["L1_attn_W_qkv"]
    P["W1qP"] = _qk_pack(np.ascontiguousarray(Wqkv[:, 0:1024]).reshape(1024, 16, 64).transpose(1, 0, 2))
    P["W1kP"] = _qk_pack(np.ascontiguousarray(Wqkv[:, 1024:2048]).reshape(1024, 16, 64).transpose(1, 0, 2))
    P["W1v"] = np.ascontiguousarray(Wqkv[:, 2048:3072], f32)
    P["pmT"] = np.ascontiguousarray(ins["L1_attn_proj"].transpose(0, 2, 1), f32)
    P["W1o"] = np.ascontiguousarray(ins["L1_attn_W_out"], f32)
    P["Wf"] = np.ascontiguousarray(ins["L1_ffn_weight"], f32)
    P["identb"] = np.eye(128, dtype=BF)
    P["identf"] = np.eye(128, dtype=f32)
    P["tri"] = np.tril(np.ones((128, 128), f32)).astype(BF)
    return P


def shard_rows(x_full, c):
    g, m = c // 4, c % 4
    xb = x_full[g]
    return np.ascontiguousarray(
        np.concatenate([xb[m * 1024:(m + 1) * 1024], xb[(7 - m) * 1024:(8 - m) * 1024]], axis=0),
        np.float32)


def unshard_rows(shards):
    out = np.empty((B, S, E), np.float32)
    for c in range(8):
        g, m = c // 4, c % 4
        out[g, m * 1024:(m + 1) * 1024] = shards[c][0:1024]
        out[g, (7 - m) * 1024:(8 - m) * 1024] = shards[c][1024:2048]
    return out


class Ctx:
    pass


def alloc_common(nc, stack):
    cx = Ctx()
    cx.nc = nc
    cx.R = Rec()
    ec = stack.enter_context
    t = {}

    def sb(name, shape, dt):
        t[name] = ec(nc.sbuf_tensor(name, shape, dt))
    cx.t = t
    sb("X", [128, NTB * 1024], BF16)
    sb("OS0", [128, 1024], F32)
    sb("OS1", [128, 1024], F32)
    sb("XN", [128, 8 * NT], BF16)
    sb("QD", [128, 8 * NT], BF16)
    sb("KSTR0", [128, 2048], BF16)
    sb("KSTR1", [128, 2048], BF16)
    sb("WEB", [128, 8192], BF16)
    sb("VAP", [128, 16 * 1040], BF16)
    sb("VT0", [128, 8 * 130], BF16)
    sb("VT1", [128, 8 * 130], BF16)
    sb("VD4", [64, NT], BF16)
    sb("ES0", [128, 512], BF16)
    sb("ES1", [128, 512], BF16)
    sb("AVSB", [128, 512], BF16)
    sb("W0", [128, 1024], BF16)
    sb("W1", [128, 1024], BF16)
    sb("WR0", [32, 1024], BF16)
    sb("XT0", [128, 1024], BF16)
    sb("STATS", [128, 164], F32)
    sb("CONSTB", [128, 256], BF16)
    sb("ONE1", [1, 128], BF16)
    sb("BQK", [128, 16], F32)
    sb("CB", [128, 4], F32)
    sb("BVD", [64, 4], F32)
    sb("BR", [1, 1024], BF16)
    sb("WG", [128, 64], BF16)
    sb("BG1", [1, 8], BF16)
    sb("NEGD", [128, 48], F32)
    sb("KT64", [128, 64], BF16)
    sb("K264", [128, 64], BF16)
    sb("KVA", [128, 2080], F32)
    sb("KVB", [128, 2080], F32)
    sb("QFT", [128, 256], BF16)
    sb("ATN", [128, 64], BF16)
    sb("KF", [128, 256], BF16)
    sb("MKB", [128, 128], BF16)
    sb("ZB5", [128, 512], BF16)
    sb("EPI", [128, 528], BF16)
    sb("KVS", [128, 130], F32)
    sb("KVH0", [128, 130], BF16)
    sb("KVH1", [128, 130], BF16)
    sb("WV2", [32, 128], BF16)
    sb("PM0", [64, 128], BF16)
    sb("PM1", [64, 128], BF16)
    sb("MX", [128, 16], F32)
    sb("SG", [128, 24], F32)
    sb("SR16", [16, 128], BF16)
    sb("ZS", [128, 256], F32)
    sb("ST16", [16, 1], F32)
    cx.pb = [ec(nc.psum_tensor(f"pb{i}", [128, 512], F32)) for i in range(6)]
    cx.pt = [ec(nc.psum_tensor(f"pt{i}", [128, 1024], BF16)) for i in range(2)]
    return cx


def att_alias(cx):
    return cx.t["VAP"]


def emit_consts(cx, dram):
    R, t = cx.R, cx.t
    ld = R.sem("ldc")
    pairs = [
        (dram["identb"], t["CONSTB"][:, 0:128]),
        (dram["tri"], t["CONSTB"][:, 128:256]),
        (dram["bqP"], t["BQK"][:, 0:8]),
        (dram["bkP"], t["BQK"][:, 8:16]),
        (dram["bvdP"], t["BVD"][:]),
    ]
    for src, dst in pairs:
        R.op("sync", lambda e, src=src, dst=dst: e.dma_start(out=dst, in_=src), inc=(ld, 16))
    R.op("gpsimd", lambda e: e.memset(t["ONE1"][:], 1.0), inc=(ld, 1))
    R.op("gpsimd", lambda e: e.memset(t["CB"][:, 0:1], EPS_LN), inc=(ld, 1))
    R.op("gpsimd", lambda e: e.memset(t["CB"][:, 1:2], -STAB0), inc=(ld, 1))
    return ld


def emit_ln(cx, dep, x_src="X", tag="ln"):
    R, t = cx.R, cx.t
    X, XN = t[x_src], t["XN"]
    SM = t["STATS"]
    s_sq = R.sem(tag + "_sq")
    s_st = R.sem(tag + "_st")
    s_xn = R.sem(tag + "_xn")
    s_tr = R.sem(tag + "_tr")
    s_ev = R.sem(tag + "_ev")
    s_s2 = R.sem(tag + "_s2")
    for tb in range(NTB):
        xt = X[:, tb * 1024:(tb + 1) * 1024]
        sqb = t["W0"] if tb % 2 == 0 else t["W1"]
        xnb = t["XT0"]
        c = 10 * tb
        wsq = list(dep) + ([] if tb < 2 else [(s_s2, tb - 1)])
        R.op("scalar", lambda e, xt=xt, sqb=sqb: e.activation(out=sqb[:], in_=xt, func=AF.Square),
             wait=wsq, inc=(s_sq, 1))
        R.op("vector", lambda e, xt=xt, c=c: e.reduce_sum(SM[:, c:c + 1], xt, axis=AX.X), wait=list(dep))
        R.op("vector", lambda e, sqb=sqb, c=c: e.reduce_sum(SM[:, c + 1:c + 2], sqb[:], axis=AX.X),
             wait=[(s_sq, tb + 1)], inc=(s_s2, 1))
        R.op("vector", lambda e, c=c: e.tensor_scalar(out=SM[:, c + 2:c + 3], in0=SM[:, c:c + 1],
                                                      scalar1=1.0 / 1024, scalar2=None, op0=ALU.mult))
        R.op("vector", lambda e, c=c: e.tensor_scalar(out=SM[:, c + 3:c + 4], in0=SM[:, c + 1:c + 2],
                                                      scalar1=1.0 / 1024, scalar2=None, op0=ALU.mult))
        R.op("vector", lambda e, c=c: e.tensor_tensor(out=SM[:, c + 4:c + 5], in0=SM[:, c + 2:c + 3],
                                                      in1=SM[:, c + 2:c + 3], op=ALU.mult))
        R.op("vector", lambda e, c=c: e.tensor_tensor(out=SM[:, c + 5:c + 6], in0=SM[:, c + 3:c + 4],
                                                      in1=SM[:, c + 4:c + 5], op=ALU.subtract), inc=(s_xn, 1))
        R.op("scalar", lambda e, c=c: e.activation(out=SM[:, c + 6:c + 7], in_=SM[:, c + 5:c + 6],
                                                   func=AF.Ln, bias=t["CB"][:, 0:1]), wait=[(s_xn, 2 * tb + 1)])
        R.op("scalar", lambda e, c=c: e.activation(out=SM[:, c + 7:c + 8], in_=SM[:, c + 6:c + 7],
                                                   func=AF.Exp, scale=-0.5), inc=(s_st, 1))
        wxn = [(s_st, tb + 1)] + ([] if tb < 1 else [(s_tr, tb * 8)])
        R.op("vector", lambda e, c=c: e.scalar_tensor_tensor(out=SM[:, c + 8:c + 9], in0=SM[:, c + 2:c + 3],
                                                             scalar=-1.0, in1=SM[:, c + 7:c + 8],
                                                             op0=ALU.mult, op1=ALU.mult), wait=wxn)
        R.op("vector", lambda e, xt=xt, xnb=xnb, c=c: e.tensor_scalar(
            out=xnb[:], in0=xt, scalar1=SM[:, c + 7:c + 8], scalar2=SM[:, c + 8:c + 9],
            op0=ALU.mult, op1=ALU.add), inc=(s_xn, 1))
        for f in range(8):
            wt = [(s_xn, 2 * tb + 2)] if f == 0 else []
            if f == 0 and tb >= 2:
                wt.append((s_ev, tb - 1))
            R.op("tensor", lambda e, xnb=xnb, f=f, tb=tb: e.transpose(
                cx.pt[tb % 2][:, f * 128:(f + 1) * 128], xnb[:, f * 128:(f + 1) * 128],
                cx.t["CONSTB"][:, 0:128]), wait=wt, inc=(s_tr, 1))
        R.op("scalar", lambda e, tb=tb: e.copy(
            out=XN.rearrange("p (f c) -> p f c", f=8)[:, :, tb * 128:(tb + 1) * 128],
            in_=cx.pt[tb % 2].rearrange("p (f c) -> p f c", f=8)[:, :, :]),
            wait=[(s_tr, (tb + 1) * 8)], inc=(s_ev, 1))
    return s_ev


def _mm_proj_pairs(cx, Wdram, n_pairs, m_width, dst, bias_ap_fn, dep, tag,
                   kc_n=8, dep_w=(), dst_stride=NT, after_pair=None):
    R, t = cx.R, cx.t
    XN = t["XN"]
    s_wld = R.sem(f"{tag}_wld")
    s_mm = R.sem(f"{tag}_mm")
    s_done = R.sem(f"{tag}_ev")
    s_dma = R.sem(f"{tag}_dma") if after_pair is not None else None
    for pr in range(n_pairs):
        wb = t["W0"] if pr % 2 == 0 else t["W1"]
        wld = (list(dep_w) if pr < 2 else []) + ([] if pr < 2 else [(s_mm, 2 * pr - 2)])
        R.op("gpsimd", lambda e, pr=pr, wb=wb: e.dma_start(
            out=wb[0:128, 0:kc_n * m_width],
            in_=Wdram[pr].rearrange("p k m -> p (k m)")), wait=wld, inc=(s_wld, 16))
        for nq2 in range(2):
            it = pr * 2 + nq2
            b0, b1 = (it % 2) * 2, (it % 2) * 2 + 1
            for kc in range(kc_n):
                w = []
                if kc == 0:
                    w.append((s_wld, 16 * (pr + 1)))
                    if it == 0:
                        w.extend(dep)
                    if it >= 2:
                        w.append((s_done, 2 * it - 2))
                for half in range(2):
                    nq = nq2 * 2 + half
                    pbk = cx.pb[b0 if half == 0 else b1]
                    R.op("tensor", lambda e, kc=kc, nq=nq, wb=wb, pbk=pbk: e.matmul(
                        pbk[0:m_width, :],
                        wb[:, kc * m_width:(kc + 1) * m_width],
                        XN[:, kc * NT + nq * 512: kc * NT + (nq + 1) * 512],
                        start=(kc == 0), stop=(kc == kc_n - 1)),
                        wait=w if half == 0 else [],
                        inc=(s_mm, 1) if (kc == kc_n - 1 and half == 1) else None)
            for half in range(2):
                nq = nq2 * 2 + half
                pbk = cx.pb[b0 if half == 0 else b1]
                bias = bias_ap_fn(pr)
                wev = [(s_mm, it + 1)] if half == 0 else []
                if it == 0 and half == 0:
                    wev.extend(dep)
                if after_pair is not None and pr >= 1 and it == 2 * pr and half == 0:
                    wev.append((s_dma, 16 * pr))
                if bias is not None:
                    R.op("vector", lambda e, pr=pr, nq=nq, pbk=pbk, bias=bias: e.tensor_scalar(
                        out=dst[0:m_width, pr * dst_stride + nq * 512: pr * dst_stride + (nq + 1) * 512],
                        in0=pbk[0:m_width, :], scalar1=bias, scalar2=None, op0=ALU.add),
                        wait=wev, inc=(s_done, 1))
                else:
                    R.op("vector", lambda e, pr=pr, nq=nq, pbk=pbk: e.tensor_copy(
                        out=dst[0:m_width, pr * dst_stride + nq * 512: pr * dst_stride + (nq + 1) * 512],
                        in_=pbk[0:m_width, :]),
                        wait=wev, inc=(s_done, 1))
        if after_pair is not None:
            dram_ap, sbuf_ap = after_pair(pr)
            R.op("sync", lambda e, dram_ap=dram_ap, sbuf_ap=sbuf_ap: e.dma_start(
                out=dram_ap, in_=sbuf_ap), wait=[(s_done, 4 * (pr + 1))], inc=(s_dma, 16))
    if after_pair is not None:
        return s_done, s_mm, s_dma
    return s_done, s_mm


def emit_l0_proj(cx, dram, dep_xn, dep_c):
    t = cx.t
    dep = list(dep_xn) + list(dep_c)
    s_q, m_q = _mm_proj_pairs(cx, dram["WqP"], 8, 128, t["QD"],
                              lambda pr: t["BQK"][:, pr:pr + 1], dep, "p0q",
                              dep_w=list(dep_xn))
    return s_q, m_q


def emit_k_proj(cx, Wdram, bias_col0, k_dram, dep_xn, dep_c, dep_w, tag):
    t = cx.t
    dep = list(dep_xn) + list(dep_c)
    bias_fn = (lambda pr: None) if bias_col0 is None else (
        lambda pr: t["BQK"][:, bias_col0 + pr:bias_col0 + pr + 1])
    s_k, m_k, s_dma = _mm_proj_pairs(
        cx, Wdram, 8, 128, t["KSTR0"], bias_fn, dep, tag,
        dep_w=list(dep_w), dst_stride=0,
        after_pair=lambda pr: (k_dram[pr * 128:(pr + 1) * 128, :], t["KSTR0"][:, :]))
    return s_k, m_k, s_dma


def emit_l0_vd(cx, dram, vd_dram, dep_xn, dep_c, dep_w):
    t = cx.t
    dep = list(dep_xn) + list(dep_c)
    s_v, m_v, s_dma = _mm_proj_pairs(
        cx, dram["WvdP"], 4, 64, t["VD4"],
        lambda pr: t["BVD"][:, pr:pr + 1], dep, "p0v",
        dep_w=list(dep_w), dst_stride=0,
        after_pair=lambda pr: (vd_dram[pr * 64:(pr + 1) * 64, :], t["VD4"][0:64, :]))
    return s_v, m_v, s_dma


def emit_load_x(cx, x_dram, cast=True):
    R, t = cx.R, cx.t
    s = R.sem("ldx")
    eng = "gpsimd" if cast else "sync"
    R.op(eng, lambda e: e.dma_start(
        out=t["X"].rearrange("p (tb c) -> p tb c", tb=NTB),
        in_=x_dram.rearrange("(tb p) c -> p tb c", p=128)), inc=(s, 16))
    return s



def emit_l0_attn(cx, dram, kg_fn, vdg_fn, own_ge, dep_q, dep_kv, dep_x, dep_pb, dep_c=(), dbg=None, groups=range(8)):
    R, t = cx.R, cx.t
    QD, X = t["QD"], t["X"]
    s_ini = R.sem("a_ini")
    s_kld = R.sem("a_kld")
    s_vld = R.sem("a_vld")
    s_vt = R.sem("a_vt")
    s_vte = R.sem("a_vte")
    s_sc = R.sem("a_sc")
    s_ex = R.sem("a_ex")
    s_mk = R.sem("a_mk")
    s_av = R.sem("a_av")
    s_ep = R.sem("a_ep")
    s_ept = R.sem("a_ept")
    s_stt = R.sem("a_stt")
    s_ku = R.sem("a_ku")
    emit_l0_attn._dep_c = list(dep_c)
    es_free = {0: 0, 1: 0}
    es_i = 0
    it = 0
    started = set()
    R.op("gpsimd", lambda e: e.memset(t["ZB5"][:], 0.0), inc=(s_ini, 1))
    for vtb in (t["VT0"], t["VT1"]):
        for col in (64, 129):
            R.op("gpsimd", lambda e, vtb=vtb, col=col: e.memset(
                vtb.rearrange("p (tb c) -> p tb c", tb=8)[:, :, col:col + 1], 1.0),
                inc=(s_ini, 1))
    first_te = [True]
    for g in groups:
        wc = [(s_vt, s_vt.v)]
        R.op("gpsimd", lambda e, g=g: e.dma_start(out=t["WV2"][:], in_=dram["Wvu2"][g]),
             wait=wc, inc=(s_vld, 16))
        R.op("gpsimd", lambda e, g=g: e.dma_start(out=t["BR"][0:1, 0:128], in_=dram["bvu2"][g]),
             wait=[(s_vt, s_vt.v)], inc=(s_vld, 16))
        wv_ready = (s_vld, s_vld.v)
        for ei, ge in enumerate(own_ge):
            for ke in range(ge, 8):
                kb = t["KSTR0"] if it % 2 == 0 else t["KSTR1"]
                vtb = t["VT0"] if it % 2 == 0 else t["VT1"]
                wk = list(dep_kv) + ([(s_ku, it - 1)] if it >= 2 else [])
                R.op("sync", lambda e, g=g, ke=ke, kb=kb: e.dma_start(
                    out=kb[:, 0:1024], in_=kg_fn(g, ke)), wait=wk, inc=(s_kld, 16))
                R.op("sync", lambda e, g=g, ke=ke: e.dma_start(
                    out=t["WR0"][:, 0:1024], in_=vdg_fn(g, ke)),
                    wait=[(s_vt, s_vt.v)], inc=(s_vld, 16))
                vld_here = (s_vld, s_vld.v)
                for tb8 in range(8):
                    w = []
                    if tb8 == 0:
                        w = [wv_ready, vld_here, (s_ex, s_ex.v),
                             (s_vte, s_vte.v), (s_ini, 5)]
                        if it >= 2:
                            w.append((s_vte, s_vte.v))
                        if first_te[0]:
                            w += list(dep_q) + list(dep_pb) + list(emit_l0_attn._dep_c)
                            first_te[0] = False
                    else:
                        w = [(s_vte, s_vte.v)]
                    R.op("tensor", lambda e, tb8=tb8: e.matmul(
                        cx.pb[4][:, 0:128],
                        t["WR0"][0:32, tb8 * 128:(tb8 + 1) * 128], t["WV2"][:],
                        start=True, stop=False), wait=w)
                    R.op("tensor", lambda e: e.matmul(
                        cx.pb[4][:, 0:128], t["ONE1"][0:1, 0:128], t["BR"][0:1, 0:128],
                        start=False, stop=True), inc=(s_vt, 1))
                    wev = [(s_vt, s_vt.v)]
                    if tb8 == 0 and it >= 2:
                        wev.append((s_ku, it - 1))
                    R.op("scalar", lambda e, tb8=tb8, vtb=vtb: e.copy(
                        out=vtb[:, tb8 * 130:tb8 * 130 + 130].rearrange(
                            "p (hh c) -> p hh c", hh=2)[:, :, 0:64],
                        in_=cx.pb[4][:, 0:128].rearrange("p (hh c) -> p hh c", hh=2)),
                        wait=wev, inc=(s_vte, 1))
                if dbg is not None and "vt" in dbg and it == 8:
                    R.op("sync", lambda e, vtb=vtb: e.dma_start(out=dbg["vt"], in_=vtb[:, :]),
                         wait=[(s_vte, s_vte.v)], inc=(s_ini, 16))
                for hh in range(2):
                    tcs = [(tc, (tc + 1) * 128 if ke == ge else 1024) for tc in range(8)]
                    for tc, ns in tcs:
                        for sh in range((ns + 511) // 512):
                            n = min(512, ns - sh * 512)
                            par = es_i % 2
                            esb = t["ES0"] if par == 0 else t["ES1"]
                            scb = cx.pb[4 + par]
                            wsc = [(s_kld, s_kld.v), (s_vte, s_vte.v), (s_ex, s_ex.v)]
                            if SERIAL_ATT:
                                wsc += [(s_av, s_av.v), (s_mk, s_mk.v), (s_stt, s_stt.v)]
                            R.op("tensor", lambda e, g=g, hh=hh, tc=tc, sh=sh, n=n, ei=ei, kb=kb, scb=scb: e.matmul(
                                scb[:, 0:n],
                                kb[hh * 64:hh * 64 + 64, tc * 128:(tc + 1) * 128],
                                QD[hh * 64:hh * 64 + 64,
                                   g * NT + ei * 1024 + sh * 512: g * NT + ei * 1024 + sh * 512 + n],
                                start=True, stop=True),
                                wait=wsc,
                                inc=(s_sc, 1))
                            wex = [(s_sc, s_sc.v)]
                            if es_i == 0:
                                wex += list(emit_l0_attn._dep_c)
                            if es_free[par] > 0:
                                wex.append((s_av, es_free[par]))
                            R.op("scalar", lambda e, esb=esb, scb=scb, n=n: e.activation(
                                out=esb[:, 0:n], in_=scb[:, 0:n], func=AF.Exp,
                                scale=1.0 / 32, bias=t["CB"][:, 1:2]),
                                wait=wex, inc=(s_ex, 1))
                            masked = (ke == ge) and (tc // 4 == sh)
                            j0 = tc * 128 - sh * 512 if masked else n
                            if masked:
                                R.op("vector", lambda e, esb=esb, j0=j0: e.tensor_tensor(
                                    out=t["MKB"][:], in0=esb[:, j0:j0 + 128],
                                    in1=t["CONSTB"][:, 128:256], op=ALU.mult),
                                    wait=[(s_ex, s_ex.v)], inc=(s_mk, 1))
                            if dbg is not None and "es" in dbg and es_i == 0:
                                R.op("sync", lambda e, esb=esb, n=n: e.dma_start(
                                    out=dbg["es"][:, 0:n], in_=esb[:, 0:n]),
                                    wait=[(s_mk, s_mk.v)] if masked else [(s_ex, s_ex.v)],
                                    inc=(s_ini, 16))
                            bank = 2 * hh + sh
                            key = (g, ei, bank)
                            st_flag = key not in started
                            started.add(key)
                            stop_flag = False
                            w = [(s_ex, s_ex.v)]
                            if st_flag:
                                w.append((s_ep, s_ep.v))
                            if j0 > 0:
                                R.op("tensor", lambda e, vtb=vtb, hh=hh, tc=tc, bank=bank, esb=esb, j0=j0,
                                     st=st_flag, sp=(stop_flag and not masked): e.matmul(
                                    cx.pb[bank][0:65, 0:j0],
                                    vtb[:, tc * 130 + hh * 65: tc * 130 + hh * 65 + 65],
                                    esb[:, 0:j0], start=st, stop=sp),
                                    wait=w, inc=(s_av, 1))
                                st_flag = False
                            if masked and not _osmod.environ.get("SKIPMKB"):
                                w2 = [(s_mk, s_mk.v)]
                                if st_flag:
                                    w2.append((s_ep, s_ep.v))
                                R.op("tensor", lambda e, vtb=vtb, hh=hh, tc=tc, bank=bank, j0=j0,
                                     st=st_flag, sp=stop_flag: e.matmul(
                                    cx.pb[bank][0:65, j0:j0 + 128],
                                    vtb[:, tc * 130 + hh * 65: tc * 130 + hh * 65 + 65],
                                    t["MKB"][:], start=st, stop=sp),
                                    wait=w2, inc=(s_av, 1))
                            es_free[par] = s_av.v
                            es_i += 1
                R.op("tensor", None, inc=(s_ku, 1))
                it += 1
            for bank in range(4):
                R.op("tensor", lambda e, bank=bank: e.matmul(
                    cx.pb[bank][0:65, 0:512],
                    (t["VT0"] if it % 2 == 1 else t["VT1"])[:, 0:65],
                    t["ZB5"][:], start=False, stop=True), inc=(s_av, 1))
            for hh in range(2):
                h = 2 * g + hh
                for sh in range(2):
                    R.op("scalar", lambda e, hh=hh, sh=sh: e.copy(
                        out=t["AVSB"][0:65, 0:512], in_=cx.pb[2 * hh + sh][0:65, 0:512]),
                        wait=[(s_av, s_av.v), (s_ept, s_ept.v)], inc=(s_ep, 1))
                    if dbg is not None and "x1banks" in dbg and g == 0:
                        r0 = (ei * 2 + hh) * 128
                        R.op("sync", lambda e, sh=sh, r0=r0: e.dma_start(
                            out=dbg["x1banks"][r0:r0 + 128, sh * 512:(sh + 1) * 512],
                            in_=t["AVSB"][0:128, 0:512]),
                            wait=[(s_ep, s_ep.v)], inc=(s_ini, 16))
                    for s4 in range(4):
                        slot = sh * 4 + s4
                        w = [(s_ep, s_ep.v)] if s4 == 0 else []
                        if s4 == 0 and sh == 0:
                            w.append((s_stt, s_stt.v))
                        R.op("tensor", lambda e, s4=s4, slot=slot: e.transpose(
                            cx.pt[0][:, slot * 128: slot * 128 + 65],
                            t["AVSB"][0:65, s4 * 128:(s4 + 1) * 128],
                            t["CONSTB"][0:65, 0:65]), wait=w, inc=(s_ept, 1))
                for slot in range(8):
                    wep = [(s_ept, s_ept.v), (s_stt, s_stt.v)] if slot == 0 else []
                    R.op("vector", lambda e, slot=slot: e.tensor_copy(
                        out=t["EPI"][:, slot * 66: slot * 66 + 65],
                        in_=cx.pt[0][:, slot * 128: slot * 128 + 65]),
                        wait=wep, inc=(s_ep, 1) if slot == 7 else None)
                for slot in range(8):
                    tb = ei * 8 + slot
                    w = [(s_ep, s_ep.v)] if slot == 0 else []
                    if g == 0 and ei == 0 and hh == 0 and slot == 0:
                        w += list(dep_x)
                    if dbg is not None:
                        R.op("scalar", lambda e, slot=slot, tb=tb, h=h: e.copy(
                            out=t["ZS"][:, h * 16 + tb: h * 16 + tb + 1],
                            in_=cx.pt[0][:, slot * 128 + 64: slot * 128 + 65]),
                            wait=[(s_ept, s_ept.v)])
                    R.op("vector", lambda e, slot=slot: e.reciprocal(
                        t["STATS"][:, 160:161], t["EPI"][:, slot * 66 + 64: slot * 66 + 65]),
                        wait=w)
                    if _osmod.environ.get("DBGZ"):
                        R.op("vector", lambda e, slot=slot, tb=tb, h=h: e.tensor_copy(
                            out=X[:, tb * 1024 + h * 64: tb * 1024 + h * 64 + 65],
                            in_=t["EPI"][:, slot * 66: slot * 66 + 65]), inc=(s_stt, 1))
                    elif _osmod.environ.get("DBGATT"):
                        R.op("vector", lambda e, slot=slot, tb=tb, h=h: e.tensor_scalar(
                            out=X[:, tb * 1024 + h * 64: tb * 1024 + h * 64 + 64],
                            in0=t["EPI"][:, slot * 66: slot * 66 + 64],
                            scalar1=t["STATS"][:, 160:161], scalar2=None,
                            op0=ALU.mult), inc=(s_stt, 1))
                    else:
                        R.op("vector", lambda e, slot=slot, tb=tb, h=h: e.scalar_tensor_tensor(
                            out=X[:, tb * 1024 + h * 64: tb * 1024 + h * 64 + 64],
                            in0=t["EPI"][:, slot * 66: slot * 66 + 64],
                            scalar=t["STATS"][:, 160:161],
                            in1=X[:, tb * 1024 + h * 64: tb * 1024 + h * 64 + 64],
                            op0=ALU.mult, op1=ALU.add), inc=(s_stt, 1))
            started = {k for k in started if k[0] != g or k[1] != ei}
    return s_stt


def emit_moe(cx, dram, dep):
    R, t = cx.R, cx.t
    X, XN = t["X"], t["XN"]
    s = R.sem("moe")
    R.op("gpsimd", lambda e: e.dma_start(
        out=t["WG"][:, 0:64].rearrange("p (kc c) -> p kc c", kc=8),
        in_=dram["WgP"].rearrange("(kc p) c -> p kc c", p=128)), wait=dep, inc=(s, 16))
    R.op("gpsimd", lambda e: e.dma_start(out=t["BG1"][:], in_=dram["bgP"]), inc=(s, 16))
    SG = t["SG"]
    WC = t["KVA"]
    for tb in range(NTB):
        for kc in range(8):
            R.op("tensor", lambda e, tb=tb, kc=kc: e.matmul(
                cx.pb[0][:, 0:8],
                XN[:, kc * NT + tb * 128: kc * NT + tb * 128 + 128],
                t["WG"][:, kc * 8:(kc + 1) * 8], start=(kc == 0), stop=False))
        R.op("tensor", lambda e: e.matmul(
            cx.pb[0][:, 0:8], t["ONE1"][0:1, 0:128], t["BG1"][:],
            start=False, stop=True))
        R.op("vector", lambda e: e.tensor_copy(out=SG[:, 0:8], in_=cx.pb[0][:, 0:8]))
        R.op("vector", lambda e: e.reduce_max(SG[:, 16:17], SG[:, 0:8], axis=AX.X, negate=True))
        R.op("scalar", lambda e: e.activation(out=SG[:, 8:16], in_=SG[:, 0:8],
                                              func=AF.Exp, bias=SG[:, 16:17]))
        R.op("vector", lambda e: e.tensor_scalar(out=SG[:, 0:8], in0=SG[:, 8:16],
                                                 scalar1=0.99999, scalar2=None, op0=ALU.is_ge))
        R.op("vector", lambda e: e.tensor_tensor(out=SG[:, 16:24], in0=SG[:, 8:16],
                                                 in1=SG[:, 0:8], op=ALU.subtract))
        R.op("vector", lambda e: e.reduce_max(SG[:, 17:18] if False else t["NEGD"][:, 0:1],
                                              SG[:, 16:24], axis=AX.X))
        R.op("vector", lambda e: e.tensor_scalar(out=SG[:, 16:24], in0=SG[:, 8:16],
                                                 scalar1=t["NEGD"][:, 0:1], scalar2=None,
                                                 op0=ALU.is_ge))
        R.op("vector", lambda e: e.tensor_tensor(out=SG[:, 8:16], in0=SG[:, 8:16],
                                                 in1=SG[:, 16:24], op=ALU.mult))
        R.op("vector", lambda e: e.reduce_sum(t["NEGD"][:, 1:2], SG[:, 8:16], axis=AX.X))
        R.op("vector", lambda e: e.reciprocal(t["NEGD"][:, 2:3], t["NEGD"][:, 1:2]))
        R.op("vector", lambda e, tb=tb: e.tensor_scalar(out=WC[:, tb * 8:(tb + 1) * 8],
                                                        in0=SG[:, 8:16],
                                                        scalar1=t["NEGD"][:, 2:3], scalar2=None,
                                                        op0=ALU.mult))
    for ex in range(8):
        R.op("gpsimd", lambda e, ex=ex: e.dma_start(
            out=t["WEB"].rearrange("p (kc c) -> p kc c", kc=8),
            in_=dram["WeP"][ex].rearrange("(kc p) c -> p kc c", p=128)), inc=(s, 16))
        R.op("gpsimd", lambda e, ex=ex: e.dma_start(
            out=t["BR"][0:1, 0:1024], in_=dram["beP"][ex]), inc=(s, 16))
        for tb in range(NTB):
            for half in range(2):
                pbk = cx.pb[half]
                for kc in range(8):
                    R.op("tensor", lambda e, tb=tb, kc=kc, half=half, pbk=pbk: e.matmul(
                        pbk[:, 0:512],
                        XN[:, kc * NT + tb * 128: kc * NT + tb * 128 + 128],
                        t["WEB"][:, kc * 1024 + half * 512: kc * 1024 + half * 512 + 512],
                        start=(kc == 0), stop=False))
                R.op("tensor", lambda e, half=half, pbk=pbk: e.matmul(
                    pbk[:, 0:512], t["ONE1"][0:1, 0:128],
                    t["BR"][0:1, half * 512:(half + 1) * 512], start=False, stop=True))
                R.op("vector", lambda e, tb=tb, half=half, ex=ex, pbk=pbk: e.scalar_tensor_tensor(
                    out=X[:, tb * 1024 + half * 512: tb * 1024 + half * 512 + 512],
                    in0=pbk[:, 0:512], scalar=WC[:, tb * 8 + ex: tb * 8 + ex + 1],
                    in1=X[:, tb * 1024 + half * 512: tb * 1024 + half * 512 + 512],
                    op0=ALU.mult, op1=ALU.add), inc=(s, 1))
    return s


def emit_perf_a(cx, dram, outs, dep):
    R, t = cx.R, cx.t
    X, XN = t["X"], t["XN"]
    s = R.sem("pa")
    s_q, m_q = _mm_proj_pairs(cx, dram["W1qP"], 8, 128, t["QD"], lambda pr: None,
                              list(dep), "paq", dep_w=list(dep))
    R.op("sync", lambda e: e.dma_start(
        out=outs["QD1"].rearrange("(pr p) c -> p pr c", p=128),
        in_=t["QD"].rearrange("p (pr c) -> p pr c", pr=8)), inc=(s, 16))
    s_k, m_k, s_kd = emit_k_proj(cx, dram["W1kP"], None, outs["KD1"], list(dep), [], [], "pak")
    R.op("gpsimd", lambda e: e.memset(
        t["VAP"].rearrange("p (tb h c) -> p tb h c", tb=16, h=16)[:, :, :, 64:65], 1.0))
    for kc in range(8):
        R.op("gpsimd", lambda e, kc=kc: e.dma_start(
            out=t["W0"][:, 0:1024], in_=dram["W1v"][kc * 128:(kc + 1) * 128, :]), inc=(s, 16))
        for tb in range(NTB):
            for half in range(2):
                R.op("tensor", lambda e, kc=kc, tb=tb, half=half: e.matmul(
                    cx.pb[2 + half][:, 0:512],
                    XN[:, kc * NT + tb * 128: kc * NT + tb * 128 + 128],
                    t["W0"][:, half * 512:(half + 1) * 512],
                    start=(kc == 0), stop=(kc == 7)))
                if kc == 7:
                    R.op("scalar", lambda e, tb=tb, half=half: e.copy(
                        out=t["VAP"][:, tb * 1040 + half * 520: tb * 1040 + (half + 1) * 520]
                        .rearrange("p (h c) -> p h c", h=8)[:, :, 0:64],
                        in_=cx.pb[2 + half][:, 0:512].rearrange("p (h c) -> p h c", h=8)))
    R.op("sync", lambda e: e.dma_start(
        out=outs["VA1"].rearrange("(tb p) c -> p tb c", p=128),
        in_=t["VAP"].rearrange("p (tb c) -> p tb c", tb=16)), inc=(s, 16))
    for h in range(16):
        pr, hh = h // 2, h % 2
        R.op("gpsimd", lambda e, h=h: e.dma_start(out=t["PM0"][:], in_=dram["pmT"][h]), inc=(s, 16))
        R.op("scalar", lambda e: e.mul(out=t["PM1"][:], in_=t["PM0"][:], mul=0.125))
        for tb in range(NTB):
            qsl0 = t["QD"][hh * 64:hh * 64 + 64, pr * NT + tb * 128: pr * NT + tb * 128 + 128]
            R.op("vector", lambda e, qsl0=qsl0: e.tensor_copy(out=t["KSTR1"][0:64, 0:128], in_=qsl0))
            qsl = t["KSTR1"][0:64, 0:128]
            ksl_dram = outs["KD1"][pr * 128 + hh * 64: pr * 128 + hh * 64 + 64,
                                   tb * 128:(tb + 1) * 128]
            R.op("sync", lambda e, ksl_dram=ksl_dram: e.dma_start(
                out=t["KSTR0"][0:64, 0:128], in_=ksl_dram), inc=(s, 16))
            R.op("tensor", lambda e: e.matmul(cx.pb[0][:, 0:128], t["KSTR0"][0:64, 0:128],
                                              t["PM1"][:], start=True, stop=True))
            R.op("vector", lambda e: e.reduce_max(t["NEGD"][:, 3:4], cx.pb[0][:, 0:128], axis=AX.X))
            if tb == 0:
                R.op("vector", lambda e, h=h: e.tensor_copy(out=t["MX"][:, h % 16:h % 16 + 1],
                                                            in_=t["NEGD"][:, 3:4]))
            else:
                R.op("vector", lambda e, h=h: e.tensor_tensor(
                    out=t["MX"][:, h % 16:h % 16 + 1], in0=t["MX"][:, h % 16:h % 16 + 1],
                    in1=t["NEGD"][:, 3:4], op=ALU.max))
            R.op("tensor", lambda e: e.transpose(cx.pt[0][:, 0:64], t["KSTR0"][0:64, 0:128],
                                                 t["CONSTB"][0:64, 0:64]))
            R.op("vector", lambda e: e.tensor_copy(out=t["KT64"][:], in_=cx.pt[0][:, 0:64]))
            R.op("scalar", lambda e: e.activation(out=t["K264"][:], in_=t["KT64"][:], func=AF.Square))
            R.op("vector", lambda e: e.reduce_sum(t["NEGD"][:, 4:5], t["K264"][:], axis=AX.X))
            R.op("vector", lambda e, h=h, tb=tb: e.tensor_scalar(
                out=t["NEGD"][:, 16 + h:17 + h], in0=t["NEGD"][:, 4:5],
                scalar1=0.5, scalar2=None, op0=ALU.mult))
            R.op("tensor", lambda e, qsl=qsl: e.matmul(cx.pb[1][:, 0:128], qsl, t["PM1"][:],
                                                       start=True, stop=True))
            R.op("vector", lambda e: e.reduce_max(t["NEGD"][:, 5:6], cx.pb[1][:, 0:128],
                                                  axis=AX.X, negate=True))
            R.op("tensor", lambda e, qsl=qsl: e.transpose(cx.pt[0][:, 0:64], qsl,
                                                          t["CONSTB"][0:64, 0:64]))
            R.op("vector", lambda e: e.tensor_copy(out=t["KT64"][:], in_=cx.pt[0][:, 0:64]))
            R.op("scalar", lambda e: e.activation(out=t["K264"][:], in_=t["KT64"][:], func=AF.Square))
            R.op("vector", lambda e: e.reduce_sum(t["NEGD"][:, 6:7], t["K264"][:], axis=AX.X))
            R.op("vector", lambda e: e.scalar_tensor_tensor(
                out=t["NEGD"][:, 7:8], in0=t["NEGD"][:, 6:7], scalar=-0.5,
                in1=t["NEGD"][:, 5:6], op0=ALU.mult, op1=ALU.add))
            R.op("vector", lambda e, h=h: e.tensor_scalar(
                out=t["SG"][:, h:h + 1], in0=t["NEGD"][:, 7:8],
                scalar1=1.0, scalar2=-LN16, op0=ALU.mult, op1=ALU.add))
            if h == 15:
                R.op("sync", lambda e, tb=tb: e.dma_start(
                    out=outs["NEGDQ"][tb * 128:(tb + 1) * 128, :], in_=t["SG"][:, 0:16]),
                    inc=(s, 16))
                R.op("sync", lambda e, tb=tb: e.dma_start(
                    out=outs["DIAGK"][tb * 128:(tb + 1) * 128, :], in_=t["NEGD"][:, 16:32]),
                    inc=(s, 16))
    R.op("vector", lambda e: e.tensor_copy(out=t["MKB"][:, 0:16], in_=t["MX"][:, 0:16]))
    R.op("tensor", lambda e: e.transpose(cx.pt[0][0:16, 0:128], t["MKB"][:, 0:16],
                                         t["CONSTB"][:, 0:128]))
    R.op("vector", lambda e: e.tensor_copy(out=t["SR16"][0:16, 0:128], in_=cx.pt[0][0:16, 0:128]))
    R.op("vector", lambda e: e.reduce_max(t["ST16"][0:16, 0:1], t["SR16"][0:16, 0:128], axis=AX.X))
    R.op("vector", lambda e: e.tensor_copy(out=t["MKB"][0:16, 16:17], in_=t["ST16"][0:16, 0:1]))
    R.op("tensor", lambda e: e.transpose(cx.pt[0][0:1, 0:16], t["MKB"][0:16, 16:17],
                                         t["CONSTB"][0:16, 0:16]))
    R.op("vector", lambda e: e.tensor_copy(out=t["NEGD"][0:1, 40:41] if False else t["STATS"][0:1, 0:16], in_=cx.pt[0][0:1, 0:16]))
    R.op("sync", lambda e: e.dma_start(out=outs["STABL"], in_=t["STATS"][0:1, 0:16]), inc=(s, 16))
    return s


def emit_perf_b(cx, dram, ins_d, outs, dep):
    R, t = cx.R, cx.t
    s = R.sem("pb_")
    if ins_d.get("STABG_pre") is not None:
        R.op("sync", lambda e: e.dma_start(out=t["STATS"][0:1, 96:112],
                                           in_=ins_d["STABG_pre"]), wait=dep, inc=(s, 16))
    else:
        R.op("sync", lambda e: e.dma_start(out=t["STATS"][0:1, 0:64], in_=ins_d["STABG"]),
             wait=dep, inc=(s, 16))
        R.op("vector", lambda e: e.tensor_tensor(out=t["STATS"][0:1, 64:80],
                                                 in0=t["STATS"][0:1, 0:16],
                                                 in1=t["STATS"][0:1, 16:32], op=ALU.max))
        R.op("vector", lambda e: e.tensor_tensor(out=t["STATS"][0:1, 80:96],
                                                 in0=t["STATS"][0:1, 32:48],
                                                 in1=t["STATS"][0:1, 48:64], op=ALU.max))
        R.op("vector", lambda e: e.tensor_tensor(out=t["STATS"][0:1, 96:112],
                                                 in0=t["STATS"][0:1, 64:80],
                                                 in1=t["STATS"][0:1, 80:96], op=ALU.max))
    R.op("vector", lambda e: e.tensor_copy(out=t["BR"][0:1, 0:16], in_=t["STATS"][0:1, 96:112]))
    for h in range(16):
        R.op("tensor", lambda e, h=h: e.matmul(cx.pb[2][:, h:h + 1], t["ONE1"][0:1, 0:128],
                                               t["BR"][0:1, h:h + 1], start=True, stop=True))
    R.op("vector", lambda e: e.tensor_copy(out=t["NEGD"][:, 0:16], in_=cx.pb[2][:, 0:16]))
    R.op("sync", lambda e: e.dma_start(
        out=t["VAP"].rearrange("p (tb c) -> p tb c", tb=16),
        in_=ins_d["VA1o"].rearrange("(tb p) c -> p tb c", p=128)), inc=(s, 16))
    R.op("sync", lambda e: e.dma_start(
        out=t["KVB"][:, 0:256].rearrange("p (tb c) -> p tb c", tb=16),
        in_=ins_d["DIAGKo"].rearrange("(tb p) c -> p tb c", p=128)), inc=(s, 16))
    for h in range(16):
        pr, hh = h // 2, h % 2
        R.op("gpsimd", lambda e, h=h: e.dma_start(out=t["PM0"][:], in_=dram["pmT"][h]), inc=(s, 16))
        R.op("scalar", lambda e: e.mul(out=t["PM1"][:], in_=t["PM0"][:], mul=0.125))
        R.op("sync", lambda e, pr=pr, hh=hh: e.dma_start(
            out=t["KSTR0"][0:64, 0:2048],
            in_=ins_d["KD1o"][pr * 128 + hh * 64: pr * 128 + hh * 64 + 64, :]), inc=(s, 16))
        for tb in range(NTB):
            R.op("vector", lambda e, h=h, tb=tb: e.tensor_tensor(
                out=t["NEGD"][:, 33:34], in0=t["KVB"][:, tb * 16 + h: tb * 16 + h + 1],
                in1=t["NEGD"][:, h:h + 1], op=ALU.add))
            R.op("vector", lambda e: e.tensor_scalar(out=t["NEGD"][:, 34:35], in0=t["NEGD"][:, 33:34],
                                                     scalar1=-1.0, scalar2=-LN16,
                                                     op0=ALU.mult, op1=ALU.add))
            R.op("tensor", lambda e, tb=tb: e.matmul(
                cx.pb[0][:, 0:128], t["KSTR0"][0:64, tb * 128:(tb + 1) * 128],
                t["PM1"][:], start=True, stop=True))
            R.op("scalar", lambda e: e.activation(out=t["KF"][:, 0:128], in_=cx.pb[0][:, 0:128],
                                                  func=AF.Exp, bias=t["NEGD"][:, 34:35], scale=1.0))
            R.op("scalar", lambda e: e.activation(out=t["KF"][:, 128:256], in_=cx.pb[0][:, 0:128],
                                                  func=AF.Exp, bias=t["NEGD"][:, 34:35], scale=-1.0))
            for sgn in range(2):
                R.op("tensor", lambda e, tb=tb, sgn=sgn, h=h: e.matmul(
                    cx.pb[2 + sgn][0:128, 0:65],
                    t["KF"][:, sgn * 128:(sgn + 1) * 128],
                    t["VAP"][:, tb * 1040 + h * 65: tb * 1040 + h * 65 + 65],
                    start=(tb == 0), stop=(tb == NTB - 1)))
        for sgn in range(2):
            R.op("vector", lambda e, sgn=sgn: e.tensor_copy(out=t["KVS"][:, sgn * 65:(sgn + 1) * 65],
                                                            in_=cx.pb[2 + sgn][0:128, 0:65]))
            R.op("sync", lambda e, h=h, sgn=sgn: e.dma_start(
                out=outs["KVL"][h * 256 + sgn * 128: h * 256 + sgn * 128 + 128, :],
                in_=t["KVS"][:, sgn * 65:(sgn + 1) * 65]), inc=(s, 16))
    return s


def emit_perf_c(cx, dram, ins_d, dep):
    R, t = cx.R, cx.t
    X = t["X"]
    s = R.sem("pc_")
    if ins_d.get("KVG_pre") is not None:
        R.op("sync", lambda e: e.dma_start(
            out=t["KVA"].rearrange("p (g c) -> p g c", g=32),
            in_=ins_d["KVG_pre"].rearrange("(g p) c -> p g c", p=128)), wait=dep, inc=(s, 16))
    else:
        R.op("sync", lambda e: e.dma_start(
            out=t["KVA"].rearrange("p (g c) -> p g c", g=32),
            in_=ins_d["KV4"][0].rearrange("(g p) c -> p g c", p=128)), wait=dep, inc=(s, 16))
        for r in range(1, 4):
            R.op("sync", lambda e, r=r: e.dma_start(
                out=t["KVB"].rearrange("p (g c) -> p g c", g=32),
                in_=ins_d["KV4"][r].rearrange("(g p) c -> p g c", p=128)), inc=(s, 16))
            R.op("vector", lambda e: e.tensor_tensor(out=t["KVA"][:], in0=t["KVA"][:],
                                                     in1=t["KVB"][:], op=ALU.add))
    R.op("vector", lambda e: e.tensor_copy(out=t["KSTR1"][:, 0:2015], in_=t["KVA"][:, 0:2015]))
    R.op("vector", lambda e: e.tensor_copy(out=t["MKB"][:, 0:65], in_=t["KVA"][:, 2015:2080]))
    R.op("sync", lambda e: e.dma_start(
        out=t["QD"].rearrange("p (pr c) -> p pr c", pr=8),
        in_=ins_d["QD1o"].rearrange("(pr p) c -> p pr c", p=128)), inc=(s, 16))
    R.op("sync", lambda e: e.dma_start(
        out=t["KVB"][:, 0:256].rearrange("p (tb c) -> p tb c", tb=16),
        in_=ins_d["NEGDQo"].rearrange("(tb p) c -> p tb c", p=128)), inc=(s, 16))
    for h in range(16):
        pr, hh = h // 2, h % 2
        R.op("gpsimd", lambda e, h=h: e.dma_start(out=t["PM0"][:], in_=dram["pmT"][h]), inc=(s, 16))
        R.op("scalar", lambda e: e.mul(out=t["PM1"][:], in_=t["PM0"][:], mul=0.125))
        for tb in range(NTB):
            R.op("vector", lambda e, tb=tb, h=h: e.tensor_copy(
                out=t["NEGD"][:, 0:1], in_=t["KVB"][:, tb * 16 + h: tb * 16 + h + 1]))
            qsl0 = t["QD"][hh * 64:hh * 64 + 64, pr * NT + tb * 128: pr * NT + tb * 128 + 128]
            R.op("vector", lambda e, qsl0=qsl0: e.tensor_copy(out=t["QFT"][0:64, 0:128], in_=qsl0))
            qsl = t["QFT"][0:64, 0:128]
            R.op("tensor", lambda e, qsl=qsl: e.matmul(cx.pb[0][:, 0:128], qsl, t["PM1"][:],
                                                       start=True, stop=True))
            R.op("scalar", lambda e: e.activation(out=t["KF"][:, 0:128], in_=cx.pb[0][:, 0:128],
                                                  func=AF.Exp, bias=t["NEGD"][:, 0:1], scale=1.0))
            R.op("scalar", lambda e: e.activation(out=t["KF"][:, 128:256], in_=cx.pb[0][:, 0:128],
                                                  func=AF.Exp, bias=t["NEGD"][:, 0:1], scale=-1.0))
            for sgn in range(2):
                R.op("tensor", lambda e, sgn=sgn: e.transpose(
                    cx.pt[0][:, sgn * 128:(sgn + 1) * 128],
                    t["KF"][:, sgn * 128:(sgn + 1) * 128], t["CONSTB"][:, 0:128]))
            R.op("vector", lambda e: e.tensor_copy(out=t["QFT"][:], in_=cx.pt[0][:, 0:256]))
            for sgn in range(2):
                g_ = h * 2 + sgn
                if g_ < 31:
                    rhs = t["KSTR1"][:, g_ * 65: g_ * 65 + 65]
                else:
                    rhs = t["MKB"][:, 0:65]
                R.op("tensor", lambda e, sgn=sgn, rhs=rhs: e.matmul(
                    cx.pb[1][:, 0:65], t["QFT"][:, sgn * 128:(sgn + 1) * 128], rhs,
                    start=(sgn == 0), stop=(sgn == 1)))
            R.op("vector", lambda e: e.tensor_copy(out=t["NEGD"][:, 1:2], in_=cx.pb[1][:, 64:65]))
            R.op("vector", lambda e: e.tensor_scalar(out=t["NEGD"][:, 2:3], in0=t["NEGD"][:, 1:2],
                                                     scalar1=1.0, scalar2=EPS_PERF,
                                                     op0=ALU.mult, op1=ALU.add))
            R.op("vector", lambda e: e.reciprocal(t["NEGD"][:, 3:4], t["NEGD"][:, 2:3]))
            R.op("vector", lambda e: e.tensor_scalar(out=t["ATN"][:], in0=cx.pb[1][:, 0:64],
                                                     scalar1=t["NEGD"][:, 3:4], scalar2=None,
                                                     op0=ALU.mult))
            R.op("tensor", lambda e: e.transpose(cx.pt[0][:, 0:128] if False else cx.pt[0][0:64, 256:384],
                                                 t["ATN"][:], t["CONSTB"][:, 0:128]))
            R.op("vector", lambda e, h=h, tb=tb, hh=hh, pr=pr: e.tensor_copy(
                out=att_alias(cx)[hh * 64:hh * 64 + 64, pr * 2048 + tb * 128: pr * 2048 + tb * 128 + 128],
                in_=cx.pt[0][0:64, 256:384]))
    R.op("gpsimd", lambda e: e.dma_start(
        out=t["WEB"].rearrange("p (c n) -> p c n", c=8),
        in_=dram["W1o"].rearrange("(c p) n -> p c n", p=128)), inc=(s, 16))
    ATT = att_alias(cx)
    for tb in range(NTB):
        for half in range(2):
            for c in range(8):
                R.op("tensor", lambda e, tb=tb, half=half, c=c: e.matmul(
                    cx.pb[2][:, 0:512],
                    ATT[:, c * 2048 + tb * 128: c * 2048 + tb * 128 + 128],
                    t["WEB"][:, c * 1024 + half * 512: c * 1024 + half * 512 + 512],
                    start=(c == 0), stop=(c == 7)))
            R.op("vector", lambda e, tb=tb, half=half: e.tensor_tensor(
                out=X[:, tb * 1024 + half * 512: tb * 1024 + half * 512 + 512],
                in0=X[:, tb * 1024 + half * 512: tb * 1024 + half * 512 + 512],
                in1=cx.pb[2][:, 0:512], op=ALU.add), inc=(s, 1))
    return s


def emit_ffn(cx, dram, out_dram, dep):
    R, t = cx.R, cx.t
    X, XN = t["X"], t["XN"]
    s = R.sem("ff")
    for oc in range(8):
        R.op("gpsimd", lambda e, oc=oc: e.dma_start(
            out=t["W0"][:, 0:1024], in_=dram["Wf"][oc * 128:(oc + 1) * 128, :]),
            wait=dep, inc=(s, 16))
        R.op("scalar", lambda e: e.sign(out=t["W1"][:, 0:1024], in_=t["W0"][:, 0:1024]))
        for ic in range(8):
            R.op("tensor", lambda e, ic=ic: e.transpose(
                cx.pt[0][:, 0:128], t["W1"][:, ic * 128:(ic + 1) * 128], t["CONSTB"][:, 0:128]))
            R.op("vector", lambda e, ic=ic, oc=oc: e.tensor_copy(
                out=t["WEB"][:, ic * 1024 + oc * 128: ic * 1024 + oc * 128 + 128],
                in_=cx.pt[0][:, 0:128]))
    for tb in range(NTB):
        for half in range(2):
            for ic in range(8):
                R.op("tensor", lambda e, tb=tb, half=half, ic=ic: e.matmul(
                    cx.pb[0][:, 0:512],
                    XN[:, ic * NT + tb * 128: ic * NT + tb * 128 + 128],
                    t["WEB"][:, ic * 1024 + half * 512: ic * 1024 + half * 512 + 512],
                    start=(ic == 0), stop=(ic == 7)))
            osb = t["OS0"] if tb % 2 == 0 else t["OS1"]
            R.op("vector", lambda e, tb=tb, half=half, osb=osb: e.tensor_tensor(
                out=osb[:, half * 512:(half + 1) * 512],
                in0=X[:, tb * 1024 + half * 512: tb * 1024 + half * 512 + 512],
                in1=cx.pb[0][:, 0:512], op=ALU.add))
        R.op("sync", lambda e, tb=tb, osb=osb: e.dma_start(
            out=out_dram[tb * 128:(tb + 1) * 128, :], in_=osb[:]), inc=(s, 16))
    return s


def np_ln(x):
    m = x.mean(-1, keepdims=True)
    v = x.var(-1, keepdims=True)
    return (x - m) / np.sqrt(v + EPS_LN)


NPDT = {F32: np.float32, BF16: BF}


def declare_inputs(nc, arrs, names):
    d = {}
    for n in names:
        a = arrs[n]
        dt = BF16 if a.dtype == BF else F32
        d[n] = nc.dram_tensor(n, list(a.shape), dt, kind="ExternalInput").ap()
    return d


def emit_store(cx, dram_out, sbuf_ap, wait, s_st):
    cx.R.op("sync", lambda e: e.dma_start(out=dram_out, in_=sbuf_ap), wait=wait, inc=(s_st, 16))


SHP = {"WqP": [8, 128, 8, 128], "WkP": [8, 128, 8, 128], "WvdP": [4, 128, 8, 64],
       "bqP": [128, 8], "bkP": [128, 8], "bvdP": [64, 4],
       "identb": [128, 128], "tri": [128, 128],
       "Wvu2": [8, 32, 128], "bvu2": [8, 1, 128],
       "WeP": [8, 1024, 1024], "beP": [8, 1, 1024], "WgP": [1024, 8], "bgP": [1, 8],
       "W1qP": [8, 128, 8, 128], "W1kP": [8, 128, 8, 128], "W1v": [1024, 1024],
       "pmT": [16, 64, 128], "W1o": [1024, 1024], "Wf": [1024, 1024]}
BF16_CONSTS = {"identb", "tri"}


def _decl(nc, names):
    return {n: nc.dram_tensor(n, SHP[n], BF16 if n in BF16_CONSTS else F32,
                              kind="ExternalInput").ap() for n in names}


def _fin(R, sems):
    R.op("sync", None, wait=[(s, s.v) for s in sems])


def build_L1():
    from contextlib import ExitStack
    nc = bass.Bass("TRN2", target_bir_lowering=False, debug=False)
    with ExitStack() as stack:
        xd = nc.dram_tensor("x", [NT, 1024], F32, kind="ExternalInput").ap()
        dram = _decl(nc, ["WkP", "WvdP", "bqP", "bkP", "bvdP", "identb", "tri"])
        oKD = nc.dram_tensor("oKD", [1024, NT], BF16, kind="ExternalOutput").ap()
        oVD = nc.dram_tensor("oVD", [256, NT], BF16, kind="ExternalOutput").ap()
        cx = alloc_common(nc, stack)
        R = cx.R
        ldc = emit_consts(cx, dram)
        ldx = emit_load_x(cx, xd)
        s_ev = emit_ln(cx, [(ldx, 16), (ldc, ldc.v)])
        s_k, m_k, s_kdma = emit_k_proj(cx, dram["WkP"], 8, oKD, [(s_ev, NTB)],
                                       [(ldc, ldc.v)], [(s_ev, NTB)], "p0k")
        s_v, m_v, s_vdma = emit_l0_vd(cx, dram, oVD, [(s_ev, NTB)], [(ldc, ldc.v)],
                                      [(m_k, 16)])
        _fin(R, [s_kdma, s_vdma])
        R.flush(nc, stack)
    return nc


def build_L2(m):
    from contextlib import ExitStack
    nc = bass.Bass("TRN2", target_bir_lowering=False, debug=False)
    with ExitStack() as stack:
        xd = nc.dram_tensor("x", [NT, 1024], F32, kind="ExternalInput").ap()
        dram = _decl(nc, ["WqP", "bqP", "bkP", "bvdP", "identb", "tri", "Wvu2", "bvu2"])
        KGd = nc.dram_tensor("KG", [1024, 8192], BF16, kind="ExternalInput").ap()
        VDGd = nc.dram_tensor("VDG", [256, 8192], BF16, kind="ExternalInput").ap()
        oX1 = nc.dram_tensor("oX1", [NT, 1024], BF16, kind="ExternalOutput").ap()
        cx = alloc_common(nc, stack)
        R = cx.R
        ldc = emit_consts(cx, dram)
        ldx = emit_load_x(cx, xd)
        s_ev = emit_ln(cx, [(ldx, 16), (ldc, ldc.v)])
        s_q, m_q = emit_l0_proj(cx, dram, [(s_ev, NTB)], [(ldc, ldc.v)])
        kg_fn = lambda g, ke: KGd[g * 128:(g + 1) * 128, ke * 1024:(ke + 1) * 1024]
        vdg_fn = lambda g, ke: VDGd[(g // 2) * 64 + (g % 2) * 32:(g // 2) * 64 + (g % 2) * 32 + 32,
                                    ke * 1024:(ke + 1) * 1024]
        s_stt = emit_l0_attn(cx, dram, kg_fn, vdg_fn, [m, 7 - m],
                             dep_q=[(s_q, s_q.v)], dep_kv=[], dep_x=[(s_ev, NTB)],
                             dep_pb=[(s_q, s_q.v)], dep_c=[(ldc, ldc.v)])
        st = R.sem("st2")
        R.op("sync", lambda e: e.dma_start(
            out=oX1.rearrange("(tb p) c -> p tb c", p=128),
            in_=cx.t["X"].rearrange("p (tb c) -> p tb c", tb=NTB)),
            wait=[(s_stt, s_stt.v)], inc=(st, 16))
        _fin(R, [st])
        R.flush(nc, stack)
    return nc


def build_L3():
    from contextlib import ExitStack
    nc = bass.Bass("TRN2", target_bir_lowering=False, debug=False)
    with ExitStack() as stack:
        xd = nc.dram_tensor("x", [NT, 1024], BF16, kind="ExternalInput").ap()
        dram = _decl(nc, ["WeP", "beP", "WgP", "bgP", "W1qP", "W1kP", "W1v", "pmT",
                          "W1o", "Wf", "identb", "tri", "bqP", "bkP", "bvdP"])
        oOut = nc.dram_tensor("oOut", [NT, 1024], F32, kind="ExternalOutput").ap()
        iQD1 = nc.dram_tensor("iQD1", [1024, NT], BF16).ap()
        iKD1 = nc.dram_tensor("iKD1", [1024, NT], BF16).ap()
        iVA1 = nc.dram_tensor("iVA1", [NT, 1040], BF16).ap()
        iNEGDQ = nc.dram_tensor("iNEGDQ", [NT, 16], F32).ap()
        iDIAGK = nc.dram_tensor("iDIAGK", [NT, 16], F32).ap()
        iSTABL = nc.dram_tensor("iSTABL", [1, 16], F32).ap()
        iSTABG = nc.dram_tensor("iSTABG", [1, 16], F32).ap()
        iKVL = nc.dram_tensor("iKVL", [4096, 65], F32).ap()
        iKVG = nc.dram_tensor("iKVG", [4096, 65], F32).ap()
        cx = alloc_common(nc, stack)
        R = cx.R
        ldc = emit_consts(cx, dram)
        ldx = emit_load_x(cx, xd, cast=False)
        s_ev = emit_ln(cx, [(ldx, 16), (ldc, ldc.v)])
        s_m = emit_moe(cx, dram, [(s_ev, NTB)])
        s_ev2 = emit_ln(cx, [(s_m, s_m.v)], tag="ln2")
        outs_a = {"QD1": iQD1, "KD1": iKD1, "VA1": iVA1, "NEGDQ": iNEGDQ,
                  "DIAGK": iDIAGK, "STABL": iSTABL}
        s_a = emit_perf_a(cx, dram, outs_a, [(s_ev2, NTB)])
        cc = R.sem("cc")
        R.op("gpsimd", lambda e: e.collective_compute(
            "AllReduce", mybir.AluOpType.max, replica_groups=RG,
            ins=[iSTABL.opt() if hasattr(iSTABL, "opt") else iSTABL],
            outs=[iSTABG.opt() if hasattr(iSTABG, "opt") else iSTABG]), inc=(cc, 1))
        ins_b = {"KD1o": iKD1, "VA1o": iVA1, "DIAGKo": iDIAGK, "STABG_pre": iSTABG}
        s_b = emit_perf_b(cx, dram, ins_b, {"KVL": iKVL}, [(cc, 1)])
        R.op("gpsimd", lambda e: e.collective_compute(
            "AllReduce", mybir.AluOpType.add, replica_groups=RG,
            ins=[iKVL.opt() if hasattr(iKVL, "opt") else iKVL],
            outs=[iKVG.opt() if hasattr(iKVG, "opt") else iKVG]), inc=(cc, 1))
        ins_c = {"QD1o": iQD1, "NEGDQo": iNEGDQ, "KVG_pre": iKVG}
        s_c = emit_perf_c(cx, dram, ins_c, [(cc, 2)])
        s_ev3 = emit_ln(cx, [(s_c, s_c.v)], tag="ln3")
        s_f = emit_ffn(cx, dram, oOut, [(s_ev3, NTB)])
        _fin(R, [s_f])
        R.flush(nc, stack)
    return nc


def build_L4a_unused():
    from contextlib import ExitStack
    nc = bass.Bass("TRN2", target_bir_lowering=False, debug=False)
    with ExitStack() as stack:
        xd = nc.dram_tensor("x", [NT, 1024], BF16, kind="ExternalInput").ap()
        dram = _decl(nc, ["W1qP", "W1kP", "W1v", "pmT", "identb", "tri", "bqP", "bkP", "bvdP"])
        outs = {
            "QD1": nc.dram_tensor("oQD1", [1024, NT], BF16, kind="ExternalOutput").ap(),
            "KD1": nc.dram_tensor("oKD1", [1024, NT], BF16, kind="ExternalOutput").ap(),
            "VA1": nc.dram_tensor("oVA1", [NT, 1040], BF16, kind="ExternalOutput").ap(),
            "NEGDQ": nc.dram_tensor("oNEGDQ", [NT, 16], F32, kind="ExternalOutput").ap(),
            "DIAGK": nc.dram_tensor("oDIAGK", [NT, 16], F32, kind="ExternalOutput").ap(),
            "STABL": nc.dram_tensor("oSTABL", [1, 16], F32, kind="ExternalOutput").ap(),
        }
        cx = alloc_common(nc, stack)
        R = cx.R
        ldc = emit_consts(cx, dram)
        ldx = emit_load_x(cx, xd, cast=False)
        s_ev = emit_ln(cx, [(ldx, 16), (ldc, ldc.v)])
        s = emit_perf_a(cx, dram, outs, [(s_ev, NTB)])
        _fin(R, [s])
        R.flush(nc, stack)
    return nc


def build_L4b():
    from contextlib import ExitStack
    nc = bass.Bass("TRN2", target_bir_lowering=False, debug=False)
    with ExitStack() as stack:
        dram = _decl(nc, ["pmT", "identb", "tri", "bqP", "bkP", "bvdP"])
        ins_d = {
            "KD1o": nc.dram_tensor("KD1o", [1024, NT], BF16, kind="ExternalInput").ap(),
            "VA1o": nc.dram_tensor("VA1o", [NT, 1040], BF16, kind="ExternalInput").ap(),
            "DIAGKo": nc.dram_tensor("DIAGKo", [NT, 16], F32, kind="ExternalInput").ap(),
            "STABG": nc.dram_tensor("STABG", [1, 64], F32, kind="ExternalInput").ap(),
        }
        oKVL = nc.dram_tensor("oKVL", [4096, 65], F32, kind="ExternalOutput").ap()
        cx = alloc_common(nc, stack)
        R = cx.R
        ldc = emit_consts(cx, dram)
        s = emit_perf_b(cx, dram, ins_d, {"KVL": oKVL}, [(ldc, ldc.v)])
        _fin(R, [s])
        R.flush(nc, stack)
    return nc


def build_L4c():
    from contextlib import ExitStack
    nc = bass.Bass("TRN2", target_bir_lowering=False, debug=False)
    with ExitStack() as stack:
        xd = nc.dram_tensor("x", [NT, 1024], BF16, kind="ExternalInput").ap()
        dram = _decl(nc, ["pmT", "W1o", "Wf", "identb", "tri", "bqP", "bkP", "bvdP"])
        ins_d = {
            "QD1o": nc.dram_tensor("QD1o", [1024, NT], BF16, kind="ExternalInput").ap(),
            "NEGDQo": nc.dram_tensor("NEGDQo", [NT, 16], F32, kind="ExternalInput").ap(),
            "KV4": nc.dram_tensor("KV4", [4, 4096, 65], F32, kind="ExternalInput").ap(),
        }
        oX3 = nc.dram_tensor("oOut", [NT, 1024], F32, kind="ExternalOutput").ap()
        cx = alloc_common(nc, stack)
        R = cx.R
        ldc = emit_consts(cx, dram)
        ldx = emit_load_x(cx, xd, cast=False)
        s = emit_perf_c(cx, dram, ins_d, [(ldx, 16), (ldc, ldc.v)])
        s_ev3 = emit_ln(cx, [(s, s.v)], tag="ln3")
        s_f = emit_ffn(cx, dram, oX3, [(s_ev3, NTB)])
        _fin(R, [s_f])
        R.flush(nc, stack)
    return nc


def build_L5():
    from contextlib import ExitStack
    nc = bass.Bass("TRN2", target_bir_lowering=False, debug=False)
    with ExitStack() as stack:
        xd = nc.dram_tensor("x", [NT, 1024], BF16, kind="ExternalInput").ap()
        dram = _decl(nc, ["Wf", "identb", "tri", "bqP", "bkP", "bvdP"])
        oOut = nc.dram_tensor("oOut", [NT, 1024], F32, kind="ExternalOutput").ap()
        cx = alloc_common(nc, stack)
        R = cx.R
        ldc = emit_consts(cx, dram)
        ldx = emit_load_x(cx, xd, cast=False)
        s_ev = emit_ln(cx, [(ldx, 16), (ldc, ldc.v)])
        s = emit_ffn(cx, dram, oOut, [(s_ev, NTB)])
        _fin(R, [s])
        R.flush(nc, stack)
    return nc


def kernel(**ins):
    P = pack_consts(ins)
    x = np.asarray(ins["L0_attn_x"], np.float32)
    CN1 = ["WkP", "WvdP", "bqP", "bkP", "bvdP", "identb", "tri"]
    nc1 = build_L1()
    im1 = [dict(x=shard_rows(x, c), **{n: P[n] for n in CN1}) for c in range(8)]
    r1 = run_bass_kernel_spmd(nc1, im1, core_ids=list(range(8))).results
    KGs, VDGs = {}, {}
    for g in range(2):
        KG = np.zeros((1024, 8192), P["identb"].dtype)
        VDG = np.zeros((256, 8192), P["identb"].dtype)
        for mm in range(4):
            c = g * 4 + mm
            for ei, ge in enumerate([mm, 7 - mm]):
                KG[:, ge * 1024:(ge + 1) * 1024] = r1[c]["oKD"][:, ei * 1024:(ei + 1) * 1024]
                VDG[:, ge * 1024:(ge + 1) * 1024] = r1[c]["oVD"][:, ei * 1024:(ei + 1) * 1024]
        KGs[g], VDGs[g] = KG, VDG
    CN2 = ["WqP", "bqP", "bkP", "bvdP", "identb", "tri", "Wvu2", "bvu2"]
    x1s = [None] * 8
    for mm in range(4):
        nc2 = build_L2(mm)
        cores = [mm, 4 + mm]
        im2 = [dict(x=shard_rows(x, c), KG=KGs[c // 4], VDG=VDGs[c // 4],
                    **{n: P[n] for n in CN2}) for c in cores]
        r2 = run_bass_kernel_spmd(nc2, im2, core_ids=cores).results
        for i, c in enumerate(cores):
            x1s[c] = r2[i]["oX1"]
    CN3 = ["WeP", "beP", "WgP", "bgP", "W1qP", "W1kP", "W1v", "pmT",
           "W1o", "Wf", "identb", "tri", "bqP", "bkP", "bvdP"]
    nc3 = build_L3()
    im3 = [dict(x=x1s[c], **{n: P[n] for n in CN3}) for c in range(8)]
    r3 = run_bass_kernel_spmd(nc3, im3, core_ids=list(range(8))).results
    return unshard_rows([r3[c]["oOut"] for c in range(8)])




CONST_NAMES_A = ["WqP", "WkP", "WvdP", "bqP", "bkP", "bvdP", "identb", "identf", "tri"]


def build_stageA_test():
    from contextlib import ExitStack
    nc = bass.Bass("TRN2", target_bir_lowering=False, debug=False)
    stack = ExitStack()
    with stack:
        xd = nc.dram_tensor("x", [NT, 1024], F32, kind="ExternalInput").ap()
        P = {n: None for n in CONST_NAMES_A}
        shapes = {"WqP": [8, 128, 8, 128], "WkP": [8, 128, 8, 128], "WvdP": [4, 128, 8, 64],
                  "bqP": [128, 8], "bkP": [128, 8], "bvdP": [64, 4],
                  "identb": [128, 128], "identf": [128, 128], "tri": [128, 128]}
        dts = {"identb": BF16, "tri": BF16}
        dram = {n: nc.dram_tensor(n, shapes[n], dts.get(n, F32), kind="ExternalInput").ap()
                for n in CONST_NAMES_A}
        oXN = nc.dram_tensor("oXN", [128, 8 * NT], BF16, kind="ExternalOutput").ap()
        oQD = nc.dram_tensor("oQD", [128, 8 * NT], BF16, kind="ExternalOutput").ap()
        oKD = nc.dram_tensor("oKD", [1024, NT], BF16, kind="ExternalOutput").ap()
        oVD = nc.dram_tensor("oVD", [256, NT], BF16, kind="ExternalOutput").ap()
        cx = alloc_common(nc, stack)
        R = cx.R
        ldc = emit_consts(cx, dram)
        ldx = emit_load_x(cx, xd)
        s_ev = emit_ln(cx, [(ldx, 16)])
        s_q, m_q = emit_l0_proj(cx, dram, [(s_ev, NTB)], [(ldc, ldc.v)])
        s_k, m_k, s_kdma = emit_k_proj(cx, dram["WkP"], 8, oKD, [(s_ev, NTB)],
                                       [(ldc, ldc.v)], [(m_q, 16)], "p0k")
        s_v, m_v, s_vdma = emit_l0_vd(cx, dram, oVD, [(s_ev, NTB)], [(ldc, ldc.v)],
                                      [(m_k, 16)])
        st = R.sem("st")
        emit_store(cx, oXN, cx.t["XN"][:], [(s_ev, NTB)], st)
        emit_store(cx, oQD, cx.t["QD"][:], [(s_q, s_q.v)], st)
        R.op("sync", None, wait=[(st, st.v), (s_vdma, s_vdma.v), (s_kdma, s_kdma.v)])
        R.flush(nc, stack)
    return nc


CONST_NAMES_B = CONST_NAMES_A + ["Wvu2", "bvu2"]


def build_stageB_test():
    from contextlib import ExitStack
    nc = bass.Bass("TRN2", target_bir_lowering=False, debug=False)
    stack = ExitStack()
    with stack:
        xd = nc.dram_tensor("x", [NT, 1024], F32, kind="ExternalInput").ap()
        shapes = {"WqP": [8, 128, 8, 128], "WkP": [8, 128, 8, 128], "WvdP": [4, 128, 8, 64],
                  "bqP": [128, 8], "bkP": [128, 8], "bvdP": [64, 4],
                  "identb": [128, 128], "identf": [128, 128], "tri": [128, 128],
                  "Wvu2": [8, 32, 128], "bvu2": [8, 1, 128]}
        dts = {"identb": BF16, "tri": BF16}
        dram = {n: nc.dram_tensor(n, shapes[n], dts.get(n, F32), kind="ExternalInput").ap()
                for n in CONST_NAMES_B}
        KGd = nc.dram_tensor("KG", [1024, 8192], BF16, kind="ExternalInput").ap()
        VDGd = nc.dram_tensor("VDG", [256, 8192], BF16, kind="ExternalInput").ap()
        oX1 = nc.dram_tensor("oX1", [NT, 1024], BF16, kind="ExternalOutput").ap()
        oZ = nc.dram_tensor("oZ", [128, 256], F32, kind="ExternalOutput").ap()
        cx = alloc_common(nc, stack)
        R = cx.R
        ldc = emit_consts(cx, dram)
        ldx = emit_load_x(cx, xd)
        s_ev = emit_ln(cx, [(ldx, 16), (ldc, ldc.v)])
        s_q, m_q = emit_l0_proj(cx, dram, [(s_ev, NTB)], [(ldc, ldc.v)])
        kg_fn = lambda g, ke: KGd[g * 128:(g + 1) * 128, ke * 1024:(ke + 1) * 1024]
        vdg_fn = lambda g, ke: VDGd[(g // 2) * 64 + (g % 2) * 32:(g // 2) * 64 + (g % 2) * 32 + 32,
                                    ke * 1024:(ke + 1) * 1024]
        s_stt = emit_l0_attn(cx, dram, kg_fn, vdg_fn, [0, 7],
                             dep_q=[(s_q, s_q.v)], dep_kv=[], dep_x=[(s_ev, NTB)],
                             dep_pb=[(s_q, s_q.v)], dep_c=[(ldc, ldc.v)],
                             dbg={"x1banks": oX1} if _osmod.environ.get("DBGB") else None, groups=range(int(_osmod.environ.get("NGROUPS", "8"))))
        st = R.sem("st")
        if not _osmod.environ.get("DBGB"):
            R.op("sync", lambda e: e.dma_start(
                out=oX1.rearrange("(tb p) c -> p tb c", p=128),
                in_=cx.t["X"].rearrange("p (tb c) -> p tb c", tb=NTB)),
                wait=[(s_stt, s_stt.v)], inc=(st, 16))
        else:
            pass
        R.op("sync", lambda e: e.dma_start(out=oZ, in_=cx.t["ZS"][:]),
            wait=[(s_stt, s_stt.v)], inc=(st, 16))
        R.op("sync", None, wait=[(st, st.v)])
        R.flush(nc, stack)
    return nc





# version 09 (final)
# speedup vs baseline: 1.3159x; 1.3105x over previous
import os as _osmod
import numpy as np
import ml_dtypes
import concourse.bass as bass
import concourse.mybir as mybir
from concourse.bass_utils import run_bass_kernel_spmd

F32 = mybir.dt.float32
BF16 = mybir.dt.bfloat16
AF = mybir.ActivationFunctionType
ALU = mybir.AluOpType
AX = mybir.AxisListType

B, S, E = 2, 8192, 1024
H, D, RK = 16, 64, 16
LN16 = float(np.log(16.0))
STAB0 = 4.0
EPS_LN = 1e-5
EPS_PERF = 1e-6
NT = 2048
NTB = 16
SERIAL_ATT = True
RG = [[0, 1, 2, 3], [4, 5, 6, 7]]
BF = ml_dtypes.bfloat16


class Sem:
    __slots__ = ("name", "v", "h")

    def __init__(self, name):
        self.name, self.v, self.h = name, 0, None


TOTAL_ORDER = True


class Rec:
    ENGINES = ("tensor", "vector", "scalar", "gpsimd", "sync")

    def __init__(self):
        self.q = {e: [] for e in self.ENGINES}
        self.sems = []
        self.total = TOTAL_ORDER
        if self.total:
            self.G = self.sem("Gtot")

    def sem(self, name):
        s = Sem(name)
        self.sems.append(s)
        return s

    def op(self, eng, fn, wait=(), inc=None):
        if self.total:
            ws = [(self.G, self.G.v)]
            amt = inc[1] if inc is not None else 1
            if inc is not None:
                inc[0].v += inc[1]
            if fn is None:
                self.q[eng].append((ws, None, None))
                return
            self.G.v += amt
            self.q[eng].append((ws, fn, (self.G, amt)))
            return
        ws = [(w, w.v) if isinstance(w, Sem) else w for w in wait]
        if inc is not None:
            inc[0].v += inc[1]
        self.q[eng].append((ws, fn, inc))

    def flush(self, nc, stack):
        for s in self.sems:
            s.h = stack.enter_context(nc.semaphore(s.name))
        block = stack.enter_context(nc.Block())
        for eng in self.ENGINES:
            items = self.q[eng]
            if not items:
                continue

            def make(items):
                def body(e):
                    for ws, fn, inc in items:
                        for s, v in ws:
                            if v > 0:
                                e.wait_ge(s.h, v)
                        if fn is not None:
                            inst = fn(e)
                            if inc is not None:
                                inst.then_inc(inc[0].h, inc[1])
                        elif inc is not None:
                            e.nop().then_inc(inc[0].h, inc[1])
                return body
            getattr(block, eng)(make(items))


def _qk_pack(Wm):
    Wp = Wm.reshape(8, 2, 1024, 64).transpose(0, 2, 1, 3).reshape(8, 1024, 128)
    Wp = Wp.reshape(8, 8, 128, 128).transpose(0, 2, 1, 3)
    return np.ascontiguousarray(Wp, np.float32)


def pack_consts(ins):
    f32 = np.float32
    P = {}
    P["WqP"] = _qk_pack(ins["L0_attn_Wq"])
    P["WkP"] = _qk_pack(ins["L0_attn_Wk"])
    P["bqP"] = np.ascontiguousarray(ins["L0_attn_bq"].reshape(8, 128).T, f32)
    P["bkP"] = np.ascontiguousarray(ins["L0_attn_bk"].reshape(8, 128).T, f32)
    Wvd = ins["L0_attn_Wvd"].reshape(4, 4, 1024, 16).transpose(0, 2, 1, 3).reshape(4, 1024, 64)
    Wvd = Wvd.reshape(4, 8, 128, 64).transpose(0, 2, 1, 3)
    P["WvdP"] = np.ascontiguousarray(Wvd, f32)
    P["bvdP"] = np.ascontiguousarray(ins["L0_attn_bvd"].reshape(4, 64).T, f32)
    Wvu = ins["L0_attn_Wvu"]
    W2 = np.zeros((8, 32, 128), f32)
    for g in range(8):
        W2[g, 0:16, 0:64] = Wvu[2 * g]
        W2[g, 16:32, 64:128] = Wvu[2 * g + 1]
    P["Wvu2"] = W2
    P["bvu2"] = np.ascontiguousarray(ins["L0_attn_bvu"].reshape(8, 1, 128), f32)
    P["WeP"] = np.ascontiguousarray(ins["L0_ffn_We"], f32)
    P["beP"] = np.ascontiguousarray(ins["L0_ffn_be"].reshape(8, 1, 1024), f32)
    P["WgP"] = np.ascontiguousarray(ins["L0_ffn_Wg"], f32)
    P["bgP"] = np.ascontiguousarray(ins["L0_ffn_bg"].reshape(1, 8), f32)
    Wqkv = ins["L1_attn_W_qkv"]
    P["W1qP"] = _qk_pack(np.ascontiguousarray(Wqkv[:, 0:1024]).reshape(1024, 16, 64).transpose(1, 0, 2))
    P["W1kP"] = _qk_pack(np.ascontiguousarray(Wqkv[:, 1024:2048]).reshape(1024, 16, 64).transpose(1, 0, 2))
    P["W1v"] = np.ascontiguousarray(Wqkv[:, 2048:3072], f32)
    P["pmT"] = np.ascontiguousarray(ins["L1_attn_proj"].transpose(0, 2, 1), f32)
    P["W1o"] = np.ascontiguousarray(ins["L1_attn_W_out"], f32)
    P["Wf"] = np.ascontiguousarray(ins["L1_ffn_weight"], f32)
    P["identb"] = np.eye(128, dtype=BF)
    P["identf"] = np.eye(128, dtype=f32)
    P["tri"] = np.tril(np.ones((128, 128), f32)).astype(BF)
    return P


def shard_rows(x_full, c):
    g, m = c // 4, c % 4
    xb = x_full[g]
    return np.ascontiguousarray(
        np.concatenate([xb[m * 1024:(m + 1) * 1024], xb[(7 - m) * 1024:(8 - m) * 1024]], axis=0),
        np.float32)


def unshard_rows(shards):
    out = np.empty((B, S, E), np.float32)
    for c in range(8):
        g, m = c // 4, c % 4
        out[g, m * 1024:(m + 1) * 1024] = shards[c][0:1024]
        out[g, (7 - m) * 1024:(8 - m) * 1024] = shards[c][1024:2048]
    return out


class Ctx:
    pass


def alloc_common(nc, stack):
    cx = Ctx()
    cx.nc = nc
    cx.R = Rec()
    ec = stack.enter_context
    t = {}

    def sb(name, shape, dt):
        t[name] = ec(nc.sbuf_tensor(name, shape, dt))
    cx.t = t
    sb("X", [128, NTB * 1024], BF16)
    sb("OS0", [128, 1024], F32)
    sb("OS1", [128, 1024], F32)
    sb("XN", [128, 8 * NT], BF16)
    sb("QD", [128, 8 * NT], BF16)
    sb("KSTR0", [128, 2048], BF16)
    sb("KSTR1", [128, 2048], BF16)
    sb("WEB", [128, 8192], BF16)
    sb("VAP", [128, 16 * 1040], BF16)
    sb("VT0", [128, 8 * 130], BF16)
    sb("VT1", [128, 8 * 130], BF16)
    sb("VD4", [64, NT], BF16)
    sb("ES0", [128, 512], BF16)
    sb("ES1", [128, 512], BF16)
    sb("AVSB", [128, 512], BF16)
    sb("W0", [128, 1024], BF16)
    sb("W1", [128, 1024], BF16)
    sb("WR0", [32, 1024], BF16)
    sb("XT0", [128, 1024], BF16)
    sb("STATS", [128, 164], F32)
    sb("CONSTB", [128, 256], BF16)
    sb("ONE1", [1, 128], BF16)
    sb("BQK", [128, 16], F32)
    sb("CB", [128, 4], F32)
    sb("BVD", [64, 4], F32)
    sb("BR", [1, 1024], BF16)
    sb("WG", [128, 64], BF16)
    sb("BG1", [1, 8], BF16)
    sb("NEGD", [128, 48], F32)
    sb("KT64", [128, 64], BF16)
    sb("K264", [128, 64], BF16)
    sb("KVA", [128, 2080], F32)
    sb("KVB", [128, 2080], F32)
    sb("QFT", [128, 256], BF16)
    sb("ATN", [128, 64], BF16)
    sb("KF", [128, 256], BF16)
    sb("MKB", [128, 128], BF16)
    sb("MDT", [128, 128], BF16)
    sb("BK1", [128, 16], F32)
    sb("BK2", [128, 16], F32)
    sb("ZB5", [128, 512], BF16)
    sb("EPI", [128, 528], BF16)
    sb("KVS", [128, 130], F32)
    sb("KVH0", [128, 130], BF16)
    sb("KVH1", [128, 130], BF16)
    sb("WV2", [32, 128], BF16)
    sb("PM0", [64, 128], BF16)
    sb("PM1", [64, 128], BF16)
    sb("MX", [128, 16], F32)
    sb("SG", [128, 24], F32)
    sb("SR16", [16, 128], BF16)
    sb("ZS", [128, 256], F32)
    sb("ST16", [16, 1], F32)
    cx.pb = [ec(nc.psum_tensor(f"pb{i}", [128, 512], F32)) for i in range(6)]
    cx.pt = [ec(nc.psum_tensor(f"pt{i}", [128, 1024], BF16)) for i in range(2)]
    return cx


def att_alias(cx):
    return cx.t["VAP"]


def emit_consts(cx, dram):
    R, t = cx.R, cx.t
    ld = R.sem("ldc")
    pairs = [
        (dram["identb"], t["CONSTB"][:, 0:128]),
        (dram["tri"], t["CONSTB"][:, 128:256]),
        (dram["bqP"], t["BQK"][:, 0:8]),
        (dram["bkP"], t["BQK"][:, 8:16]),
        (dram["bvdP"], t["BVD"][:]),
    ]
    for src, dst in pairs:
        R.op("sync", lambda e, src=src, dst=dst: e.dma_start(out=dst, in_=src), inc=(ld, 16))
    R.op("gpsimd", lambda e: e.memset(t["ONE1"][:], 1.0), inc=(ld, 1))
    R.op("gpsimd", lambda e: e.memset(t["CB"][:, 0:1], EPS_LN), inc=(ld, 1))
    R.op("gpsimd", lambda e: e.memset(t["CB"][:, 1:2], -STAB0), inc=(ld, 1))
    return ld


def emit_ln(cx, dep, x_src="X", tag="ln"):
    R, t = cx.R, cx.t
    X, XN = t[x_src], t["XN"]
    SM = t["STATS"]
    s_sq = R.sem(tag + "_sq")
    s_st = R.sem(tag + "_st")
    s_xn = R.sem(tag + "_xn")
    s_tr = R.sem(tag + "_tr")
    s_ev = R.sem(tag + "_ev")
    s_s2 = R.sem(tag + "_s2")
    for tb in range(NTB):
        xt = X[:, tb * 1024:(tb + 1) * 1024]
        sqb = t["W0"] if tb % 2 == 0 else t["W1"]
        xnb = t["XT0"]
        c = 10 * tb
        wsq = list(dep) + ([] if tb < 2 else [(s_s2, tb - 1)])
        R.op("scalar", lambda e, xt=xt, sqb=sqb: e.activation(out=sqb[:], in_=xt, func=AF.Square),
             wait=wsq, inc=(s_sq, 1))
        R.op("vector", lambda e, xt=xt, c=c: e.reduce_sum(SM[:, c:c + 1], xt, axis=AX.X), wait=list(dep))
        R.op("vector", lambda e, sqb=sqb, c=c: e.reduce_sum(SM[:, c + 1:c + 2], sqb[:], axis=AX.X),
             wait=[(s_sq, tb + 1)], inc=(s_s2, 1))
        R.op("vector", lambda e, c=c: e.tensor_scalar(out=SM[:, c + 2:c + 3], in0=SM[:, c:c + 1],
                                                      scalar1=1.0 / 1024, scalar2=None, op0=ALU.mult))
        R.op("vector", lambda e, c=c: e.tensor_scalar(out=SM[:, c + 3:c + 4], in0=SM[:, c + 1:c + 2],
                                                      scalar1=1.0 / 1024, scalar2=None, op0=ALU.mult))
        R.op("vector", lambda e, c=c: e.tensor_tensor(out=SM[:, c + 4:c + 5], in0=SM[:, c + 2:c + 3],
                                                      in1=SM[:, c + 2:c + 3], op=ALU.mult))
        R.op("vector", lambda e, c=c: e.tensor_tensor(out=SM[:, c + 5:c + 6], in0=SM[:, c + 3:c + 4],
                                                      in1=SM[:, c + 4:c + 5], op=ALU.subtract), inc=(s_xn, 1))
        R.op("scalar", lambda e, c=c: e.activation(out=SM[:, c + 6:c + 7], in_=SM[:, c + 5:c + 6],
                                                   func=AF.Ln, bias=t["CB"][:, 0:1]), wait=[(s_xn, 2 * tb + 1)])
        R.op("scalar", lambda e, c=c: e.activation(out=SM[:, c + 7:c + 8], in_=SM[:, c + 6:c + 7],
                                                   func=AF.Exp, scale=-0.5), inc=(s_st, 1))
        wxn = [(s_st, tb + 1)] + ([] if tb < 1 else [(s_tr, tb * 8)])
        R.op("vector", lambda e, c=c: e.scalar_tensor_tensor(out=SM[:, c + 8:c + 9], in0=SM[:, c + 2:c + 3],
                                                             scalar=-1.0, in1=SM[:, c + 7:c + 8],
                                                             op0=ALU.mult, op1=ALU.mult), wait=wxn)
        R.op("vector", lambda e, xt=xt, xnb=xnb, c=c: e.tensor_scalar(
            out=xnb[:], in0=xt, scalar1=SM[:, c + 7:c + 8], scalar2=SM[:, c + 8:c + 9],
            op0=ALU.mult, op1=ALU.add), inc=(s_xn, 1))
        for f in range(8):
            wt = [(s_xn, 2 * tb + 2)] if f == 0 else []
            if f == 0 and tb >= 2:
                wt.append((s_ev, tb - 1))
            R.op("tensor", lambda e, xnb=xnb, f=f, tb=tb: e.transpose(
                cx.pt[tb % 2][:, f * 128:(f + 1) * 128], xnb[:, f * 128:(f + 1) * 128],
                cx.t["CONSTB"][:, 0:128]), wait=wt, inc=(s_tr, 1))
        R.op("scalar", lambda e, tb=tb: e.copy(
            out=XN.rearrange("p (f c) -> p f c", f=8)[:, :, tb * 128:(tb + 1) * 128],
            in_=cx.pt[tb % 2].rearrange("p (f c) -> p f c", f=8)[:, :, :]),
            wait=[(s_tr, (tb + 1) * 8)], inc=(s_ev, 1))
    return s_ev


def _mm_proj_pairs(cx, Wdram, n_pairs, m_width, dst, bias_ap_fn, dep, tag,
                   kc_n=8, dep_w=(), dst_stride=NT, after_pair=None):
    R, t = cx.R, cx.t
    XN = t["XN"]
    s_wld = R.sem(f"{tag}_wld")
    s_mm = R.sem(f"{tag}_mm")
    s_done = R.sem(f"{tag}_ev")
    s_dma = R.sem(f"{tag}_dma") if after_pair is not None else None
    for pr in range(n_pairs):
        wb = t["W0"] if pr % 2 == 0 else t["W1"]
        wld = (list(dep_w) if pr < 2 else []) + ([] if pr < 2 else [(s_mm, 2 * pr - 2)])
        R.op("gpsimd", lambda e, pr=pr, wb=wb: e.dma_start(
            out=wb[0:128, 0:kc_n * m_width],
            in_=Wdram[pr].rearrange("p k m -> p (k m)")), wait=wld, inc=(s_wld, 16))
        for nq2 in range(2):
            it = pr * 2 + nq2
            b0, b1 = (it % 2) * 2, (it % 2) * 2 + 1
            for kc in range(kc_n):
                w = []
                if kc == 0:
                    w.append((s_wld, 16 * (pr + 1)))
                    if it == 0:
                        w.extend(dep)
                    if it >= 2:
                        w.append((s_done, 2 * it - 2))
                for half in range(2):
                    nq = nq2 * 2 + half
                    pbk = cx.pb[b0 if half == 0 else b1]
                    R.op("tensor", lambda e, kc=kc, nq=nq, wb=wb, pbk=pbk: e.matmul(
                        pbk[0:m_width, :],
                        wb[:, kc * m_width:(kc + 1) * m_width],
                        XN[:, kc * NT + nq * 512: kc * NT + (nq + 1) * 512],
                        start=(kc == 0), stop=(kc == kc_n - 1)),
                        wait=w if half == 0 else [],
                        inc=(s_mm, 1) if (kc == kc_n - 1 and half == 1) else None)
            for half in range(2):
                nq = nq2 * 2 + half
                pbk = cx.pb[b0 if half == 0 else b1]
                bias = bias_ap_fn(pr)
                wev = [(s_mm, it + 1)] if half == 0 else []
                if it == 0 and half == 0:
                    wev.extend(dep)
                if after_pair is not None and pr >= 1 and it == 2 * pr and half == 0:
                    wev.append((s_dma, 16 * pr))
                if bias is not None:
                    R.op("vector", lambda e, pr=pr, nq=nq, pbk=pbk, bias=bias: e.tensor_scalar(
                        out=dst[0:m_width, pr * dst_stride + nq * 512: pr * dst_stride + (nq + 1) * 512],
                        in0=pbk[0:m_width, :], scalar1=bias, scalar2=None, op0=ALU.add),
                        wait=wev, inc=(s_done, 1))
                else:
                    R.op("vector", lambda e, pr=pr, nq=nq, pbk=pbk: e.tensor_copy(
                        out=dst[0:m_width, pr * dst_stride + nq * 512: pr * dst_stride + (nq + 1) * 512],
                        in_=pbk[0:m_width, :]),
                        wait=wev, inc=(s_done, 1))
        if after_pair is not None:
            dram_ap, sbuf_ap = after_pair(pr)
            R.op("sync", lambda e, dram_ap=dram_ap, sbuf_ap=sbuf_ap: e.dma_start(
                out=dram_ap, in_=sbuf_ap), wait=[(s_done, 4 * (pr + 1))], inc=(s_dma, 16))
    if after_pair is not None:
        return s_done, s_mm, s_dma
    return s_done, s_mm


def emit_l0_proj(cx, dram, dep_xn, dep_c):
    t = cx.t
    dep = list(dep_xn) + list(dep_c)
    s_q, m_q = _mm_proj_pairs(cx, dram["WqP"], 8, 128, t["QD"],
                              lambda pr: t["BQK"][:, pr:pr + 1], dep, "p0q",
                              dep_w=list(dep_xn))
    return s_q, m_q


def emit_k_proj(cx, Wdram, bias_col0, k_dram, dep_xn, dep_c, dep_w, tag):
    t = cx.t
    dep = list(dep_xn) + list(dep_c)
    bias_fn = (lambda pr: None) if bias_col0 is None else (
        lambda pr: t["BQK"][:, bias_col0 + pr:bias_col0 + pr + 1])
    s_k, m_k, s_dma = _mm_proj_pairs(
        cx, Wdram, 8, 128, t["KSTR0"], bias_fn, dep, tag,
        dep_w=list(dep_w), dst_stride=0,
        after_pair=lambda pr: (k_dram[pr * 128:(pr + 1) * 128, :], t["KSTR0"][:, :]))
    return s_k, m_k, s_dma


def emit_l0_vd(cx, dram, vd_dram, dep_xn, dep_c, dep_w):
    t = cx.t
    dep = list(dep_xn) + list(dep_c)
    s_v, m_v, s_dma = _mm_proj_pairs(
        cx, dram["WvdP"], 4, 64, t["VD4"],
        lambda pr: t["BVD"][:, pr:pr + 1], dep, "p0v",
        dep_w=list(dep_w), dst_stride=0,
        after_pair=lambda pr: (vd_dram[pr * 64:(pr + 1) * 64, :], t["VD4"][0:64, :]))
    return s_v, m_v, s_dma


def emit_load_x(cx, x_dram, cast=True):
    R, t = cx.R, cx.t
    s = R.sem("ldx")
    eng = "gpsimd" if cast else "sync"
    R.op(eng, lambda e: e.dma_start(
        out=t["X"].rearrange("p (tb c) -> p tb c", tb=NTB),
        in_=x_dram.rearrange("(tb p) c -> p tb c", p=128)), inc=(s, 16))
    return s



def emit_l0_attn(cx, dram, kg_fn, vdg_fn, own_ge, dep_q, dep_kv, dep_x, dep_pb, dep_c=(), dbg=None, groups=range(8)):
    R, t = cx.R, cx.t
    QD, X = t["QD"], t["X"]
    s_ini = R.sem("a_ini")
    s_kld = R.sem("a_kld")
    s_vld = R.sem("a_vld")
    s_vt = R.sem("a_vt")
    s_vte = R.sem("a_vte")
    s_sc = R.sem("a_sc")
    s_ex = R.sem("a_ex")
    s_mk = R.sem("a_mk")
    s_av = R.sem("a_av")
    s_ep = R.sem("a_ep")
    s_ept = R.sem("a_ept")
    s_stt = R.sem("a_stt")
    s_ku = R.sem("a_ku")
    emit_l0_attn._dep_c = list(dep_c)
    es_free = {0: 0, 1: 0}
    es_i = 0
    it = 0
    started = set()
    R.op("gpsimd", lambda e: e.memset(t["ZB5"][:], 0.0), inc=(s_ini, 1))
    for vtb in (t["VT0"], t["VT1"]):
        for col in (64, 129):
            R.op("gpsimd", lambda e, vtb=vtb, col=col: e.memset(
                vtb.rearrange("p (tb c) -> p tb c", tb=8)[:, :, col:col + 1], 1.0),
                inc=(s_ini, 1))
    R.op("sync", lambda e: e.dma_start(out=t["BK1"][:], in_=dram["BIASK"]), inc=(s_kld, 16))
    R.op("sync", lambda e: e.dma_start(out=t["BK2"][:], in_=dram["BIASK2"]), inc=(s_kld, 16))
    first_te = [True]
    for g in groups:
        wc = [(s_vt, s_vt.v)]
        R.op("gpsimd", lambda e, g=g: e.dma_start(out=t["WV2"][:], in_=dram["Wvu2"][g]),
             wait=wc, inc=(s_vld, 16))
        R.op("gpsimd", lambda e, g=g: e.dma_start(out=t["BR"][0:1, 0:128], in_=dram["bvu2"][g]),
             wait=[(s_vt, s_vt.v)], inc=(s_vld, 16))
        wv_ready = (s_vld, s_vld.v)
        for ei, ge in enumerate(own_ge):
            for ke in range(8):
                kb = t["KSTR0"] if it % 2 == 0 else t["KSTR1"]
                vtb = t["VT0"] if it % 2 == 0 else t["VT1"]
                R.op("sync", lambda e, ei=ei, ke=ke: e.dma_start(
                    out=t["MDT"][:], in_=dram["MD"][ei * 8 + ke]), inc=(s_kld, 16))
                wk = list(dep_kv) + ([(s_ku, it - 1)] if it >= 2 else [])
                R.op("sync", lambda e, g=g, ke=ke, kb=kb: e.dma_start(
                    out=kb[:, 0:1024], in_=kg_fn(g, ke)), wait=wk, inc=(s_kld, 16))
                R.op("sync", lambda e, g=g, ke=ke: e.dma_start(
                    out=t["WR0"][:, 0:1024], in_=vdg_fn(g, ke)),
                    wait=[(s_vt, s_vt.v)], inc=(s_vld, 16))
                vld_here = (s_vld, s_vld.v)
                for tb8 in range(8):
                    w = []
                    if tb8 == 0:
                        w = [wv_ready, vld_here, (s_ex, s_ex.v),
                             (s_vte, s_vte.v), (s_ini, 5)]
                        if it >= 2:
                            w.append((s_vte, s_vte.v))
                        if first_te[0]:
                            w += list(dep_q) + list(dep_pb) + list(emit_l0_attn._dep_c)
                            first_te[0] = False
                    else:
                        w = [(s_vte, s_vte.v)]
                    R.op("tensor", lambda e, tb8=tb8: e.matmul(
                        cx.pb[4][:, 0:128],
                        t["WR0"][0:32, tb8 * 128:(tb8 + 1) * 128], t["WV2"][:],
                        start=True, stop=False), wait=w)
                    R.op("tensor", lambda e: e.matmul(
                        cx.pb[4][:, 0:128], t["ONE1"][0:1, 0:128], t["BR"][0:1, 0:128],
                        start=False, stop=True), inc=(s_vt, 1))
                    wev = [(s_vt, s_vt.v)]
                    if tb8 == 0 and it >= 2:
                        wev.append((s_ku, it - 1))
                    R.op("scalar", lambda e, tb8=tb8, vtb=vtb: e.copy(
                        out=vtb[:, tb8 * 130:tb8 * 130 + 130].rearrange(
                            "p (hh c) -> p hh c", hh=2)[:, :, 0:64],
                        in_=cx.pb[4][:, 0:128].rearrange("p (hh c) -> p hh c", hh=2)),
                        wait=wev, inc=(s_vte, 1))
                if dbg is not None and "vt" in dbg and it == 8:
                    R.op("sync", lambda e, vtb=vtb: e.dma_start(out=dbg["vt"], in_=vtb[:, :]),
                         wait=[(s_vte, s_vte.v)], inc=(s_ini, 16))
                for hh in range(2):
                    tcs = [(tc, 1024) for tc in range(8)]
                    for tc, ns in tcs:
                        for sh in range((ns + 511) // 512):
                            n = min(512, ns - sh * 512)
                            par = es_i % 2
                            esb = t["ES0"] if par == 0 else t["ES1"]
                            scb = cx.pb[4 + par]
                            wsc = [(s_kld, s_kld.v), (s_vte, s_vte.v), (s_ex, s_ex.v)]
                            if SERIAL_ATT:
                                wsc += [(s_av, s_av.v), (s_mk, s_mk.v), (s_stt, s_stt.v)]
                            R.op("tensor", lambda e, g=g, hh=hh, tc=tc, sh=sh, n=n, ei=ei, kb=kb, scb=scb: e.matmul(
                                scb[:, 0:n],
                                kb[hh * 64:hh * 64 + 64, tc * 128:(tc + 1) * 128],
                                QD[hh * 64:hh * 64 + 64,
                                   g * NT + ei * 1024 + sh * 512: g * NT + ei * 1024 + sh * 512 + n],
                                start=True, stop=True),
                                wait=wsc,
                                inc=(s_sc, 1))
                            wex = [(s_sc, s_sc.v)]
                            if es_i == 0:
                                wex += list(emit_l0_attn._dep_c)
                            if es_free[par] > 0:
                                wex.append((s_av, es_free[par]))
                            bcol = ei * 8 + ke
                            if sh < tc // 4:
                                R.op("scalar", lambda e, esb=esb, scb=scb, n=n, bcol=bcol: e.activation(
                                    out=esb[:, 0:n], in_=scb[:, 0:n], func=AF.Exp,
                                    scale=1.0 / 32, bias=t["BK1"][:, bcol:bcol + 1]),
                                    wait=wex, inc=(s_ex, 1))
                            elif sh > tc // 4:
                                R.op("scalar", lambda e, esb=esb, scb=scb, n=n, bcol=bcol: e.activation(
                                    out=esb[:, 0:n], in_=scb[:, 0:n], func=AF.Exp,
                                    scale=1.0 / 32, bias=t["BK2"][:, bcol:bcol + 1]),
                                    wait=wex, inc=(s_ex, 1))
                            else:
                                jd = tc * 128 - sh * 512
                                R.op("scalar", lambda e, esb=esb, scb=scb, jd=jd, bcol=bcol: e.activation(
                                    out=esb[:, 0:jd + 128], in_=scb[:, 0:jd + 128], func=AF.Exp,
                                    scale=1.0 / 32, bias=t["BK1"][:, bcol:bcol + 1]),
                                    wait=wex)
                                if jd + 128 < 512:
                                    R.op("scalar", lambda e, esb=esb, scb=scb, jd=jd, bcol=bcol: e.activation(
                                        out=esb[:, jd + 128:512], in_=scb[:, jd + 128:512], func=AF.Exp,
                                        scale=1.0 / 32, bias=t["BK2"][:, bcol:bcol + 1]))
                                R.op("scalar", None, inc=(s_ex, 1))
                            masked = (tc // 4 == sh)
                            j0 = tc * 128 - sh * 512 if masked else n
                            if masked:
                                R.op("vector", lambda e, esb=esb, j0=j0: e.tensor_tensor(
                                    out=t["MKB"][:], in0=esb[:, j0:j0 + 128],
                                    in1=t["MDT"][:], op=ALU.mult),
                                    wait=[(s_ex, s_ex.v)], inc=(s_mk, 1))
                            if dbg is not None and "es" in dbg and es_i == 0:
                                R.op("sync", lambda e, esb=esb, n=n: e.dma_start(
                                    out=dbg["es"][:, 0:n], in_=esb[:, 0:n]),
                                    wait=[(s_mk, s_mk.v)] if masked else [(s_ex, s_ex.v)],
                                    inc=(s_ini, 16))
                            bank = 2 * hh + sh
                            key = (g, ei, bank)
                            st_flag = key not in started
                            started.add(key)
                            stop_flag = False
                            w = [(s_ex, s_ex.v)]
                            if st_flag:
                                w.append((s_ep, s_ep.v))
                            if j0 > 0:
                                R.op("tensor", lambda e, vtb=vtb, hh=hh, tc=tc, bank=bank, esb=esb, j0=j0,
                                     st=st_flag, sp=(stop_flag and not masked): e.matmul(
                                    cx.pb[bank][0:65, 0:j0],
                                    vtb[:, tc * 130 + hh * 65: tc * 130 + hh * 65 + 65],
                                    esb[:, 0:j0], start=st, stop=sp),
                                    wait=w, inc=(s_av, 1))
                                st_flag = False
                            if masked:
                                w2 = [(s_mk, s_mk.v)]
                                if st_flag:
                                    w2.append((s_ep, s_ep.v))
                                R.op("tensor", lambda e, vtb=vtb, hh=hh, tc=tc, bank=bank, j0=j0,
                                     st=st_flag, sp=stop_flag: e.matmul(
                                    cx.pb[bank][0:65, j0:j0 + 128],
                                    vtb[:, tc * 130 + hh * 65: tc * 130 + hh * 65 + 65],
                                    t["MKB"][:], start=st, stop=sp),
                                    wait=w2, inc=(s_av, 1))
                                if j0 + 128 < 512:
                                    R.op("tensor", lambda e, vtb=vtb, hh=hh, tc=tc, bank=bank, j0=j0,
                                         esb=esb, sp=stop_flag: e.matmul(
                                        cx.pb[bank][0:65, j0 + 128:512],
                                        vtb[:, tc * 130 + hh * 65: tc * 130 + hh * 65 + 65],
                                        esb[:, j0 + 128:512], start=False, stop=sp),
                                        inc=(s_av, 1))
                            es_free[par] = s_av.v
                            es_i += 1
                R.op("tensor", None, inc=(s_ku, 1))
                it += 1
            for bank in range(4):
                R.op("tensor", lambda e, bank=bank: e.matmul(
                    cx.pb[bank][0:65, 0:512],
                    (t["VT0"] if it % 2 == 1 else t["VT1"])[:, 0:65],
                    t["ZB5"][:], start=False, stop=True), inc=(s_av, 1))
            for hh in range(2):
                h = 2 * g + hh
                for sh in range(2):
                    R.op("scalar", lambda e, hh=hh, sh=sh: e.copy(
                        out=t["AVSB"][0:65, 0:512], in_=cx.pb[2 * hh + sh][0:65, 0:512]),
                        wait=[(s_av, s_av.v), (s_ept, s_ept.v)], inc=(s_ep, 1))
                    if dbg is not None and "x1banks" in dbg and g == 0:
                        r0 = (ei * 2 + hh) * 128
                        R.op("sync", lambda e, sh=sh, r0=r0: e.dma_start(
                            out=dbg["x1banks"][r0:r0 + 128, sh * 512:(sh + 1) * 512],
                            in_=t["AVSB"][0:128, 0:512]),
                            wait=[(s_ep, s_ep.v)], inc=(s_ini, 16))
                    for s4 in range(4):
                        slot = sh * 4 + s4
                        w = [(s_ep, s_ep.v)] if s4 == 0 else []
                        if s4 == 0 and sh == 0:
                            w.append((s_stt, s_stt.v))
                        R.op("tensor", lambda e, s4=s4, slot=slot: e.transpose(
                            cx.pt[0][:, slot * 128: slot * 128 + 65],
                            t["AVSB"][0:65, s4 * 128:(s4 + 1) * 128],
                            t["CONSTB"][0:65, 0:65]), wait=w, inc=(s_ept, 1))
                for slot in range(8):
                    wep = [(s_ept, s_ept.v), (s_stt, s_stt.v)] if slot == 0 else []
                    R.op("vector", lambda e, slot=slot: e.tensor_copy(
                        out=t["EPI"][:, slot * 66: slot * 66 + 65],
                        in_=cx.pt[0][:, slot * 128: slot * 128 + 65]),
                        wait=wep, inc=(s_ep, 1) if slot == 7 else None)
                for slot in range(8):
                    tb = ei * 8 + slot
                    w = [(s_ep, s_ep.v)] if slot == 0 else []
                    if g == 0 and ei == 0 and hh == 0 and slot == 0:
                        w += list(dep_x)
                    if dbg is not None:
                        R.op("scalar", lambda e, slot=slot, tb=tb, h=h: e.copy(
                            out=t["ZS"][:, h * 16 + tb: h * 16 + tb + 1],
                            in_=cx.pt[0][:, slot * 128 + 64: slot * 128 + 65]),
                            wait=[(s_ept, s_ept.v)])
                    R.op("vector", lambda e, slot=slot: e.reciprocal(
                        t["STATS"][:, 160:161], t["EPI"][:, slot * 66 + 64: slot * 66 + 65]),
                        wait=w)
                    if _osmod.environ.get("DBGZ"):
                        R.op("vector", lambda e, slot=slot, tb=tb, h=h: e.tensor_copy(
                            out=X[:, tb * 1024 + h * 64: tb * 1024 + h * 64 + 65],
                            in_=t["EPI"][:, slot * 66: slot * 66 + 65]), inc=(s_stt, 1))
                    elif _osmod.environ.get("DBGATT"):
                        R.op("vector", lambda e, slot=slot, tb=tb, h=h: e.tensor_scalar(
                            out=X[:, tb * 1024 + h * 64: tb * 1024 + h * 64 + 64],
                            in0=t["EPI"][:, slot * 66: slot * 66 + 64],
                            scalar1=t["STATS"][:, 160:161], scalar2=None,
                            op0=ALU.mult), inc=(s_stt, 1))
                    else:
                        R.op("vector", lambda e, slot=slot, tb=tb, h=h: e.scalar_tensor_tensor(
                            out=X[:, tb * 1024 + h * 64: tb * 1024 + h * 64 + 64],
                            in0=t["EPI"][:, slot * 66: slot * 66 + 64],
                            scalar=t["STATS"][:, 160:161],
                            in1=X[:, tb * 1024 + h * 64: tb * 1024 + h * 64 + 64],
                            op0=ALU.mult, op1=ALU.add), inc=(s_stt, 1))
            started = {k for k in started if k[0] != g or k[1] != ei}
    return s_stt


def emit_moe(cx, dram, dep):
    R, t = cx.R, cx.t
    X, XN = t["X"], t["XN"]
    s = R.sem("moe")
    R.op("gpsimd", lambda e: e.dma_start(
        out=t["WG"][:, 0:64].rearrange("p (kc c) -> p kc c", kc=8),
        in_=dram["WgP"].rearrange("(kc p) c -> p kc c", p=128)), wait=dep, inc=(s, 16))
    R.op("gpsimd", lambda e: e.dma_start(out=t["BG1"][:], in_=dram["bgP"]), inc=(s, 16))
    SG = t["SG"]
    WC = t["KVA"]
    for tb in range(NTB):
        for kc in range(8):
            R.op("tensor", lambda e, tb=tb, kc=kc: e.matmul(
                cx.pb[0][:, 0:8],
                XN[:, kc * NT + tb * 128: kc * NT + tb * 128 + 128],
                t["WG"][:, kc * 8:(kc + 1) * 8], start=(kc == 0), stop=False))
        R.op("tensor", lambda e: e.matmul(
            cx.pb[0][:, 0:8], t["ONE1"][0:1, 0:128], t["BG1"][:],
            start=False, stop=True))
        R.op("vector", lambda e: e.tensor_copy(out=SG[:, 0:8], in_=cx.pb[0][:, 0:8]))
        R.op("vector", lambda e: e.reduce_max(SG[:, 16:17], SG[:, 0:8], axis=AX.X, negate=True))
        R.op("scalar", lambda e: e.activation(out=SG[:, 8:16], in_=SG[:, 0:8],
                                              func=AF.Exp, bias=SG[:, 16:17]))
        R.op("vector", lambda e: e.tensor_scalar(out=SG[:, 0:8], in0=SG[:, 8:16],
                                                 scalar1=0.99999, scalar2=None, op0=ALU.is_ge))
        R.op("vector", lambda e: e.tensor_tensor(out=SG[:, 16:24], in0=SG[:, 8:16],
                                                 in1=SG[:, 0:8], op=ALU.subtract))
        R.op("vector", lambda e: e.reduce_max(SG[:, 17:18] if False else t["NEGD"][:, 0:1],
                                              SG[:, 16:24], axis=AX.X))
        R.op("vector", lambda e: e.tensor_scalar(out=SG[:, 16:24], in0=SG[:, 8:16],
                                                 scalar1=t["NEGD"][:, 0:1], scalar2=None,
                                                 op0=ALU.is_ge))
        R.op("vector", lambda e: e.tensor_tensor(out=SG[:, 8:16], in0=SG[:, 8:16],
                                                 in1=SG[:, 16:24], op=ALU.mult))
        R.op("vector", lambda e: e.reduce_sum(t["NEGD"][:, 1:2], SG[:, 8:16], axis=AX.X))
        R.op("vector", lambda e: e.reciprocal(t["NEGD"][:, 2:3], t["NEGD"][:, 1:2]))
        R.op("vector", lambda e, tb=tb: e.tensor_scalar(out=WC[:, tb * 8:(tb + 1) * 8],
                                                        in0=SG[:, 8:16],
                                                        scalar1=t["NEGD"][:, 2:3], scalar2=None,
                                                        op0=ALU.mult))
    for ex in range(8):
        R.op("gpsimd", lambda e, ex=ex: e.dma_start(
            out=t["WEB"].rearrange("p (kc c) -> p kc c", kc=8),
            in_=dram["WeP"][ex].rearrange("(kc p) c -> p kc c", p=128)), inc=(s, 16))
        R.op("gpsimd", lambda e, ex=ex: e.dma_start(
            out=t["BR"][0:1, 0:1024], in_=dram["beP"][ex]), inc=(s, 16))
        for tb in range(NTB):
            for half in range(2):
                pbk = cx.pb[half]
                for kc in range(8):
                    R.op("tensor", lambda e, tb=tb, kc=kc, half=half, pbk=pbk: e.matmul(
                        pbk[:, 0:512],
                        XN[:, kc * NT + tb * 128: kc * NT + tb * 128 + 128],
                        t["WEB"][:, kc * 1024 + half * 512: kc * 1024 + half * 512 + 512],
                        start=(kc == 0), stop=False))
                R.op("tensor", lambda e, half=half, pbk=pbk: e.matmul(
                    pbk[:, 0:512], t["ONE1"][0:1, 0:128],
                    t["BR"][0:1, half * 512:(half + 1) * 512], start=False, stop=True))
                R.op("vector", lambda e, tb=tb, half=half, ex=ex, pbk=pbk: e.scalar_tensor_tensor(
                    out=X[:, tb * 1024 + half * 512: tb * 1024 + half * 512 + 512],
                    in0=pbk[:, 0:512], scalar=WC[:, tb * 8 + ex: tb * 8 + ex + 1],
                    in1=X[:, tb * 1024 + half * 512: tb * 1024 + half * 512 + 512],
                    op0=ALU.mult, op1=ALU.add), inc=(s, 1))
    return s


def emit_perf_a(cx, dram, outs, dep):
    R, t = cx.R, cx.t
    X, XN = t["X"], t["XN"]
    s = R.sem("pa")
    s_q, m_q = _mm_proj_pairs(cx, dram["W1qP"], 8, 128, t["QD"], lambda pr: None,
                              list(dep), "paq", dep_w=list(dep))
    R.op("sync", lambda e: e.dma_start(
        out=outs["QD1"].rearrange("(pr p) c -> p pr c", p=128),
        in_=t["QD"].rearrange("p (pr c) -> p pr c", pr=8)), inc=(s, 16))
    s_k, m_k, s_kd = emit_k_proj(cx, dram["W1kP"], None, outs["KD1"], list(dep), [], [], "pak")
    R.op("gpsimd", lambda e: e.memset(
        t["VAP"].rearrange("p (tb h c) -> p tb h c", tb=16, h=16)[:, :, :, 64:65], 1.0))
    for kc in range(8):
        R.op("gpsimd", lambda e, kc=kc: e.dma_start(
            out=t["W0"][:, 0:1024], in_=dram["W1v"][kc * 128:(kc + 1) * 128, :]), inc=(s, 16))
        for tb in range(NTB):
            for half in range(2):
                R.op("tensor", lambda e, kc=kc, tb=tb, half=half: e.matmul(
                    cx.pb[2 + half][:, 0:512],
                    XN[:, kc * NT + tb * 128: kc * NT + tb * 128 + 128],
                    t["W0"][:, half * 512:(half + 1) * 512],
                    start=(kc == 0), stop=(kc == 7)))
                if kc == 7:
                    R.op("scalar", lambda e, tb=tb, half=half: e.copy(
                        out=t["VAP"][:, tb * 1040 + half * 520: tb * 1040 + (half + 1) * 520]
                        .rearrange("p (h c) -> p h c", h=8)[:, :, 0:64],
                        in_=cx.pb[2 + half][:, 0:512].rearrange("p (h c) -> p h c", h=8)))
    R.op("sync", lambda e: e.dma_start(
        out=outs["VA1"].rearrange("(tb p) c -> p tb c", p=128),
        in_=t["VAP"].rearrange("p (tb c) -> p tb c", tb=16)), inc=(s, 16))
    for h in range(16):
        pr, hh = h // 2, h % 2
        R.op("gpsimd", lambda e, h=h: e.dma_start(out=t["PM0"][:], in_=dram["pmT"][h]), inc=(s, 16))
        R.op("scalar", lambda e: e.mul(out=t["PM1"][:], in_=t["PM0"][:], mul=0.125))
        for tb in range(NTB):
            qsl0 = t["QD"][hh * 64:hh * 64 + 64, pr * NT + tb * 128: pr * NT + tb * 128 + 128]
            R.op("vector", lambda e, qsl0=qsl0: e.tensor_copy(out=t["KSTR1"][0:64, 0:128], in_=qsl0))
            qsl = t["KSTR1"][0:64, 0:128]
            ksl_dram = outs["KD1"][pr * 128 + hh * 64: pr * 128 + hh * 64 + 64,
                                   tb * 128:(tb + 1) * 128]
            R.op("sync", lambda e, ksl_dram=ksl_dram: e.dma_start(
                out=t["KSTR0"][0:64, 0:128], in_=ksl_dram), inc=(s, 16))
            R.op("tensor", lambda e: e.matmul(cx.pb[0][:, 0:128], t["KSTR0"][0:64, 0:128],
                                              t["PM1"][:], start=True, stop=True))
            R.op("vector", lambda e: e.reduce_max(t["NEGD"][:, 3:4], cx.pb[0][:, 0:128], axis=AX.X))
            if tb == 0:
                R.op("vector", lambda e, h=h: e.tensor_copy(out=t["MX"][:, h % 16:h % 16 + 1],
                                                            in_=t["NEGD"][:, 3:4]))
            else:
                R.op("vector", lambda e, h=h: e.tensor_tensor(
                    out=t["MX"][:, h % 16:h % 16 + 1], in0=t["MX"][:, h % 16:h % 16 + 1],
                    in1=t["NEGD"][:, 3:4], op=ALU.max))
            R.op("tensor", lambda e: e.transpose(cx.pt[0][:, 0:64], t["KSTR0"][0:64, 0:128],
                                                 t["CONSTB"][0:64, 0:64]))
            R.op("vector", lambda e: e.tensor_copy(out=t["KT64"][:], in_=cx.pt[0][:, 0:64]))
            R.op("scalar", lambda e: e.activation(out=t["K264"][:], in_=t["KT64"][:], func=AF.Square))
            R.op("vector", lambda e: e.reduce_sum(t["NEGD"][:, 4:5], t["K264"][:], axis=AX.X))
            R.op("vector", lambda e, h=h, tb=tb: e.tensor_scalar(
                out=t["NEGD"][:, 16 + h:17 + h], in0=t["NEGD"][:, 4:5],
                scalar1=0.5, scalar2=None, op0=ALU.mult))
            R.op("tensor", lambda e, qsl=qsl: e.matmul(cx.pb[1][:, 0:128], qsl, t["PM1"][:],
                                                       start=True, stop=True))
            R.op("vector", lambda e: e.reduce_max(t["NEGD"][:, 5:6], cx.pb[1][:, 0:128],
                                                  axis=AX.X, negate=True))
            R.op("tensor", lambda e, qsl=qsl: e.transpose(cx.pt[0][:, 0:64], qsl,
                                                          t["CONSTB"][0:64, 0:64]))
            R.op("vector", lambda e: e.tensor_copy(out=t["KT64"][:], in_=cx.pt[0][:, 0:64]))
            R.op("scalar", lambda e: e.activation(out=t["K264"][:], in_=t["KT64"][:], func=AF.Square))
            R.op("vector", lambda e: e.reduce_sum(t["NEGD"][:, 6:7], t["K264"][:], axis=AX.X))
            R.op("vector", lambda e: e.scalar_tensor_tensor(
                out=t["NEGD"][:, 7:8], in0=t["NEGD"][:, 6:7], scalar=-0.5,
                in1=t["NEGD"][:, 5:6], op0=ALU.mult, op1=ALU.add))
            R.op("vector", lambda e, h=h: e.tensor_scalar(
                out=t["SG"][:, h:h + 1], in0=t["NEGD"][:, 7:8],
                scalar1=1.0, scalar2=-LN16, op0=ALU.mult, op1=ALU.add))
            if h == 15:
                R.op("sync", lambda e, tb=tb: e.dma_start(
                    out=outs["NEGDQ"][tb * 128:(tb + 1) * 128, :], in_=t["SG"][:, 0:16]),
                    inc=(s, 16))
                R.op("sync", lambda e, tb=tb: e.dma_start(
                    out=outs["DIAGK"][tb * 128:(tb + 1) * 128, :], in_=t["NEGD"][:, 16:32]),
                    inc=(s, 16))
    R.op("vector", lambda e: e.tensor_copy(out=t["MKB"][:, 0:16], in_=t["MX"][:, 0:16]))
    R.op("tensor", lambda e: e.transpose(cx.pt[0][0:16, 0:128], t["MKB"][:, 0:16],
                                         t["CONSTB"][:, 0:128]))
    R.op("vector", lambda e: e.tensor_copy(out=t["SR16"][0:16, 0:128], in_=cx.pt[0][0:16, 0:128]))
    R.op("vector", lambda e: e.reduce_max(t["ST16"][0:16, 0:1], t["SR16"][0:16, 0:128], axis=AX.X))
    R.op("vector", lambda e: e.tensor_copy(out=t["MKB"][0:16, 16:17], in_=t["ST16"][0:16, 0:1]))
    R.op("tensor", lambda e: e.transpose(cx.pt[0][0:1, 0:16], t["MKB"][0:16, 16:17],
                                         t["CONSTB"][0:16, 0:16]))
    R.op("vector", lambda e: e.tensor_copy(out=t["NEGD"][0:1, 40:41] if False else t["STATS"][0:1, 0:16], in_=cx.pt[0][0:1, 0:16]))
    R.op("sync", lambda e: e.dma_start(out=outs["STABL"], in_=t["STATS"][0:1, 0:16]), inc=(s, 16))
    return s


def emit_perf_b(cx, dram, ins_d, outs, dep):
    R, t = cx.R, cx.t
    s = R.sem("pb_")
    R.op("sync", lambda e: e.dma_start(out=t["STATS"][0:1, 0:64], in_=ins_d["STABG"]),
         wait=dep, inc=(s, 16))
    R.op("vector", lambda e: e.tensor_tensor(out=t["STATS"][0:1, 64:80],
                                             in0=t["STATS"][0:1, 0:16],
                                             in1=t["STATS"][0:1, 16:32], op=ALU.max))
    R.op("vector", lambda e: e.tensor_tensor(out=t["STATS"][0:1, 80:96],
                                             in0=t["STATS"][0:1, 32:48],
                                             in1=t["STATS"][0:1, 48:64], op=ALU.max))
    R.op("vector", lambda e: e.tensor_tensor(out=t["STATS"][0:1, 96:112],
                                             in0=t["STATS"][0:1, 64:80],
                                             in1=t["STATS"][0:1, 80:96], op=ALU.max))
    R.op("vector", lambda e: e.tensor_copy(out=t["BR"][0:1, 0:16], in_=t["STATS"][0:1, 96:112]))
    for h in range(16):
        R.op("tensor", lambda e, h=h: e.matmul(cx.pb[2][:, h:h + 1], t["ONE1"][0:1, 0:128],
                                               t["BR"][0:1, h:h + 1], start=True, stop=True))
    R.op("vector", lambda e: e.tensor_copy(out=t["NEGD"][:, 0:16], in_=cx.pb[2][:, 0:16]))
    R.op("sync", lambda e: e.dma_start(
        out=t["VAP"].rearrange("p (tb c) -> p tb c", tb=16),
        in_=ins_d["VA1o"].rearrange("(tb p) c -> p tb c", p=128)), inc=(s, 16))
    R.op("sync", lambda e: e.dma_start(
        out=t["KVB"][:, 0:256].rearrange("p (tb c) -> p tb c", tb=16),
        in_=ins_d["DIAGKo"].rearrange("(tb p) c -> p tb c", p=128)), inc=(s, 16))
    for h in range(16):
        pr, hh = h // 2, h % 2
        R.op("gpsimd", lambda e, h=h: e.dma_start(out=t["PM0"][:], in_=dram["pmT"][h]), inc=(s, 16))
        R.op("scalar", lambda e: e.mul(out=t["PM1"][:], in_=t["PM0"][:], mul=0.125))
        R.op("sync", lambda e, pr=pr, hh=hh: e.dma_start(
            out=t["KSTR0"][0:64, 0:2048],
            in_=ins_d["KD1o"][pr * 128 + hh * 64: pr * 128 + hh * 64 + 64, :]), inc=(s, 16))
        for tb in range(NTB):
            R.op("vector", lambda e, h=h, tb=tb: e.tensor_tensor(
                out=t["NEGD"][:, 33:34], in0=t["KVB"][:, tb * 16 + h: tb * 16 + h + 1],
                in1=t["NEGD"][:, h:h + 1], op=ALU.add))
            R.op("vector", lambda e: e.tensor_scalar(out=t["NEGD"][:, 34:35], in0=t["NEGD"][:, 33:34],
                                                     scalar1=-1.0, scalar2=-LN16,
                                                     op0=ALU.mult, op1=ALU.add))
            R.op("tensor", lambda e, tb=tb: e.matmul(
                cx.pb[0][:, 0:128], t["KSTR0"][0:64, tb * 128:(tb + 1) * 128],
                t["PM1"][:], start=True, stop=True))
            R.op("scalar", lambda e: e.activation(out=t["KF"][:, 0:128], in_=cx.pb[0][:, 0:128],
                                                  func=AF.Exp, bias=t["NEGD"][:, 34:35], scale=1.0))
            R.op("scalar", lambda e: e.activation(out=t["KF"][:, 128:256], in_=cx.pb[0][:, 0:128],
                                                  func=AF.Exp, bias=t["NEGD"][:, 34:35], scale=-1.0))
            for sgn in range(2):
                R.op("tensor", lambda e, tb=tb, sgn=sgn, h=h: e.matmul(
                    cx.pb[2 + sgn][0:128, 0:65],
                    t["KF"][:, sgn * 128:(sgn + 1) * 128],
                    t["VAP"][:, tb * 1040 + h * 65: tb * 1040 + h * 65 + 65],
                    start=(tb == 0), stop=(tb == NTB - 1)))
        for sgn in range(2):
            R.op("vector", lambda e, sgn=sgn: e.tensor_copy(out=t["KVS"][:, sgn * 65:(sgn + 1) * 65],
                                                            in_=cx.pb[2 + sgn][0:128, 0:65]))
            R.op("sync", lambda e, h=h, sgn=sgn: e.dma_start(
                out=outs["KVL"][h * 256 + sgn * 128: h * 256 + sgn * 128 + 128, :],
                in_=t["KVS"][:, sgn * 65:(sgn + 1) * 65]), inc=(s, 16))
    return s


def emit_perf_c(cx, dram, ins_d, dep):
    R, t = cx.R, cx.t
    X = t["X"]
    s = R.sem("pc_")
    R.op("sync", lambda e: e.dma_start(
        out=t["KVA"].rearrange("p (g c) -> p g c", g=32),
        in_=ins_d["KV4"][0].rearrange("(g p) c -> p g c", p=128)), wait=dep, inc=(s, 16))
    for r in range(1, 4):
        R.op("sync", lambda e, r=r: e.dma_start(
            out=t["KVB"].rearrange("p (g c) -> p g c", g=32),
            in_=ins_d["KV4"][r].rearrange("(g p) c -> p g c", p=128)), inc=(s, 16))
        R.op("vector", lambda e: e.tensor_tensor(out=t["KVA"][:], in0=t["KVA"][:],
                                                 in1=t["KVB"][:], op=ALU.add))
    R.op("vector", lambda e: e.tensor_copy(out=t["KSTR1"][:, 0:2015], in_=t["KVA"][:, 0:2015]))
    R.op("vector", lambda e: e.tensor_copy(out=t["MKB"][:, 0:65], in_=t["KVA"][:, 2015:2080]))
    R.op("sync", lambda e: e.dma_start(
        out=t["QD"].rearrange("p (pr c) -> p pr c", pr=8),
        in_=ins_d["QD1o"].rearrange("(pr p) c -> p pr c", p=128)), inc=(s, 16))
    R.op("sync", lambda e: e.dma_start(
        out=t["KVB"][:, 0:256].rearrange("p (tb c) -> p tb c", tb=16),
        in_=ins_d["NEGDQo"].rearrange("(tb p) c -> p tb c", p=128)), inc=(s, 16))
    for h in range(16):
        pr, hh = h // 2, h % 2
        R.op("gpsimd", lambda e, h=h: e.dma_start(out=t["PM0"][:], in_=dram["pmT"][h]), inc=(s, 16))
        R.op("scalar", lambda e: e.mul(out=t["PM1"][:], in_=t["PM0"][:], mul=0.125))
        for tb in range(NTB):
            R.op("vector", lambda e, tb=tb, h=h: e.tensor_copy(
                out=t["NEGD"][:, 0:1], in_=t["KVB"][:, tb * 16 + h: tb * 16 + h + 1]))
            qsl0 = t["QD"][hh * 64:hh * 64 + 64, pr * NT + tb * 128: pr * NT + tb * 128 + 128]
            R.op("vector", lambda e, qsl0=qsl0: e.tensor_copy(out=t["QFT"][0:64, 0:128], in_=qsl0))
            qsl = t["QFT"][0:64, 0:128]
            R.op("tensor", lambda e, qsl=qsl: e.matmul(cx.pb[0][:, 0:128], qsl, t["PM1"][:],
                                                       start=True, stop=True))
            R.op("scalar", lambda e: e.activation(out=t["KF"][:, 0:128], in_=cx.pb[0][:, 0:128],
                                                  func=AF.Exp, bias=t["NEGD"][:, 0:1], scale=1.0))
            R.op("scalar", lambda e: e.activation(out=t["KF"][:, 128:256], in_=cx.pb[0][:, 0:128],
                                                  func=AF.Exp, bias=t["NEGD"][:, 0:1], scale=-1.0))
            for sgn in range(2):
                R.op("tensor", lambda e, sgn=sgn: e.transpose(
                    cx.pt[0][:, sgn * 128:(sgn + 1) * 128],
                    t["KF"][:, sgn * 128:(sgn + 1) * 128], t["CONSTB"][:, 0:128]))
            R.op("vector", lambda e: e.tensor_copy(out=t["QFT"][:], in_=cx.pt[0][:, 0:256]))
            for sgn in range(2):
                g_ = h * 2 + sgn
                if g_ < 31:
                    rhs = t["KSTR1"][:, g_ * 65: g_ * 65 + 65]
                else:
                    rhs = t["MKB"][:, 0:65]
                R.op("tensor", lambda e, sgn=sgn, rhs=rhs: e.matmul(
                    cx.pb[1][:, 0:65], t["QFT"][:, sgn * 128:(sgn + 1) * 128], rhs,
                    start=(sgn == 0), stop=(sgn == 1)))
            R.op("vector", lambda e: e.tensor_copy(out=t["NEGD"][:, 1:2], in_=cx.pb[1][:, 64:65]))
            R.op("vector", lambda e: e.tensor_scalar(out=t["NEGD"][:, 2:3], in0=t["NEGD"][:, 1:2],
                                                     scalar1=1.0, scalar2=EPS_PERF,
                                                     op0=ALU.mult, op1=ALU.add))
            R.op("vector", lambda e: e.reciprocal(t["NEGD"][:, 3:4], t["NEGD"][:, 2:3]))
            R.op("vector", lambda e: e.tensor_scalar(out=t["ATN"][:], in0=cx.pb[1][:, 0:64],
                                                     scalar1=t["NEGD"][:, 3:4], scalar2=None,
                                                     op0=ALU.mult))
            R.op("tensor", lambda e: e.transpose(cx.pt[0][:, 0:128] if False else cx.pt[0][0:64, 256:384],
                                                 t["ATN"][:], t["CONSTB"][:, 0:128]))
            R.op("vector", lambda e, h=h, tb=tb, hh=hh, pr=pr: e.tensor_copy(
                out=att_alias(cx)[hh * 64:hh * 64 + 64, pr * 2048 + tb * 128: pr * 2048 + tb * 128 + 128],
                in_=cx.pt[0][0:64, 256:384]))
    R.op("gpsimd", lambda e: e.dma_start(
        out=t["WEB"].rearrange("p (c n) -> p c n", c=8),
        in_=dram["W1o"].rearrange("(c p) n -> p c n", p=128)), inc=(s, 16))
    ATT = att_alias(cx)
    for tb in range(NTB):
        for half in range(2):
            for c in range(8):
                R.op("tensor", lambda e, tb=tb, half=half, c=c: e.matmul(
                    cx.pb[2][:, 0:512],
                    ATT[:, c * 2048 + tb * 128: c * 2048 + tb * 128 + 128],
                    t["WEB"][:, c * 1024 + half * 512: c * 1024 + half * 512 + 512],
                    start=(c == 0), stop=(c == 7)))
            R.op("vector", lambda e, tb=tb, half=half: e.tensor_tensor(
                out=X[:, tb * 1024 + half * 512: tb * 1024 + half * 512 + 512],
                in0=X[:, tb * 1024 + half * 512: tb * 1024 + half * 512 + 512],
                in1=cx.pb[2][:, 0:512], op=ALU.add), inc=(s, 1))
    return s


def emit_ffn(cx, dram, out_dram, dep):
    R, t = cx.R, cx.t
    X, XN = t["X"], t["XN"]
    s = R.sem("ff")
    for oc in range(8):
        R.op("gpsimd", lambda e, oc=oc: e.dma_start(
            out=t["W0"][:, 0:1024], in_=dram["Wf"][oc * 128:(oc + 1) * 128, :]),
            wait=dep, inc=(s, 16))
        R.op("scalar", lambda e: e.sign(out=t["W1"][:, 0:1024], in_=t["W0"][:, 0:1024]))
        for ic in range(8):
            R.op("tensor", lambda e, ic=ic: e.transpose(
                cx.pt[0][:, 0:128], t["W1"][:, ic * 128:(ic + 1) * 128], t["CONSTB"][:, 0:128]))
            R.op("vector", lambda e, ic=ic, oc=oc: e.tensor_copy(
                out=t["WEB"][:, ic * 1024 + oc * 128: ic * 1024 + oc * 128 + 128],
                in_=cx.pt[0][:, 0:128]))
    for tb in range(NTB):
        for half in range(2):
            for ic in range(8):
                R.op("tensor", lambda e, tb=tb, half=half, ic=ic: e.matmul(
                    cx.pb[0][:, 0:512],
                    XN[:, ic * NT + tb * 128: ic * NT + tb * 128 + 128],
                    t["WEB"][:, ic * 1024 + half * 512: ic * 1024 + half * 512 + 512],
                    start=(ic == 0), stop=(ic == 7)))
            osb = t["OS0"] if tb % 2 == 0 else t["OS1"]
            R.op("vector", lambda e, tb=tb, half=half, osb=osb: e.tensor_tensor(
                out=osb[:, half * 512:(half + 1) * 512],
                in0=X[:, tb * 1024 + half * 512: tb * 1024 + half * 512 + 512],
                in1=cx.pb[0][:, 0:512], op=ALU.add))
        R.op("sync", lambda e, tb=tb, osb=osb: e.dma_start(
            out=out_dram[tb * 128:(tb + 1) * 128, :], in_=osb[:]), inc=(s, 16))
    return s


def np_ln(x):
    m = x.mean(-1, keepdims=True)
    v = x.var(-1, keepdims=True)
    return (x - m) / np.sqrt(v + EPS_LN)


NPDT = {F32: np.float32, BF16: BF}


def declare_inputs(nc, arrs, names):
    d = {}
    for n in names:
        a = arrs[n]
        dt = BF16 if a.dtype == BF else F32
        d[n] = nc.dram_tensor(n, list(a.shape), dt, kind="ExternalInput").ap()
    return d


def emit_store(cx, dram_out, sbuf_ap, wait, s_st):
    cx.R.op("sync", lambda e: e.dma_start(out=dram_out, in_=sbuf_ap), wait=wait, inc=(s_st, 16))


SHP = {"WqP": [8, 128, 8, 128], "WkP": [8, 128, 8, 128], "WvdP": [4, 128, 8, 64],
       "bqP": [128, 8], "bkP": [128, 8], "bvdP": [64, 4],
       "identb": [128, 128], "tri": [128, 128],
       "Wvu2": [8, 32, 128], "bvu2": [8, 1, 128],
       "WeP": [8, 1024, 1024], "beP": [8, 1, 1024], "WgP": [1024, 8], "bgP": [1, 8],
       "W1qP": [8, 128, 8, 128], "W1kP": [8, 128, 8, 128], "W1v": [1024, 1024],
       "pmT": [16, 64, 128], "W1o": [1024, 1024], "Wf": [1024, 1024]}
BF16_CONSTS = {"identb", "tri"}


def _decl(nc, names):
    return {n: nc.dram_tensor(n, SHP[n], BF16 if n in BF16_CONSTS else F32,
                              kind="ExternalInput").ap() for n in names}


def _fin(R, sems):
    R.op("sync", None, wait=[(s, s.v) for s in sems])


def build_L1():
    from contextlib import ExitStack
    nc = bass.Bass("TRN2", target_bir_lowering=False, debug=False)
    with ExitStack() as stack:
        xd = nc.dram_tensor("x", [NT, 1024], F32, kind="ExternalInput").ap()
        dram = _decl(nc, ["WkP", "WvdP", "bqP", "bkP", "bvdP", "identb", "tri"])
        oKD = nc.dram_tensor("oKD", [1024, NT], BF16, kind="ExternalOutput").ap()
        oVD = nc.dram_tensor("oVD", [256, NT], BF16, kind="ExternalOutput").ap()
        cx = alloc_common(nc, stack)
        R = cx.R
        ldc = emit_consts(cx, dram)
        ldx = emit_load_x(cx, xd)
        s_ev = emit_ln(cx, [(ldx, 16), (ldc, ldc.v)])
        s_k, m_k, s_kdma = emit_k_proj(cx, dram["WkP"], 8, oKD, [(s_ev, NTB)],
                                       [(ldc, ldc.v)], [(s_ev, NTB)], "p0k")
        s_v, m_v, s_vdma = emit_l0_vd(cx, dram, oVD, [(s_ev, NTB)], [(ldc, ldc.v)],
                                      [(m_k, 16)])
        _fin(R, [s_kdma, s_vdma])
        R.flush(nc, stack)
    return nc


def build_L2(m=None):
    from contextlib import ExitStack
    nc = bass.Bass("TRN2", target_bir_lowering=False, debug=False)
    with ExitStack() as stack:
        xd = nc.dram_tensor("x", [NT, 1024], F32, kind="ExternalInput").ap()
        dram = _decl(nc, ["WqP", "bqP", "bkP", "bvdP", "identb", "tri", "Wvu2", "bvu2"])
        KGd = nc.dram_tensor("KG", [1024, 8192], BF16, kind="ExternalInput").ap()
        VDGd = nc.dram_tensor("VDG", [256, 8192], BF16, kind="ExternalInput").ap()
        dram["BIASK"] = nc.dram_tensor("BIASK", [128, 16], F32, kind="ExternalInput").ap()
        dram["BIASK2"] = nc.dram_tensor("BIASK2", [128, 16], F32, kind="ExternalInput").ap()
        dram["MD"] = nc.dram_tensor("MD", [16, 128, 128], BF16, kind="ExternalInput").ap()
        oX1 = nc.dram_tensor("oX1", [NT, 1024], BF16, kind="ExternalOutput").ap()
        cx = alloc_common(nc, stack)
        R = cx.R
        ldc = emit_consts(cx, dram)
        ldx = emit_load_x(cx, xd)
        s_ev = emit_ln(cx, [(ldx, 16), (ldc, ldc.v)])
        s_q, m_q = emit_l0_proj(cx, dram, [(s_ev, NTB)], [(ldc, ldc.v)])
        kg_fn = lambda g, ke: KGd[g * 128:(g + 1) * 128, ke * 1024:(ke + 1) * 1024]
        vdg_fn = lambda g, ke: VDGd[(g // 2) * 64 + (g % 2) * 32:(g // 2) * 64 + (g % 2) * 32 + 32,
                                    ke * 1024:(ke + 1) * 1024]
        s_stt = emit_l0_attn(cx, dram, kg_fn, vdg_fn, [0, 7],
                             dep_q=[(s_q, s_q.v)], dep_kv=[], dep_x=[(s_ev, NTB)],
                             dep_pb=[(s_q, s_q.v)], dep_c=[(ldc, ldc.v)])
        st = R.sem("st2")
        R.op("sync", lambda e: e.dma_start(
            out=oX1.rearrange("(tb p) c -> p tb c", p=128),
            in_=cx.t["X"].rearrange("p (tb c) -> p tb c", tb=NTB)),
            wait=[(s_stt, s_stt.v)], inc=(st, 16))
        _fin(R, [st])
        R.flush(nc, stack)
    return nc


def build_L3():
    from contextlib import ExitStack
    nc = bass.Bass("TRN2", target_bir_lowering=False, debug=False)
    with ExitStack() as stack:
        xd = nc.dram_tensor("x", [NT, 1024], BF16, kind="ExternalInput").ap()
        dram = _decl(nc, ["WeP", "beP", "WgP", "bgP", "W1qP", "W1kP", "W1v", "pmT",
                          "identb", "tri", "bqP", "bkP", "bvdP"])
        oX2 = nc.dram_tensor("oX2", [NT, 1024], BF16, kind="ExternalOutput").ap()
        outs = {
            "QD1": nc.dram_tensor("oQD1", [1024, NT], BF16, kind="ExternalOutput").ap(),
            "KD1": nc.dram_tensor("oKD1", [1024, NT], BF16, kind="ExternalOutput").ap(),
            "VA1": nc.dram_tensor("oVA1", [NT, 1040], BF16, kind="ExternalOutput").ap(),
            "NEGDQ": nc.dram_tensor("oNEGDQ", [NT, 16], F32, kind="ExternalOutput").ap(),
            "DIAGK": nc.dram_tensor("oDIAGK", [NT, 16], F32, kind="ExternalOutput").ap(),
            "STABL": nc.dram_tensor("oSTABL", [1, 16], F32, kind="ExternalOutput").ap(),
        }
        cx = alloc_common(nc, stack)
        R = cx.R
        ldc = emit_consts(cx, dram)
        ldx = emit_load_x(cx, xd, cast=False)
        s_ev = emit_ln(cx, [(ldx, 16), (ldc, ldc.v)])
        s_m = emit_moe(cx, dram, [(s_ev, NTB)])
        st = R.sem("st3")
        R.op("sync", lambda e: e.dma_start(
            out=oX2.rearrange("(tb p) c -> p tb c", p=128),
            in_=cx.t["X"].rearrange("p (tb c) -> p tb c", tb=NTB)),
            wait=[(s_m, s_m.v)], inc=(st, 16))
        s_ev2 = emit_ln(cx, [(s_m, s_m.v)], tag="ln2")
        s_a = emit_perf_a(cx, dram, outs, [(s_ev2, NTB)])
        _fin(R, [st, s_a])
        R.flush(nc, stack)
    return nc


def build_L4a_unused():
    from contextlib import ExitStack
    nc = bass.Bass("TRN2", target_bir_lowering=False, debug=False)
    with ExitStack() as stack:
        xd = nc.dram_tensor("x", [NT, 1024], BF16, kind="ExternalInput").ap()
        dram = _decl(nc, ["W1qP", "W1kP", "W1v", "pmT", "identb", "tri", "bqP", "bkP", "bvdP"])
        outs = {
            "QD1": nc.dram_tensor("oQD1", [1024, NT], BF16, kind="ExternalOutput").ap(),
            "KD1": nc.dram_tensor("oKD1", [1024, NT], BF16, kind="ExternalOutput").ap(),
            "VA1": nc.dram_tensor("oVA1", [NT, 1040], BF16, kind="ExternalOutput").ap(),
            "NEGDQ": nc.dram_tensor("oNEGDQ", [NT, 16], F32, kind="ExternalOutput").ap(),
            "DIAGK": nc.dram_tensor("oDIAGK", [NT, 16], F32, kind="ExternalOutput").ap(),
            "STABL": nc.dram_tensor("oSTABL", [1, 16], F32, kind="ExternalOutput").ap(),
        }
        cx = alloc_common(nc, stack)
        R = cx.R
        ldc = emit_consts(cx, dram)
        ldx = emit_load_x(cx, xd, cast=False)
        s_ev = emit_ln(cx, [(ldx, 16), (ldc, ldc.v)])
        s = emit_perf_a(cx, dram, outs, [(s_ev, NTB)])
        _fin(R, [s])
        R.flush(nc, stack)
    return nc


def build_L4b():
    from contextlib import ExitStack
    nc = bass.Bass("TRN2", target_bir_lowering=False, debug=False)
    with ExitStack() as stack:
        dram = _decl(nc, ["pmT", "identb", "tri", "bqP", "bkP", "bvdP"])
        ins_d = {
            "KD1o": nc.dram_tensor("KD1o", [1024, NT], BF16, kind="ExternalInput").ap(),
            "VA1o": nc.dram_tensor("VA1o", [NT, 1040], BF16, kind="ExternalInput").ap(),
            "DIAGKo": nc.dram_tensor("DIAGKo", [NT, 16], F32, kind="ExternalInput").ap(),
            "STABG": nc.dram_tensor("STABG", [1, 64], F32, kind="ExternalInput").ap(),
        }
        oKVL = nc.dram_tensor("oKVL", [4096, 65], F32, kind="ExternalOutput").ap()
        cx = alloc_common(nc, stack)
        R = cx.R
        ldc = emit_consts(cx, dram)
        s = emit_perf_b(cx, dram, ins_d, {"KVL": oKVL}, [(ldc, ldc.v)])
        _fin(R, [s])
        R.flush(nc, stack)
    return nc


def build_L4c():
    from contextlib import ExitStack
    nc = bass.Bass("TRN2", target_bir_lowering=False, debug=False)
    with ExitStack() as stack:
        xd = nc.dram_tensor("x", [NT, 1024], BF16, kind="ExternalInput").ap()
        dram = _decl(nc, ["pmT", "W1o", "Wf", "identb", "tri", "bqP", "bkP", "bvdP"])
        ins_d = {
            "QD1o": nc.dram_tensor("QD1o", [1024, NT], BF16, kind="ExternalInput").ap(),
            "NEGDQo": nc.dram_tensor("NEGDQo", [NT, 16], F32, kind="ExternalInput").ap(),
            "KV4": nc.dram_tensor("KV4", [4, 4096, 65], F32, kind="ExternalInput").ap(),
        }
        oX3 = nc.dram_tensor("oOut", [NT, 1024], F32, kind="ExternalOutput").ap()
        cx = alloc_common(nc, stack)
        R = cx.R
        ldc = emit_consts(cx, dram)
        ldx = emit_load_x(cx, xd, cast=False)
        s = emit_perf_c(cx, dram, ins_d, [(ldx, 16), (ldc, ldc.v)])
        s_ev3 = emit_ln(cx, [(s, s.v)], tag="ln3")
        s_f = emit_ffn(cx, dram, oX3, [(s_ev3, NTB)])
        _fin(R, [s_f])
        R.flush(nc, stack)
    return nc


def build_L5():
    from contextlib import ExitStack
    nc = bass.Bass("TRN2", target_bir_lowering=False, debug=False)
    with ExitStack() as stack:
        xd = nc.dram_tensor("x", [NT, 1024], BF16, kind="ExternalInput").ap()
        dram = _decl(nc, ["Wf", "identb", "tri", "bqP", "bkP", "bvdP"])
        oOut = nc.dram_tensor("oOut", [NT, 1024], F32, kind="ExternalOutput").ap()
        cx = alloc_common(nc, stack)
        R = cx.R
        ldc = emit_consts(cx, dram)
        ldx = emit_load_x(cx, xd, cast=False)
        s_ev = emit_ln(cx, [(ldx, 16), (ldc, ldc.v)])
        s = emit_ffn(cx, dram, oOut, [(s_ev, NTB)])
        _fin(R, [s])
        R.flush(nc, stack)
    return nc


def kernel(**ins):
    P = pack_consts(ins)
    x = np.asarray(ins["L0_attn_x"], np.float32)
    CN1 = ["WkP", "WvdP", "bqP", "bkP", "bvdP", "identb", "tri"]
    nc1 = build_L1()
    im1 = [dict(x=shard_rows(x, c), **{n: P[n] for n in CN1}) for c in range(8)]
    r1 = run_bass_kernel_spmd(nc1, im1, core_ids=list(range(8))).results
    KGs, VDGs = {}, {}
    for g in range(2):
        KG = np.zeros((1024, 8192), P["identb"].dtype)
        VDG = np.zeros((256, 8192), P["identb"].dtype)
        for mm in range(4):
            c = g * 4 + mm
            for ei, ge in enumerate([mm, 7 - mm]):
                KG[:, ge * 1024:(ge + 1) * 1024] = r1[c]["oKD"][:, ei * 1024:(ei + 1) * 1024]
                VDG[:, ge * 1024:(ge + 1) * 1024] = r1[c]["oVD"][:, ei * 1024:(ei + 1) * 1024]
        KGs[g], VDGs[g] = KG, VDG
    CN2 = ["WqP", "bqP", "bkP", "bvdP", "identb", "tri", "Wvu2", "bvu2"]

    def attn_ctrl(c):
        mloc = c % 4
        bk1 = np.zeros((128, 16), np.float32)
        bk2 = np.zeros((128, 16), np.float32)
        md = np.zeros((16, 128, 128), BF)
        tri_np = np.tril(np.ones((128, 128), np.float32)).astype(BF)
        ones_np = np.ones((128, 128), BF)
        for ei, ge in enumerate([mloc, 7 - mloc]):
            for ke in range(8):
                bk1[:, ei * 8 + ke] = -STAB0 if ke >= ge else -1e9
                bk2[:, ei * 8 + ke] = -STAB0 if ke > ge else -1e9
                md[ei * 8 + ke] = tri_np if ke == ge else ones_np
        return bk1, bk2, md
    nc2 = build_L2()
    im2 = []
    for c in range(8):
        bk1, bk2, md = attn_ctrl(c)
        im2.append(dict(x=shard_rows(x, c), KG=KGs[c // 4], VDG=VDGs[c // 4],
                        BIASK=bk1, BIASK2=bk2, MD=md,
                        **{n: P[n] for n in CN2}))
    r2 = run_bass_kernel_spmd(nc2, im2, core_ids=list(range(8))).results
    x1s = [r2[c]["oX1"] for c in range(8)]
    CN3 = ["WeP", "beP", "WgP", "bgP", "W1qP", "W1kP", "W1v", "pmT",
           "identb", "tri", "bqP", "bkP", "bvdP"]
    nc3 = build_L3()
    im3 = [dict(x=x1s[c], **{n: P[n] for n in CN3}) for c in range(8)]
    r3 = run_bass_kernel_spmd(nc3, im3, core_ids=list(range(8))).results
    x2s = [r3[c]["oX2"] for c in range(8)]
    r4a = r3
    CN4b = ["pmT", "identb", "tri", "bqP", "bkP", "bvdP"]
    nc4b = build_L4b()
    im4b = []
    for c in range(8):
        g = c // 4
        stabg = np.concatenate([r4a[g * 4 + j]["oSTABL"] for j in range(4)], axis=1)
        im4b.append(dict(KD1o=r4a[c]["oKD1"], VA1o=r4a[c]["oVA1"],
                         DIAGKo=r4a[c]["oDIAGK"], STABG=stabg,
                         **{n: P[n] for n in CN4b}))
    r4b = run_bass_kernel_spmd(nc4b, im4b, core_ids=list(range(8))).results
    CN4c = ["pmT", "W1o", "Wf", "identb", "tri", "bqP", "bkP", "bvdP"]
    nc4c = build_L4c()
    im4c = []
    for c in range(8):
        g = c // 4
        kv4 = np.stack([r4b[g * 4 + j]["oKVL"] for j in range(4)], axis=0)
        im4c.append(dict(x=x2s[c], QD1o=r4a[c]["oQD1"], NEGDQo=r4a[c]["oNEGDQ"],
                         KV4=kv4, **{n: P[n] for n in CN4c}))
    r4c = run_bass_kernel_spmd(nc4c, im4c, core_ids=list(range(8))).results
    return unshard_rows([r4c[c]["oOut"] for c in range(8)])


CONST_NAMES_A = ["WqP", "WkP", "WvdP", "bqP", "bkP", "bvdP", "identb", "identf", "tri"]


def build_stageA_test():
    from contextlib import ExitStack
    nc = bass.Bass("TRN2", target_bir_lowering=False, debug=False)
    stack = ExitStack()
    with stack:
        xd = nc.dram_tensor("x", [NT, 1024], F32, kind="ExternalInput").ap()
        P = {n: None for n in CONST_NAMES_A}
        shapes = {"WqP": [8, 128, 8, 128], "WkP": [8, 128, 8, 128], "WvdP": [4, 128, 8, 64],
                  "bqP": [128, 8], "bkP": [128, 8], "bvdP": [64, 4],
                  "identb": [128, 128], "identf": [128, 128], "tri": [128, 128]}
        dts = {"identb": BF16, "tri": BF16}
        dram = {n: nc.dram_tensor(n, shapes[n], dts.get(n, F32), kind="ExternalInput").ap()
                for n in CONST_NAMES_A}
        oXN = nc.dram_tensor("oXN", [128, 8 * NT], BF16, kind="ExternalOutput").ap()
        oQD = nc.dram_tensor("oQD", [128, 8 * NT], BF16, kind="ExternalOutput").ap()
        oKD = nc.dram_tensor("oKD", [1024, NT], BF16, kind="ExternalOutput").ap()
        oVD = nc.dram_tensor("oVD", [256, NT], BF16, kind="ExternalOutput").ap()
        cx = alloc_common(nc, stack)
        R = cx.R
        ldc = emit_consts(cx, dram)
        ldx = emit_load_x(cx, xd)
        s_ev = emit_ln(cx, [(ldx, 16)])
        s_q, m_q = emit_l0_proj(cx, dram, [(s_ev, NTB)], [(ldc, ldc.v)])
        s_k, m_k, s_kdma = emit_k_proj(cx, dram["WkP"], 8, oKD, [(s_ev, NTB)],
                                       [(ldc, ldc.v)], [(m_q, 16)], "p0k")
        s_v, m_v, s_vdma = emit_l0_vd(cx, dram, oVD, [(s_ev, NTB)], [(ldc, ldc.v)],
                                      [(m_k, 16)])
        st = R.sem("st")
        emit_store(cx, oXN, cx.t["XN"][:], [(s_ev, NTB)], st)
        emit_store(cx, oQD, cx.t["QD"][:], [(s_q, s_q.v)], st)
        R.op("sync", None, wait=[(st, st.v), (s_vdma, s_vdma.v), (s_kdma, s_kdma.v)])
        R.flush(nc, stack)
    return nc


CONST_NAMES_B = CONST_NAMES_A + ["Wvu2", "bvu2"]


def build_stageB_test():
    from contextlib import ExitStack
    nc = bass.Bass("TRN2", target_bir_lowering=False, debug=False)
    stack = ExitStack()
    with stack:
        xd = nc.dram_tensor("x", [NT, 1024], F32, kind="ExternalInput").ap()
        shapes = {"WqP": [8, 128, 8, 128], "WkP": [8, 128, 8, 128], "WvdP": [4, 128, 8, 64],
                  "bqP": [128, 8], "bkP": [128, 8], "bvdP": [64, 4],
                  "identb": [128, 128], "identf": [128, 128], "tri": [128, 128],
                  "Wvu2": [8, 32, 128], "bvu2": [8, 1, 128]}
        dts = {"identb": BF16, "tri": BF16}
        dram = {n: nc.dram_tensor(n, shapes[n], dts.get(n, F32), kind="ExternalInput").ap()
                for n in CONST_NAMES_B}
        KGd = nc.dram_tensor("KG", [1024, 8192], BF16, kind="ExternalInput").ap()
        VDGd = nc.dram_tensor("VDG", [256, 8192], BF16, kind="ExternalInput").ap()
        dram["BIASK"] = nc.dram_tensor("BIASK", [128, 16], F32, kind="ExternalInput").ap()
        dram["BIASK2"] = nc.dram_tensor("BIASK2", [128, 16], F32, kind="ExternalInput").ap()
        dram["MD"] = nc.dram_tensor("MD", [16, 128, 128], BF16, kind="ExternalInput").ap()
        oX1 = nc.dram_tensor("oX1", [NT, 1024], BF16, kind="ExternalOutput").ap()
        oZ = nc.dram_tensor("oZ", [128, 256], F32, kind="ExternalOutput").ap()
        cx = alloc_common(nc, stack)
        R = cx.R
        ldc = emit_consts(cx, dram)
        ldx = emit_load_x(cx, xd)
        s_ev = emit_ln(cx, [(ldx, 16), (ldc, ldc.v)])
        s_q, m_q = emit_l0_proj(cx, dram, [(s_ev, NTB)], [(ldc, ldc.v)])
        kg_fn = lambda g, ke: KGd[g * 128:(g + 1) * 128, ke * 1024:(ke + 1) * 1024]
        vdg_fn = lambda g, ke: VDGd[(g // 2) * 64 + (g % 2) * 32:(g // 2) * 64 + (g % 2) * 32 + 32,
                                    ke * 1024:(ke + 1) * 1024]
        s_stt = emit_l0_attn(cx, dram, kg_fn, vdg_fn, [0, 7],
                             dep_q=[(s_q, s_q.v)], dep_kv=[], dep_x=[(s_ev, NTB)],
                             dep_pb=[(s_q, s_q.v)], dep_c=[(ldc, ldc.v)],
                             dbg={"x1banks": oX1} if _osmod.environ.get("DBGB") else None, groups=range(int(_osmod.environ.get("NGROUPS", "8"))))
        st = R.sem("st")
        if not _osmod.environ.get("DBGB"):
            R.op("sync", lambda e: e.dma_start(
                out=oX1.rearrange("(tb p) c -> p tb c", p=128),
                in_=cx.t["X"].rearrange("p (tb c) -> p tb c", tb=NTB)),
                wait=[(s_stt, s_stt.v)], inc=(st, 16))
        else:
            pass
        R.op("sync", lambda e: e.dma_start(out=oZ, in_=cx.t["ZS"][:]),
            wait=[(s_stt, s_stt.v)], inc=(st, 16))
        R.op("sync", None, wait=[(st, st.v)])
        R.flush(nc, stack)
    return nc



